# version 02
# baseline (speedup 1.0000x reference)
import numpy as np
import ml_dtypes
import concourse.bass as bass
import concourse.mybir as mybir
from concourse.bass_utils import run_bass_kernel_spmd

F32 = mybir.dt.float32
BF16 = mybir.dt.bfloat16
FP8 = mybir.dt.float8e4
U8 = mybir.dt.uint8
I32 = mybir.dt.int32
AF = mybir.ActivationFunctionType
ALU = mybir.AluOpType
AX = mybir.AxisListType

B, S, D = 2, 4096, 2048
NH, DH = 16, 128
DFF0 = 8192
DFF1 = 5632
NI = DFF1 // 128
NSL = DFF1 // 512
T = 1024
RS = float(1.0 / np.sqrt(128.0))
MASKV = -1131371.0
EPS_LN = 1e-5
DEBUG = False


class Prog:
    ENGINES = ("sync", "scalar", "vector", "tensor", "gpsimd")

    def __init__(self, nc):
        self.nc = nc
        self.ops = []
        self.last_writer = {}
        self.readers = {}
        self.counts = {e: 0 for e in self.ENGINES}
        self.emitted = 0
        self.sems = {}
        self._barriers = []
        self._wm = {e: {f: 0 for f in self.ENGINES} for e in self.ENGINES}

    def init_sems(self, stack):
        for e in self.ENGINES:
            self.sems[e] = stack.enter_context(self.nc.semaphore(f"sem_{e}"))
        self.ccsems = [stack.enter_context(self.nc.semaphore(f"ccsem{i}"))
                       for i in range(2)]

    def op(self, eng, fn, reads=(), writes=(), inc=1, tag=""):
        idx = len(self.ops)
        deps = set()
        for b in reads:
            w = self.last_writer.get(id(b))
            if w is not None:
                deps.add(w)
        for b in writes:
            w = self.last_writer.get(id(b))
            if w is not None:
                deps.add(w)
            for r in self.readers.get(id(b), ()):
                deps.add(r)
        deps.discard(idx)
        self.counts[eng] += inc
        self.ops.append(dict(eng=eng, fn=fn, deps=deps, inc=inc, idx=idx,
                             tag=tag, count=self.counts[eng]))
        for b in reads:
            self.readers.setdefault(id(b), []).append(idx)
        for b in writes:
            self.last_writer[id(b)] = idx
            self.readers[id(b)] = [idx]
        return idx

    def barrier(self):
        self._barriers.append((len(self.ops), dict(self.counts)))

    def emit_block(self):
        start, end = self.emitted, len(self.ops)
        if start == end:
            return
        ops = self.ops
        per_eng = {e: [i for i in range(start, end) if ops[i]["eng"] == e]
                   for e in self.ENGINES}
        wm = self._wm
        sems = self.sems
        barriers = self._barriers

        def run_engine(engobj, ename):
            w = wm[ename]
            for i in per_eng[ename]:
                rec = ops[i]
                for bpos, bsnap in barriers:
                    if bpos <= i:
                        for f in self.ENGINES:
                            if f == ename:
                                w[f] = max(w[f], bsnap[f])
                            elif bsnap[f] > w[f]:
                                engobj.wait_ge(sems[f], bsnap[f])
                                w[f] = bsnap[f]
                need = {}
                for d in rec["deps"]:
                    dr = ops[d]
                    f = dr["eng"]
                    if f == ename == "tensor":
                        continue
                    if dr["count"] > w[f]:
                        need[f] = max(need.get(f, 0), dr["count"])
                for f in sorted(need):
                    engobj.wait_ge(sems[f], need[f])
                    w[f] = need[f]
                instr = rec["fn"](engobj)
                if instr is None:
                    raise RuntimeError(f"op {rec['tag']} returned None")
                instr.then_inc(sems[ename], rec["inc"])

        with self.nc.Block() as block:
            for ename in self.ENGINES:
                if per_eng[ename]:
                    getattr(block, ename)(
                        lambda engobj, _e=ename: run_engine(engobj, _e))
        self.emitted = end


def build_program():
    from contextlib import ExitStack
    nc = bass.Bass("TRN2", target_bir_lowering=False, num_devices=8)

    def din(name, shape, dt=F32):
        return nc.dram_tensor(name, shape, dt, kind="ExternalInput").ap()

    def dint(name, shape, dt=F32, dbg=False):
        kind = "ExternalOutput" if (dbg and DEBUG) else "Internal"
        return nc.dram_tensor(name, shape, dt, kind=kind).ap()

    xb_d = din("xb", [S, D])
    xown_d = din("xown", [T, D])
    wq_d = din("wq", [4, D, DH]); wk_d = din("wk", [4, D, DH]); wv_d = din("wv", [4, D, DH])
    bq_d = din("bq", [4, DH]); bk_d = din("bk", [4, DH]); bv_d = din("bv", [4, DH])
    wo_d = din("wo", [NH, DH, D])
    bo_d = din("bo", [16, 128])
    wut_d = din("wut", [D, DFF0]); wgt_d = din("wgt", [D, DFF0]); wdt_d = din("wdt", [DFF0, D])
    l1w_d = {k: din(f"l1{k}", [D, D]) for k in "qkvg"}
    l1b_d = {k: din(f"l1b{k}", [16, 128]) for k in "qkvg"}
    fwg_d = din("fwg", [DFF1, D]); fwu_d = din("fwu", [DFF1, D]); fwd_d = din("fwd", [D, DFF1])
    didx_d = din("didx", [NI * 128, 1], I32)
    zridx_d = din("zridx", [NH * 128, 1], I32)
    pmask_d = din("pmask", [8, 1])
    maskw_d = din("maskw", [128, 896], BF16)
    eye_d = din("eye", [128, 128])
    out_d = nc.dram_tensor("out", [T, D], F32, kind="ExternalOutput").ap()

    zcc_in = dint("zcc_in", [512, S], BF16)
    zcc_out = dint("zcc_out", [4096, S], BF16)
    scc_in = dint("scc_in", [1, D])
    scc_out = dint("scc_out", [8, D])
    offs_dram = dint("offs_dram", [1, D])
    s0_d = dint("s0_spill", [T, D], BF16, dbg=True)
    g0_d = dint("g0_spill", [T, D], BF16, dbg=True)
    x1_d = dint("x1_spill", [T, D], dbg=True)
    s1_d = dint("s1_spill", [T, D], BF16, dbg=True)
    t0_d = dint("t0_spill", [T, DFF1], dbg=True)
    ternd_d = dint("ternd_spill", [DFF0, D], BF16, dbg=True)
    qts_d = dint("qts_spill", [DFF1, T], BF16, dbg=True)
    qxs_d = dint("qxs_spill", [128, 16 * T], BF16, dbg=True)
    g1f_d = dint("g1f_spill", [T, D], BF16, dbg=True)
    if DEBUG:
        h_dbg = nc.dram_tensor("h_dbg", [DFF0, T], BF16,
                               kind="ExternalOutput").ap()

    P = Prog(nc)
    root = ExitStack()

    def sb(stk, name, shape, dt):
        return stk.enter_context(nc.sbuf_tensor(name, shape, dt))

    with root:
        P.init_sems(root)
        eye_f = sb(root, "eye_f", [128, 128], F32)
        eye_b = sb(root, "eye_b", [128, 128], BF16)
        ones_col_b = sb(root, "ones_col_b", [128, 1], BF16)
        ones_row_b = sb(root, "ones_row_b", [1, 128], BF16)
        ones_row_f = sb(root, "ones_row_f", [1, 128], F32)
        ones_col_f = sb(root, "ones_col_f", [128, 1], F32)
        one11_f = sb(root, "one11_f", [1, 1], F32)
        epsc = sb(root, "epsc", [128, 1], F32)
        thU = sb(root, "thU", [128, 3], F32)
        thG = sb(root, "thG", [128, 3], F32)
        thD = sb(root, "thD", [128, 3], F32)
        psA = root.enter_context(nc.psum_tensor("psA", [128, 512], F32))
        psB = root.enter_context(nc.psum_tensor("psB", [128, 512], F32))
        psE = root.enter_context(nc.psum_tensor("psE", [128, 512], F32))
        psF = root.enter_context(nc.psum_tensor("psF", [128, 512], F32))
        psTs = [root.enter_context(nc.psum_tensor(f"psT{i}", [128, 128], BF16))
                for i in range(3)]
        psV = root.enter_context(nc.psum_tensor("psV", [128, 256], F32))
        tp_ctr = [0]

        def dma(eng, dst_ap, src_ap, reads, writes, tag=""):
            P.op(eng, lambda e, d=dst_ap, s=src_ap: e.dma_start(out=d, in_=s),
                 reads=reads, writes=writes, inc=16, tag=tag)

        def vop(fn, reads, writes, eng="vector", tag=""):
            P.op(eng, fn, reads=reads, writes=writes, inc=1, tag=tag)

        def sop(fn, reads, writes, tag=""):
            P.op("scalar", fn, reads=reads, writes=writes, inc=1, tag=tag)

        def top(fn, reads, writes, tag=""):
            P.op("tensor", fn, reads=reads, writes=writes, inc=1, tag=tag)

        def memset(t, val, tag="ms"):
            vop(lambda e, tt=t, v=val: e.memset(tt[:], v), [], [t], tag=tag)

        def row_to_col(dst_ap, dst_buf, row_ap, row_bufs, tag):
            top(lambda e, r=row_ap: e.matmul(psV[:, 0:1], r, one11_f[:],
                                             start=True, stop=True),
                list(row_bufs) + [one11_f], [psV], tag + "_mm")
            vop(lambda e, d=dst_ap: e.tensor_copy(d, psV[:, 0:1]),
                [psV], [dst_buf], tag=tag + "_cp")

        dma("sync", eye_f[:], eye_d, [], [eye_f], "ld_eye")
        vop(lambda e: e.tensor_copy(eye_b[:], eye_f[:]), [eye_f], [eye_b], tag="eyeb")
        memset(ones_col_b, 1.0, "o1"); memset(ones_row_b, 1.0, "o2")
        memset(ones_row_f, 1.0, "o3"); memset(ones_col_f, 1.0, "o4")
        memset(one11_f, 1.0, "o5")
        memset(epsc, EPS_LN, "o6")

        def make_ln(stk, pref):
            sq = sb(stk, pref + "sq", [128, D], BF16)
            st_ = sb(stk, pref + "st", [128, 8], F32)
            nm = sb(stk, pref + "nm", [128, 1], F32)
            rs_ = sb(stk, pref + "rs", [128, 1], F32)

            def ln(xtile, out_bf_ap, out_bufs, tag, extra_f32=None):
                vop(lambda e: e.tensor_scalar(sq[:], xtile[:], 1.0, None,
                                              ALU.mult, ALU.add,
                                              accum_out=st_[:, 0:1]),
                    [xtile], [sq, st_], tag=tag + "_s")
                sop(lambda e: e.activation(sq[:], xtile[:], AF.Square,
                                           accum_out=st_[:, 1:2]),
                    [xtile], [sq, st_], tag + "_q")
                vop(lambda e: e.tensor_scalar(st_[:, 2:3], st_[:, 0:1],
                                              1.0 / D, None, ALU.mult),
                    [st_], [st_], tag=tag + "_m")
                vop(lambda e: e.tensor_tensor(st_[:, 3:4], st_[:, 2:3],
                                              st_[:, 2:3], ALU.mult),
                    [st_], [st_], tag=tag + "_m2")
                vop(lambda e: e.tensor_scalar(st_[:, 4:5], st_[:, 1:2],
                                              1.0 / D, None, ALU.mult),
                    [st_], [st_], tag=tag + "_e2")
                vop(lambda e: e.tensor_tensor(st_[:, 5:6], st_[:, 4:5],
                                              st_[:, 3:4], ALU.subtract),
                    [st_], [st_], tag=tag + "_v")
                sop(lambda e: e.activation(st_[:, 6:7], st_[:, 5:6], AF.Sqrt,
                                           bias=epsc[:, 0:1]),
                    [st_, epsc], [st_], tag + "_sd")
                vop(lambda e: e.reciprocal(rs_[:], st_[:, 6:7]),
                    [st_], [rs_], tag=tag + "_r")
                vop(lambda e: e.tensor_scalar(nm[:], st_[:, 2:3], -1.0, None,
                                              ALU.mult),
                    [st_], [nm], tag=tag + "_n")
                vop(lambda e: e.tensor_scalar(out_bf_ap, xtile[:], nm[:], rs_[:],
                                              ALU.add, ALU.mult),
                    [xtile, nm, rs_], out_bufs, tag=tag + "_x")
                if extra_f32 is not None:
                    ap_, bufs_ = extra_f32
                    vop(lambda e: e.tensor_scalar(ap_, xtile[:], nm[:], rs_[:],
                                                  ALU.add, ALU.mult),
                        [xtile, nm, rs_], bufs_, tag=tag + "_xf")
            return ln

        def transpose128(src_ap, src_bufs, dst_ap, dst_buf, tag):
            pt = psTs[tp_ctr[0] % 3]
            tp_ctr[0] += 1
            top(lambda e, s=src_ap, p=pt: e.transpose(p[:], s, eye_b[:]),
                list(src_bufs) + [eye_b], [pt], tag + "_t")
            vop(lambda e, d=dst_ap, p=pt: e.tensor_copy(d, p[:]),
                [pt], [dst_buf], tag=tag + "_c")

        stA = ExitStack()
        with stA:
            xnT = [sb(stA, f"xnT{i}", [128, S], BF16) for i in range(16)]
            xt = [sb(stA, "xt0", [128, D], F32)]
            xnb = sb(stA, "xnb", [128, D], BF16)
            lnA = make_ln(stA, "lA")

            for t in range(32):
                xi = xt[0]
                dma("sync", xi[:], xb_d[t * 128:(t + 1) * 128, :], [], [xi],
                    f"A_x{t}")
                lnA(xi, xnb[:], [xnb], f"A_ln{t}")
                for c in range(16):
                    transpose128(xnb[:, c * 128:(c + 1) * 128], [xnb],
                                 xnT[c][:, t * 128:(t + 1) * 128], xnT[c],
                                 f"A_tp{t}_{c}")

            accw = sb(stA, "accw", [128, 16], F32)
            wmt = [sb(stA, f"wmt{i}", [128, 512], F32) for i in range(2)]
            wabs = sb(stA, "wabs", [128, 512], BF16)
            wsum = sb(stA, "wsum", [1, 4], F32)
            memset(accw, 0.0, "accw0")

            def mean_pass(wsrc, nrow, ncol, base, tag):
                k = 0
                ntile = (nrow // 128) * (ncol // 512)
                for r in range(nrow // 128):
                    for c in range(ncol // 512):
                        wt = wmt[k % 2]
                        dma("sync", wt[:],
                            wsrc[r * 128:(r + 1) * 128, c * 512:(c + 1) * 512],
                            [], [wt], f"{tag}l{k}")
                        sop(lambda e, w=wt, kk=k: e.activation(
                                wabs[:], w[:], AF.Abs,
                                accum_out=accw[:, base + (kk % 2):base + (kk % 2) + 1]),
                            [wt], [wabs, accw], f"{tag}a{k}")
                        rd = base + 2 + (k % 2)
                        wr = base + 2 + ((k + 1) % 2)
                        vop(lambda e, kk=k, rd_=rd, wr_=wr: e.tensor_tensor(
                                accw[:, wr_:wr_ + 1],
                                accw[:, rd_:rd_ + 1],
                                accw[:, base + (kk % 2):base + (kk % 2) + 1],
                                ALU.add),
                            [accw], [accw], tag=f"{tag}c{k}")
                        k += 1
                return base + 2 + (ntile % 2)

            def mean_finish(fcol, th_t, idx, tag):
                top(lambda e: e.matmul(psV[0:1, 4:5],
                                       accw[:, fcol:fcol + 1],
                                       ones_col_f[:], start=True, stop=True),
                    [accw, ones_col_f], [psV], tag + "_r")
                vop(lambda e: e.tensor_scalar(wsum[0:1, idx:idx + 1],
                                              psV[0:1, 4:5],
                                              1.0 / 16777216.0, 1e-8,
                                              ALU.mult, ALU.add),
                    [psV], [wsum], tag=tag + "_t")
                top(lambda e: e.matmul(psV[:, 8:9], ones_row_f[:],
                                       wsum[0:1, idx:idx + 1],
                                       start=True, stop=True),
                    [ones_row_f, wsum], [psV], tag + "_b")
                vop(lambda e: e.tensor_copy(th_t[:, 2:3], psV[:, 8:9]),
                    [psV], [th_t], tag=tag + "_f")
                vop(lambda e: e.tensor_scalar(th_t[:, 0:1], psV[:, 8:9], 0.5,
                                              None, ALU.mult),
                    [psV], [th_t], tag=tag + "_p")
                vop(lambda e: e.tensor_scalar(th_t[:, 1:2], psV[:, 8:9], -0.5,
                                              None, ALU.mult),
                    [psV], [th_t], tag=tag + "_n")

            fc = mean_pass(wut_d, D, DFF0, 0, "mu")
            mean_finish(fc, thU, 0, "mu")
            fc = mean_pass(wgt_d, D, DFF0, 5, "mg")
            mean_finish(fc, thG, 1, "mg")
            fc = mean_pass(wdt_d, DFF0, D, 10, "md")
            mean_finish(fc, thD, 2, "md")

            maskw = sb(stA, "maskw_s", [128, 896], BF16)
            dma("sync", maskw[:], maskw_d, [], [maskw], "A_mask")
            wstg = sb(stA, "wstg", [128, 1024], F32)
            wbf = sb(stA, "wbf", [128, D], BF16)
            KT = sb(stA, "KT", [128, S], BF16)
            QT = sb(stA, "QT", [128, S], BF16)
            Vtok = sb(stA, "Vtok", [128, S], BF16)
            vtmp = sb(stA, "vtmp", [128, 512], BF16)
            sqb = sb(stA, "sqb", [128, 512], BF16)
            rsc = sb(stA, "rsc", [1, 512], F32)
            mks = sb(stA, "mks", [1, 4], F32)
            negM = sb(stA, "negM", [1, 512], BF16)
            aT = [sb(stA, f"aT{i}", [128, 512], BF16) for i in range(2)]
            rb = sb(stA, "rb", [128, 512], F32)
            ztile = sb(stA, "ztile", [128, 512], BF16)
            bcol = sb(stA, "bcol", [128, 3], F32)
            brow = sb(stA, "brow", [1, 128], F32)

            def bias_col(bsrc, p, col, tag):
                dma("sync", brow[:], bsrc[p:p + 1, :], [], [brow], tag + "_l")
                row_to_col(bcol[:, col:col + 1], bcol, brow[:], [brow], tag)

            def proj(wsrc, p, dst, bias_c, tag):
                wrs = wsrc[p].rearrange("(c p) e -> p c e", p=128)
                wsv = wstg[:].rearrange("p (c e) -> p c e", e=128)
                for hf in range(2):
                    dma("sync", wsv, wrs[:, hf * 8:(hf + 1) * 8, :],
                        [], [wstg], tag + f"_w{hf}")
                    vop(lambda e, h_=hf: e.tensor_copy(
                            wbf[:, h_ * 1024:(h_ + 1) * 1024], wstg[:]),
                        [wstg], [wbf], tag=tag + f"_v{hf}")
                for tk in range(8):
                    def mm(e, tkk=tk):
                        for c in range(16):
                            last = e.matmul(
                                psA[:], wbf[:, c * 128:(c + 1) * 128],
                                xnT[c][:, tkk * 512:(tkk + 1) * 512],
                                start=(c == 0), stop=(c == 15))
                        return last
                    top(mm, [wbf] + xnT, [psA], f"{tag}m{tk}")
                    sop(lambda e, tkk=tk: e.activation(
                            dst[:, tkk * 512:(tkk + 1) * 512], psA[:], AF.Identity,
                            bias=bcol[:, bias_c:bias_c + 1]),
                        [psA, bcol], [dst], f"{tag}e{tk}")

            for p in range(4):
                bias_col(bk_d, p, 1, f"p{p}bk")
                bias_col(bq_d, p, 0, f"p{p}bq")
                bias_col(bv_d, p, 2, f"p{p}bv")
                proj(wk_d, p, KT, 1, f"p{p}K")
                memset(mks, 0.0, f"p{p}km0")
                for tk in range(8):
                    sop(lambda e, tkk=tk: e.activation(
                            sqb[:], KT[:, tkk * 512:(tkk + 1) * 512], AF.Square),
                        [KT], [sqb], f"p{p}ks{tk}")
                    top(lambda e: e.matmul(psF[0:1, :], ones_col_b[:], sqb[:],
                                           start=True, stop=True),
                        [sqb, ones_col_b], [psF], f"p{p}kn{tk}")
                    vop(lambda e: e.tensor_copy(rsc[0:1, :], psF[0:1, :]),
                        [psF], [rsc], tag=f"p{p}kc{tk}")
                    vop(lambda e: e.reduce_max(mks[0:1, 2:3], rsc[0:1, :],
                                               axis=AX.X),
                        [rsc], [mks], tag=f"p{p}kd{tk}")
                    vop(lambda e, tkk=tk: e.tensor_tensor(
                            mks[0:1, 3 if tkk % 2 == 0 else 0:
                                4 if tkk % 2 == 0 else 1],
                            mks[0:1, 0 if tkk % 2 == 0 else 3:
                                1 if tkk % 2 == 0 else 4],
                            mks[0:1, 2:3], ALU.max),
                        [mks], [mks], tag=f"p{p}kx{tk}")
                sop(lambda e: e.activation(mks[0:1, 1:2], mks[0:1, 0:1], AF.Sqrt),
                    [mks], [mks], f"p{p}kq")
                proj(wq_d, p, QT, 0, f"p{p}Q")
                wrs_v = wv_d[p].rearrange("(c p) e -> p c e", p=128)
                wsv_v = wstg[:].rearrange("p (c e) -> p c e", e=128)
                for hf in range(2):
                    dma("sync", wsv_v, wrs_v[:, hf * 8:(hf + 1) * 8, :],
                        [], [wstg], f"p{p}Vw{hf}")
                    vop(lambda e, h_=hf: e.tensor_copy(
                            wbf[:, h_ * 1024:(h_ + 1) * 1024], wstg[:]),
                        [wstg], [wbf], tag=f"p{p}Vv{hf}")
                for tk in range(8):
                    def mmv(e, tkk=tk):
                        for c in range(16):
                            last = e.matmul(
                                psB[:], wbf[:, c * 128:(c + 1) * 128],
                                xnT[c][:, tkk * 512:(tkk + 1) * 512],
                                start=(c == 0), stop=(c == 15))
                        return last
                    top(mmv, [wbf] + xnT, [psB], f"p{p}vm{tk}")
                    sop(lambda e: e.activation(vtmp[:], psB[:], AF.Identity,
                                               bias=bcol[:, 2:3]),
                        [psB, bcol], [vtmp], f"p{p}ve{tk}")
                    for k in range(4):
                        transpose128(
                            vtmp[:, k * 128:(k + 1) * 128], [vtmp],
                            Vtok[:, tk * 512 + k * 128:tk * 512 + (k + 1) * 128],
                            Vtok, f"p{p}vt{tk}_{k}")

                for tq in range(8):
                    sop(lambda e, tqq=tq: e.activation(
                            sqb[:], QT[:, tqq * 512:(tqq + 1) * 512], AF.Square),
                        [QT], [sqb], f"p{p}q{tq}s")
                    top(lambda e: e.matmul(psF[0:1, :], ones_col_b[:], sqb[:],
                                           start=True, stop=True),
                        [sqb, ones_col_b], [psF], f"p{p}q{tq}n")
                    sop(lambda e: e.activation(rsc[0:1, :], psF[0:1, :], AF.Sqrt),
                        [psF], [rsc], f"p{p}q{tq}r")
                    vop(lambda e: e.tensor_scalar(negM[0:1, :], rsc[0:1, :],
                                                  mks[0:1, 1:2], -1.0,
                                                  ALU.mult, ALU.mult),
                        [rsc, mks], [negM], tag=f"p{p}q{tq}m")
                    ntk = 4 * tq + 4
                    for tkc in range(ntk):
                        ab = aT[tkc % 2]
                        dg = tkc * 128 - tq * 512
                        def mms(e, tkk=tkc, tqq=tq, dgg=dg):
                            e.matmul(psA[:], KT[:, tkk * 128:(tkk + 1) * 128],
                                     QT[:, tqq * 512:(tqq + 1) * 512],
                                     start=True, stop=False)
                            last = e.matmul(psA[:], ones_row_b[:], negM[0:1, :],
                                            start=False, stop=(dgg < 0))
                            if dgg >= 0:
                                off = 384 - dgg
                                last = e.matmul(psA[:], eye_b[:],
                                                maskw[:, off:off + 512],
                                                start=False, stop=True)
                            return last
                        top(mms, [KT, QT, negM, ones_row_b, eye_b, maskw],
                            [psA], f"p{p}q{tq}k{tkc}s")
                        sop(lambda e, b=ab: e.activation(b[:], psA[:], AF.Exp,
                                                         scale=RS),
                            [psA], [ab], f"p{p}q{tq}k{tkc}e")
                        def mmz(e, tkk=tkc, b=ab, fi=(tkc == 0),
                                la=(tkc == ntk - 1)):
                            e.matmul(psB[:], Vtok[:, tkk * 128:(tkk + 1) * 128],
                                     b[:], start=fi, stop=la)
                            return e.matmul(psE[0:1, :], ones_col_b[:], b[:],
                                            start=fi, stop=la)
                        top(mmz, [Vtok, ab, ones_col_b], [psB, psE],
                            f"p{p}q{tq}k{tkc}z")
                    vop(lambda e: e.reciprocal(rb[0:1, :], psE[0:1, :]),
                        [psE], [rb], tag=f"p{p}q{tq}i")
                    top(lambda e: e.matmul(psA[:], ones_row_f[:], rb[0:1, :],
                                           start=True, stop=True),
                        [ones_row_f, rb], [psA], f"p{p}q{tq}b")
                    vop(lambda e: e.tensor_copy(rb[:], psA[:]),
                        [psA], [rb], tag=f"p{p}q{tq}c")
                    vop(lambda e: e.tensor_tensor(ztile[:], psB[:], rb[:],
                                                  ALU.mult),
                        [psB, rb], [ztile], tag=f"p{p}q{tq}z"),
                    dma("gpsimd",
                        zcc_in[p * 128:(p + 1) * 128, tq * 512:(tq + 1) * 512],
                        ztile[:], [ztile], [zcc_in], f"p{p}q{tq}d")

            def cc_z(e):
                e.collective_compute(
                    "AllGather", ALU.bypass,
                    replica_groups=[[0, 1, 2, 3, 4, 5, 6, 7]],
                    ins=[zcc_in.opt()], outs=[zcc_out.opt()]).then_inc(
                        P.ccsems[0], 1)
                e.wait_ge(P.ccsems[0], 1)
                return e.nop()
            P.op("gpsimd", cc_z, reads=[zcc_in], writes=[zcc_out], inc=1,
                 tag="cc_z")
            P.barrier()
            P.emit_block()

        stB = ExitStack()
        with stB:
            xnoT = [sb(stB, f"xnoT{i}", [128, T], BF16) for i in range(16)]
            stB1 = ExitStack()
            with stB1:
                xt0 = sb(stB1, "xtB", [128, D], F32)
                xnb2 = sb(stB1, "xnb2", [128, D], BF16)
                lnB = make_ln(stB1, "lB")
                for t in range(8):
                    dma("sync", xt0[:], xown_d[t * 128:(t + 1) * 128, :],
                        [], [xt0], f"B_x{t}")
                    lnB(xt0, xnb2[:], [xnb2], f"B_ln{t}")
                    for c in range(16):
                        transpose128(xnb2[:, c * 128:(c + 1) * 128], [xnb2],
                                     xnoT[c][:, t * 128:(t + 1) * 128],
                                     xnoT[c], f"B_tp{t}_{c}")
                zsl = [sb(stB1, f"zsl{i}", [128, T], BF16) for i in range(16)]
                zidx = sb(stB1, "zidx", [128, 1], I32)
                wost = sb(stB1, "wost", [128, 512], F32)
                wobf = [sb(stB1, f"wobf{i}", [128, 512], BF16) for i in range(16)]
                borow = sb(stB1, "borow", [1, 512], F32)
                borowb = sb(stB1, "borowb", [1, 512], BF16)
                sev = sb(stB1, "sev", [128, 512], BF16)
                zv = zcc_out.rearrange("r (q c) -> (r q) c", c=T)
                bo_flat = bo_d.rearrange("a b -> (a b)").unsqueeze(0)
                for h in range(NH):
                    dma("gpsimd", zidx[:],
                        zridx_d[h * 128:(h + 1) * 128, 0:1],
                        [], [zidx], f"B_zi{h}")
                    P.op("gpsimd",
                         lambda e, hh=h: e.indirect_dma_start(
                             zsl[hh][:], None, zv,
                             bass.IndirectOffsetOnAxis(zidx[:], 0)),
                         reads=[zcc_out, zidx], writes=[zsl[h]], inc=16,
                         tag=f"B_zg{h}")
                for dg in range(4):
                    dma("sync", borow[0:1, :],
                        bo_flat[0:1, dg * 512:(dg + 1) * 512],
                        [], [borow], f"B_bo{dg}")
                    vop(lambda e: e.tensor_copy(borowb[0:1, :], borow[0:1, :]),
                        [borow], [borowb], tag=f"B_boc{dg}")
                    for h in range(NH):
                        dma("sync", wost[:], wo_d[h, :, dg * 512:(dg + 1) * 512],
                            [], [wost], f"B_wo{dg}_{h}")
                        vop(lambda e, hh=h: e.tensor_copy(wobf[hh][:], wost[:]),
                            [wost], [wobf[h]], tag=f"B_wc{dg}_{h}")
                    for tc in range(8):
                        def mmo(e, tcc=tc):
                            for h in range(NH):
                                e.matmul(psA[:],
                                         zsl[h][:, tcc * 128:(tcc + 1) * 128],
                                         wobf[h][:], start=(h == 0), stop=False)
                            return e.matmul(psA[:], ones_row_b[:], borowb[0:1, :],
                                            start=False, stop=True)
                        top(mmo, zsl + wobf + [ones_row_b, borowb], [psA],
                            f"B_om{dg}_{tc}")
                        sop(lambda e: e.activation(sev[:], psA[:], AF.Identity),
                            [psA], [sev], f"B_oe{dg}_{tc}")
                        dma("gpsimd",
                            s0_d[tc * 128:(tc + 1) * 128, dg * 512:(dg + 1) * 512],
                            sev[:], [sev], [s0_d], f"B_od{dg}_{tc}")
                P.barrier()
                P.emit_block()

            hT = [sb(stB, f"hT{i}", [128, T], BF16) for i in range(64)]
            stB2 = ExitStack()
            with stB2:
                tstg = [sb(stB2, f"tstg{i}", [128, 256], F32) for i in range(2)]
                ternu = [sb(stB2, f"tu{i}", [128, 256], BF16) for i in range(16)]
                terng = [sb(stB2, f"tg{i}", [128, 256], BF16) for i in range(16)]
                tlt = sb(stB2, "tlt", [128, 256], BF16)
                tgt2 = sb(stB2, "tgt2", [128, 256], BF16)
                sil = sb(stB2, "sil", [128, 512], BF16)
                gsc = sb(stB2, "gsc", [128, 512], BF16)

                def ternary(dst, src, th, tag):
                    vop(lambda e: e.tensor_scalar(tgt2[:], src[:], th[:, 0:1],
                                                  None, ALU.is_gt),
                        [src, th], [tgt2], tag=tag + "g")
                    vop(lambda e: e.tensor_scalar(tlt[:], src[:], th[:, 1:2],
                                                  None, ALU.is_lt),
                        [src, th], [tlt], eng="gpsimd", tag=tag + "l")
                    vop(lambda e: e.tensor_tensor(dst[:], tgt2[:], tlt[:],
                                                  ALU.subtract),
                        [tgt2, tlt], [dst], tag=tag + "s")

                for es in range(32):
                    for c in range(16):
                        stg = tstg[c % 2]
                        dma("sync", stg[:],
                            wut_d[c * 128:(c + 1) * 128, es * 256:(es + 1) * 256],
                            [], [stg], f"B2u{es}_{c}")
                        ternary(ternu[c], stg, thU, f"B2qu{es}_{c}")
                        dma("sync", stg[:],
                            wgt_d[c * 128:(c + 1) * 128, es * 256:(es + 1) * 256],
                            [], [stg], f"B2g{es}_{c}")
                        ternary(terng[c], stg, thG, f"B2qg{es}_{c}")
                    for e1 in range(2):
                        ech = es * 2 + e1
                        for tk in range(2):
                            def mmu(e, ee=e1, tkk=tk):
                                for c in range(16):
                                    e.matmul(psE[:],
                                             ternu[c][:, ee * 128:(ee + 1) * 128],
                                             xnoT[c][:, tkk * 512:(tkk + 1) * 512],
                                             start=(c == 0), stop=(c == 15))
                                for c in range(16):
                                    last = e.matmul(
                                        psF[:],
                                        terng[c][:, ee * 128:(ee + 1) * 128],
                                        xnoT[c][:, tkk * 512:(tkk + 1) * 512],
                                        start=(c == 0), stop=(c == 15))
                                return last
                            top(mmu, ternu + terng + xnoT, [psE, psF],
                                f"B2m{ech}_{tk}")
                            sop(lambda e: e.activation(sil[:], psE[:], AF.Silu,
                                                       scale=thU[:, 2:3]),
                                [psE, thU], [sil], f"B2s{ech}_{tk}")
                            sop(lambda e: e.activation(gsc[:], psF[:],
                                                       AF.Identity,
                                                       scale=thG[:, 2:3]),
                                [psF, thG], [gsc], f"B2g2{ech}_{tk}")
                            vop(lambda e, ec=ech, tkk=tk: e.tensor_tensor(
                                    hT[ec][:, tkk * 512:(tkk + 1) * 512],
                                    sil[:], gsc[:], ALU.mult),
                                [sil, gsc], [hT[ech]], tag=f"B2h{ech}_{tk}")
                if DEBUG:
                    for ec in range(64):
                        dma("gpsimd", h_dbg[ec * 128:(ec + 1) * 128, :],
                            hT[ec][:], [hT[ec]], [h_dbg], f"dbg_h{ec}")
                P.barrier()
                P.emit_block()

            stB3 = ExitStack()
            with stB3:
                dstg = [sb(stB3, f"dstg{i}", [128, D], F32) for i in range(2)]
                dt_b = sb(stB3, "dt_b", [128, D], BF16)
                dlt = sb(stB3, "dlt", [128, D], BF16)
                dgt = sb(stB3, "dgt", [128, D], BF16)
                for ec in range(64):
                    stg = dstg[ec % 2]
                    dma("sync", stg[:], wdt_d[ec * 128:(ec + 1) * 128, :],
                        [], [stg], f"B3l{ec}")
                    vop(lambda e, s_=stg: e.tensor_scalar(dgt[:], s_[:],
                                                  thD[:, 0:1],
                                                  None, ALU.is_gt),
                        [stg, thD], [dgt], tag=f"B3g{ec}")
                    vop(lambda e, s_=stg: e.tensor_scalar(dlt[:], s_[:],
                                                  thD[:, 1:2],
                                                  None, ALU.is_lt),
                        [stg, thD], [dlt], eng="gpsimd", tag=f"B3t{ec}")
                    vop(lambda e: e.tensor_tensor(dt_b[:], dgt[:], dlt[:],
                                                  ALU.subtract),
                        [dgt, dlt], [dt_b], tag=f"B3s{ec}")
                    dma("gpsimd", ternd_d[ec * 128:(ec + 1) * 128, :], dt_b[:],
                        [dt_b], [ternd_d], f"B3d{ec}")
                tdt = [sb(stB3, f"tdt{i}", [128, 512], BF16) for i in range(2)]
                g0e = sb(stB3, "g0e", [128, 512], BF16)
                PS4 = [psA, psB, psE, psF]
                for dg in range(4):
                    for th_half in range(2):
                        for ec in range(64):
                            tb = tdt[ec % 2]
                            dma("sync", tb[:],
                                ternd_d[ec * 128:(ec + 1) * 128,
                                        dg * 512:(dg + 1) * 512],
                                [ternd_d], [tb], f"B3r{dg}_{th_half}_{ec}")
                            def mmd(e, tb_=tb, ecc=ec, thh=th_half):
                                for t4 in range(4):
                                    tc = thh * 4 + t4
                                    last = e.matmul(
                                        PS4[t4][:],
                                        hT[ecc][:, tc * 128:(tc + 1) * 128],
                                        tb_[:], start=(ecc == 0),
                                        stop=(ecc == 63))
                                return last
                            top(mmd, [tb] + hT, PS4, f"B3m{dg}_{th_half}_{ec}")
                        for t4 in range(4):
                            tc = th_half * 4 + t4
                            sop(lambda e, t4_=t4: e.activation(
                                    g0e[:], PS4[t4_][:], AF.Silu,
                                    scale=thD[:, 2:3]),
                                [PS4[t4], thD], [g0e], f"B3e{dg}_{th_half}_{t4}")
                            dma("gpsimd",
                                g0_d[tc * 128:(tc + 1) * 128,
                                     dg * 512:(dg + 1) * 512],
                                g0e[:], [g0e], [g0_d], f"B3o{dg}_{th_half}_{t4}")
                P.barrier()
                P.emit_block()
            P.barrier()
            P.emit_block()

        stB4 = ExitStack()
        with stB4:
            s0t = sb(stB4, "s0t", [128, D], BF16)
            g0t = sb(stB4, "g0t", [128, D], BF16)
            xot = sb(stB4, "xot", [128, D], F32)
            sgt = sb(stB4, "sgt", [128, D], F32)
            x1o = sb(stB4, "x1o", [128, D], F32)
            for tc in range(8):
                dma("sync", s0t[:], s0_d[tc * 128:(tc + 1) * 128, :], [], [s0t],
                    f"B4s{tc}")
                dma("sync", g0t[:], g0_d[tc * 128:(tc + 1) * 128, :], [], [g0t],
                    f"B4g{tc}")
                dma("sync", xot[:], xown_d[tc * 128:(tc + 1) * 128, :], [], [xot],
                    f"B4x{tc}")
                vop(lambda e: e.tensor_tensor(sgt[:], s0t[:], g0t[:], ALU.mult),
                    [s0t, g0t], [sgt], tag=f"B4m{tc}")
                vop(lambda e: e.tensor_tensor(x1o[:], xot[:], sgt[:], ALU.add),
                    [xot, sgt], [x1o], tag=f"B4a{tc}")
                dma("gpsimd", x1_d[tc * 128:(tc + 1) * 128, :], x1o[:],
                    [x1o], [x1_d], f"B4o{tc}")
            P.barrier()
            P.emit_block()

        stDo = ExitStack()
        with stDo:
            qT = sb(stDo, "qT", [128, 16 * T], BF16)
            gT = sb(stDo, "gT", [128, 16 * T], BF16)
            kvT = sb(stDo, "kvT", [128, 16 * T], BF16)
            stD = ExitStack()
            with stD:
                xn1T = sb(stD, "xn1T", [128, 16 * T], BF16)
                stD1 = ExitStack()
                with stD1:
                    qxsT = sb(stD1, "qxsT", [128, 16 * T], BF16)
                    x1t = sb(stD1, "x1t", [128, D], F32)
                    xn1b = sb(stD1, "xn1b", [128, D], BF16)
                    xn1f = sb(stD1, "xn1f", [128, D], F32)
                    qarg = sb(stD1, "qarg", [128, D], F32)
                    q8 = sb(stD1, "q8", [128, D], U8)
                    qxsb = sb(stD1, "qxsb", [128, D], BF16)
                    rmx = sb(stD1, "rmx", [128, 4], F32)
                    lnD = make_ln(stD1, "lD")
                    for t in range(8):
                        dma("sync", x1t[:], x1_d[t * 128:(t + 1) * 128, :],
                            [], [x1t], f"D_x{t}")
                        lnD(x1t, xn1b[:], [xn1b], f"D_ln{t}",
                            extra_f32=(xn1f[:], [xn1f]))
                        vop(lambda e: e.reduce_max(rmx[:, 0:1], xn1f[:],
                                                   axis=AX.X,
                                                   apply_absolute_value=True),
                            [xn1f], [rmx], tag=f"D_rm{t}")
                        vop(lambda e: e.reciprocal(rmx[:, 1:2], rmx[:, 0:1]),
                            [rmx], [rmx], tag=f"D_rr{t}")
                        vop(lambda e: e.tensor_scalar(rmx[:, 2:3], rmx[:, 1:2],
                                                      224.0, None, ALU.mult),
                            [rmx], [rmx], tag=f"D_rs{t}")
                        vop(lambda e: e.tensor_scalar(rmx[:, 3:4], rmx[:, 0:1],
                                                      1.0 / 224.0, None,
                                                      ALU.mult),
                            [rmx], [rmx], tag=f"D_rd{t}")
                        vop(lambda e: e.tensor_scalar(qarg[:], xn1f[:],
                                                      rmx[:, 2:3], None,
                                                      ALU.mult),
                            [xn1f, rmx], [qarg], tag=f"D_qa{t}")
                        sop(lambda e: e.copy(q8[:].bitcast(FP8), qarg[:]),
                            [qarg], [q8], f"D_q8{t}")
                        sop(lambda e: e.activation(qxsb[:], q8[:].bitcast(FP8),
                                                   AF.Identity, scale=rmx[:, 3:4]),
                            [q8, rmx], [qxsb], f"D_qs{t}")
                        for c in range(16):
                            transpose128(xn1b[:, c * 128:(c + 1) * 128], [xn1b],
                                         xn1T[:, c * T + t * 128:
                                              c * T + (t + 1) * 128],
                                         xn1T, f"D_t1{t}_{c}")
                            transpose128(qxsb[:, c * 128:(c + 1) * 128], [qxsb],
                                         qxsT[:, c * T + t * 128:
                                              c * T + (t + 1) * 128],
                                         qxsT, f"D_t2{t}_{c}")
                    dma("gpsimd", qxs_d, qxsT[:], [qxsT], [qxs_d], "D_qxspill")
                    P.barrier()
                    P.emit_block()

                stD2 = ExitStack()
                with stD2:
                    kT = sb(stD2, "kT", [128, 16 * T], BF16)
                    wst1 = [sb(stD2, f"wst1{i}", [128, 512], F32)
                            for i in range(2)]
                    wbf1 = [sb(stD2, f"wbf1{i}", [128, 512], BF16)
                            for i in range(16)]
                    bc1 = sb(stD2, "bc1", [128, 16], F32)
                    br1 = sb(stD2, "br1", [1, 128], F32)
                    vev = sb(stD2, "vev", [128, 512], BF16)

                    def l1proj(key, dst, act, tag, is_v=False):
                        wsrc = l1w_d[key]
                        bsrc = l1b_d[key]
                        for i in range(16):
                            dma("sync", br1[:], bsrc[i:i + 1, :], [], [br1],
                                f"{tag}bl{i}")
                            row_to_col(bc1[:, i:i + 1], bc1, br1[:], [br1],
                                       f"{tag}bc{i}")
                        for eb in range(4):
                            for c in range(16):
                                w_ = wst1[c % 2]
                                dma("sync", w_[:],
                                    wsrc[c * 128:(c + 1) * 128,
                                         eb * 512:(eb + 1) * 512],
                                    [], [w_], f"{tag}w{eb}_{c}")
                                vop(lambda e, cc=c, ww=w_: e.tensor_copy(
                                        wbf1[cc][:], ww[:]),
                                    [w_], [wbf1[c]], tag=f"{tag}v{eb}_{c}")
                            for e1 in range(4):
                                ec = eb * 4 + e1
                                for tk in range(2):
                                    def mmp(e, ee=e1, tkk=tk):
                                        for c in range(16):
                                            last = e.matmul(
                                                psA[:],
                                                wbf1[c][:, ee * 128:
                                                        (ee + 1) * 128],
                                                xn1T[:, c * T + tkk * 512:
                                                     c * T + (tkk + 1) * 512],
                                                start=(c == 0), stop=(c == 15))
                                        return last
                                    top(mmp, wbf1 + [xn1T], [psA],
                                        f"{tag}m{ec}_{tk}")
                                    if is_v:
                                        sop(lambda e, ecc=ec: e.activation(
                                                vev[:], psA[:], AF.Identity,
                                                bias=bc1[:, ecc:ecc + 1]),
                                            [psA, bc1], [vev],
                                            f"{tag}e{ec}_{tk}")
                                        vop(lambda e, ecc=ec, tkk=tk:
                                            e.tensor_tensor(
                                                kvT[:, ecc * T + tkk * 512:
                                                    ecc * T + (tkk + 1) * 512],
                                                kT[:, ecc * T + tkk * 512:
                                                   ecc * T + (tkk + 1) * 512],
                                                vev[:], ALU.mult),
                                            [kT, vev], [kvT],
                                            tag=f"{tag}kv{ec}_{tk}")
                                    else:
                                        sop(lambda e, ecc=ec, tkk=tk:
                                            e.activation(
                                                dst[:, ecc * T + tkk * 512:
                                                    ecc * T + (tkk + 1) * 512],
                                                psA[:], act,
                                                bias=bc1[:, ecc:ecc + 1]),
                                            [psA, bc1], [dst],
                                            f"{tag}e{ec}_{tk}")

                    l1proj("q", qT, AF.Identity, "Dq")
                    l1proj("k", kT, AF.Identity, "Dk")
                    l1proj("v", None, AF.Identity, "Dv", is_v=True)
                    l1proj("g", gT, AF.Sigmoid, "Dg")
                    P.barrier()
                    P.emit_block()
                P.barrier()
                P.emit_block()

            stD3 = ExitStack()
            with stD3:
                ssc = sb(stD3, "ssc", [128, 16], F32)
                sscb = sb(stD3, "sscb", [128, 16], F32)
                for ec in range(16):
                    vop(lambda e, ecc=ec: e.reduce_sum(
                            ssc[:, ecc:ecc + 1],
                            kvT[:, ecc * T:(ecc + 1) * T], axis=AX.X),
                        [kvT], [ssc], tag=f"D_ss{ec}")
                vop(lambda e: e.tensor_copy(sscb[:], ssc[:]), [ssc], [sscb],
                    tag="D_sscp")
                top(lambda e: e.transpose(psV[0:16, 0:128], sscb[:], eye_f[:]),
                    [sscb, eye_f], [psV], "D_sst")
                ssrow = sb(stD3, "ssrow", [16, 128], F32)
                vop(lambda e: e.tensor_copy(ssrow[:], psV[0:16, 0:128]),
                    [psV], [ssrow], tag="D_ssv")
                dma("gpsimd",
                    scc_in.rearrange("o (a b) -> (o a) b", b=128),
                    ssrow[:], [ssrow], [scc_in], "D_ssd")
                def cc_s(e):
                    e.collective_compute(
                        "AllGather", ALU.bypass,
                        replica_groups=[[0, 1, 2, 3, 4, 5, 6, 7]],
                        ins=[scc_in.opt()], outs=[scc_out.opt()]).then_inc(
                            P.ccsems[1], 1)
                    e.wait_ge(P.ccsems[1], 1)
                    return e.nop()
                P.op("gpsimd", cc_s, reads=[scc_in], writes=[scc_out], inc=1,
                     tag="cc_s")
                gsb = sb(stD3, "gsb", [8, D], F32)
                pmk = sb(stD3, "pmk", [8, 1], F32)
                offr = sb(stD3, "offr", [1, D], F32)
                offc = sb(stD3, "offc", [128, 16], F32)
                dma("sync", gsb[:], scc_out, [scc_out], [gsb], "D_gl")
                dma("sync", pmk[:], pmask_d, [], [pmk], "D_pm")
                for j in range(4):
                    top(lambda e, jj=j: e.matmul(
                            psA[0:1, :], pmk[:],
                            gsb[:, jj * 512:(jj + 1) * 512],
                            start=True, stop=True),
                        [pmk, gsb], [psA], f"D_of{j}")
                    vop(lambda e, jj=j: e.tensor_copy(
                            offr[0:1, jj * 512:(jj + 1) * 512], psA[0:1, :]),
                        [psA], [offr], tag=f"D_oc{j}")
                for ec in range(16):
                    row_to_col(offc[:, ec:ec + 1], offc,
                               offr[0:1, ec * 128:(ec + 1) * 128], [offr],
                               f"D_ol{ec}")
                kvsT = sb(stD3, "kvsT", [128, 16 * T], BF16)
                for ec in range(16):
                    vop(lambda e, ecc=ec: e.tensor_tensor_scan(
                            kvsT[:, ecc * T:(ecc + 1) * T],
                            kvT[:, ecc * T:(ecc + 1) * T],
                            kvT[:, ecc * T:(ecc + 1) * T],
                            offc[:, ecc:ecc + 1], ALU.add, ALU.bypass),
                        [kvT, offc], [kvsT], tag=f"D_sc{ec}")
                s1f = sb(stD3, "s1f", [128, 16 * T], BF16)
                vop(lambda e: e.tensor_tensor(s1f[:], qT[:], kvsT[:], ALU.mult),
                    [qT, kvsT], [s1f], tag="D_s1a")
                vop(lambda e: e.tensor_tensor(kvsT[:], s1f[:], gT[:], ALU.mult),
                    [s1f, gT], [kvsT], tag="D_s1b")
                stok = sb(stD3, "stok", [128, D], BF16)
                for tc in range(8):
                    for ec in range(16):
                        transpose128(
                            kvsT[:, ec * T + tc * 128:ec * T + (tc + 1) * 128],
                            [kvsT], stok[:, ec * 128:(ec + 1) * 128], stok,
                            f"D_st{tc}_{ec}")
                    dma("gpsimd", s1_d[tc * 128:(tc + 1) * 128, :], stok[:],
                        [stok], [s1_d], f"D_so{tc}")
                P.barrier()
                P.emit_block()

        if True:
            stE1 = ExitStack()
            with stE1:
                qxsT = sb(stE1, "qxsT_e", [128, 16 * T], BF16)
                dma("sync", qxsT[:], qxs_d, [qxs_d], [qxsT], "E1_qxld")
                fst = [sb(stE1, f"fst{i}", [128, D], F32) for i in range(2)]
                farg = sb(stE1, "farg", [128, D], F32)
                f8 = sb(stE1, "f8", [128, D], U8)
                fqc = sb(stE1, "fqc", [128, D], BF16)
                rmw = sb(stE1, "rmw", [128, 4], F32)
                qwsu = sb(stE1, "qwsu", [128, 16 * 512], BF16)
                qwsg = sb(stE1, "qwsg", [128, 16 * 512], BF16)
                silE = sb(stE1, "silE", [128, 512], BF16)
                t0e = sb(stE1, "t0e", [128, 512], F32)
                rmt = sb(stE1, "rmt", [128, 16], F32)
                rmt2 = sb(stE1, "rmt2", [128, 8], F32)
                memset(rmt, 0.0, "E_rmt0")

                def quant_rows(wsrc, r0, dstws, ich, tag):
                    stg = fst[ich % 2]
                    dma("sync", stg[:], wsrc[r0:r0 + 128, :], [], [stg],
                        tag + "l")
                    vop(lambda e: e.reduce_max(rmw[:, 0:1], stg[:], axis=AX.X,
                                               apply_absolute_value=True),
                        [stg], [rmw], tag=tag + "m")
                    vop(lambda e: e.reciprocal(rmw[:, 1:2], rmw[:, 0:1]),
                        [rmw], [rmw], tag=tag + "r")
                    vop(lambda e: e.tensor_scalar(rmw[:, 2:3], rmw[:, 1:2],
                                                  224.0, None, ALU.mult),
                        [rmw], [rmw], tag=tag + "s")
                    vop(lambda e: e.tensor_scalar(rmw[:, 3:4], rmw[:, 0:1],
                                                  1.0 / 224.0, None, ALU.mult),
                        [rmw], [rmw], tag=tag + "d")
                    vop(lambda e: e.tensor_scalar(farg[:], stg[:], rmw[:, 2:3],
                                                  None, ALU.mult),
                        [stg, rmw], [farg], tag=tag + "a")
                    sop(lambda e: e.copy(f8[:].bitcast(FP8), farg[:]),
                        [farg], [f8], tag + "8")
                    sop(lambda e: e.activation(fqc[:], f8[:].bitcast(FP8),
                                               AF.Identity, scale=rmw[:, 3:4]),
                        [f8, rmw], [fqc], tag + "q")
                    for c in range(16):
                        transpose128(fqc[:, c * 128:(c + 1) * 128], [fqc],
                                     dstws[:, c * 512 + ich * 128:
                                           c * 512 + (ich + 1) * 128],
                                     dstws, f"{tag}t{c}")

                for isl in range(NSL):
                    for ich in range(4):
                        quant_rows(fwu_d, isl * 512 + ich * 128, qwsu, ich,
                                   f"E1u{isl}_{ich}")
                        quant_rows(fwg_d, isl * 512 + ich * 128, qwsg, ich,
                                   f"E1g{isl}_{ich}")
                    for tc in range(8):
                        def mmug(e, tcc=tc):
                            for c in range(16):
                                e.matmul(psE[:],
                                         qxsT[:, c * T + tcc * 128:
                                              c * T + (tcc + 1) * 128],
                                         qwsu[:, c * 512:(c + 1) * 512],
                                         start=(c == 0), stop=(c == 15))
                            for c in range(16):
                                last = e.matmul(
                                    psF[:],
                                    qxsT[:, c * T + tcc * 128:
                                         c * T + (tcc + 1) * 128],
                                    qwsg[:, c * 512:(c + 1) * 512],
                                    start=(c == 0), stop=(c == 15))
                            return last
                        top(mmug, [qxsT, qwsu, qwsg], [psE, psF],
                            f"E1m{isl}_{tc}")
                        sop(lambda e: e.activation(silE[:], psF[:], AF.Silu),
                            [psF], [silE], f"E1s{isl}_{tc}")
                        vop(lambda e: e.tensor_tensor(t0e[:], silE[:], psE[:],
                                                      ALU.mult),
                            [silE, psE], [t0e], tag=f"E1t{isl}_{tc}")
                        dma("gpsimd",
                            t0_d[tc * 128:(tc + 1) * 128,
                                 isl * 512:(isl + 1) * 512],
                            t0e[:], [t0e], [t0_d], f"E1d{isl}_{tc}")
                        vop(lambda e, tcc=tc: e.reduce_max(
                                rmt2[:, tcc:tcc + 1], t0e[:], axis=AX.X,
                                apply_absolute_value=True),
                            [t0e], [rmt2], tag=f"E1r{isl}_{tc}")
                        vop(lambda e, tcc=tc, ii=isl: e.tensor_tensor(
                                rmt[:, 2 * tcc + (ii + 1) % 2:
                                    2 * tcc + (ii + 1) % 2 + 1],
                                rmt[:, 2 * tcc + ii % 2:2 * tcc + ii % 2 + 1],
                                rmt2[:, tcc:tcc + 1], ALU.max),
                            [rmt, rmt2], [rmt], tag=f"E1x{isl}_{tc}")
                P.barrier()
                P.emit_block()

                stE2 = ExitStack()
                with stE2:
                    t0r = [sb(stE2, f"t0r{i}", [128, DFF1], F32) for i in range(2)]
                    targ = sb(stE2, "targ", [128, DFF1], F32)
                    t8 = sb(stE2, "t8", [128, DFF1], U8)
                    tqs = sb(stE2, "tqs", [128, DFF1], BF16)
                    rmtc = sb(stE2, "rmtc", [128, 4], F32)
                    ttile = sb(stE2, "ttile", [128, 128], BF16)
                    for tc in range(8):
                        tr = t0r[tc % 2]
                        dma("sync", tr[:], t0_d[tc * 128:(tc + 1) * 128, :],
                            [], [tr], f"E2l{tc}")
                        vop(lambda e, tcc=tc: e.reciprocal(
                                rmtc[:, 0:1], rmt[:, 2 * tcc + 1:2 * tcc + 2]),
                            [rmt], [rmtc], tag=f"E2r{tc}")
                        vop(lambda e: e.tensor_scalar(rmtc[:, 2:3],
                                                      rmtc[:, 0:1], 224.0,
                                                      None, ALU.mult),
                            [rmtc], [rmtc], tag=f"E2s{tc}")
                        vop(lambda e, tcc=tc: e.tensor_scalar(
                                rmtc[:, 1:2], rmt[:, 2 * tcc + 1:2 * tcc + 2],
                                1.0 / 224.0, None, ALU.mult),
                            [rmt, rmtc], [rmtc], tag=f"E2d{tc}")
                        vop(lambda e, tr_=tr: e.tensor_scalar(
                                targ[:], tr_[:], rmtc[:, 2:3], None, ALU.mult),
                            [tr, rmtc], [targ], tag=f"E2a{tc}")
                        sop(lambda e: e.copy(t8[:].bitcast(FP8), targ[:]),
                            [targ], [t8], f"E28{tc}")
                        sop(lambda e: e.activation(tqs[:], t8[:].bitcast(FP8),
                                                   AF.Identity, scale=rmtc[:, 1:2]),
                            [t8, rmtc], [tqs], f"E2q{tc}")
                        for ich in range(NI):
                            transpose128(tqs[:, ich * 128:(ich + 1) * 128],
                                         [tqs], ttile[:], ttile,
                                         f"E2t{tc}_{ich}")
                            dma("gpsimd",
                                qts_d[ich * 128:(ich + 1) * 128,
                                      tc * 128:(tc + 1) * 128],
                                ttile[:], [ttile], [qts_d], f"E2o{tc}_{ich}")
                    P.barrier()
                    P.emit_block()
            P.barrier()
            P.emit_block()

        stE3 = ExitStack()
        with stE3:
            qt1T = sb(stE3, "qt1T", [128, NI * 128 * 8], BF16)
            dix = sb(stE3, "dix", [128, 1], I32)
            for ich in range(NI):
                dma("gpsimd", dix[:],
                    didx_d[ich * 128:(ich + 1) * 128, 0:1],
                    [], [dix], f"E3i{ich}")
                P.op("gpsimd",
                     lambda e, ii=ich: e.indirect_dma_start(
                         qt1T[:, ii * T:(ii + 1) * T], None, qts_d,
                         bass.IndirectOffsetOnAxis(dix[:], 0)),
                     reads=[qts_d, dix], writes=[qt1T], inc=16, tag=f"E3g{ich}")

            fdst = sb(stE3, "fdst", [128, DFF1], F32)
            fdarg = sb(stE3, "fdarg", [128, DFF1], F32)
            fd8 = sb(stE3, "fd8", [128, DFF1], U8)
            fdqc = sb(stE3, "fdqc", [128, DFF1], BF16)
            rmd = sb(stE3, "rmd", [128, 4], F32)
            qdsT = sb(stE3, "qdsT", [128, NI * 512], BF16)
            g1e = sb(stE3, "g1e", [128, 512], BF16)
            for dg in range(4):
                for dc in range(4):
                    r0 = dg * 512 + dc * 128
                    dma("sync", fdst[:], fwd_d[r0:r0 + 128, :], [], [fdst],
                        f"E3l{dg}_{dc}")
                    vop(lambda e: e.reduce_max(rmd[:, 0:1], fdst[:], axis=AX.X,
                                               apply_absolute_value=True),
                        [fdst], [rmd], tag=f"E3m{dg}_{dc}")
                    vop(lambda e: e.reciprocal(rmd[:, 1:2], rmd[:, 0:1]),
                        [rmd], [rmd], tag=f"E3r{dg}_{dc}")
                    vop(lambda e: e.tensor_scalar(rmd[:, 2:3], rmd[:, 1:2],
                                                  224.0, None, ALU.mult),
                        [rmd], [rmd], tag=f"E3s{dg}_{dc}")
                    vop(lambda e: e.tensor_scalar(rmd[:, 3:4], rmd[:, 0:1],
                                                  1.0 / 224.0, None, ALU.mult),
                        [rmd], [rmd], tag=f"E3d{dg}_{dc}")
                    vop(lambda e: e.tensor_scalar(fdarg[:], fdst[:],
                                                  rmd[:, 2:3], None, ALU.mult),
                        [fdst, rmd], [fdarg], tag=f"E3a{dg}_{dc}")
                    sop(lambda e: e.copy(fd8[:].bitcast(FP8), fdarg[:]),
                        [fdarg], [fd8], f"E38{dg}_{dc}")
                    sop(lambda e: e.activation(fdqc[:], fd8[:].bitcast(FP8),
                                               AF.Identity, scale=rmd[:, 3:4]),
                        [fd8, rmd], [fdqc], f"E3q{dg}_{dc}")
                    for ich in range(NI):
                        transpose128(fdqc[:, ich * 128:(ich + 1) * 128], [fdqc],
                                     qdsT[:, ich * 512 + dc * 128:
                                          ich * 512 + (dc + 1) * 128],
                                     qdsT, f"E3t{dg}_{dc}_{ich}")
                for tc in range(8):
                    def mmdn(e, tcc=tc):
                        for ich in range(NI):
                            last = e.matmul(
                                psA[:],
                                qt1T[:, ich * T + tcc * 128:
                                     ich * T + (tcc + 1) * 128],
                                qdsT[:, ich * 512:(ich + 1) * 512],
                                start=(ich == 0), stop=(ich == NI - 1))
                        return last
                    top(mmdn, [qt1T, qdsT], [psA], f"E3dm{dg}_{tc}")
                    sop(lambda e: e.activation(g1e[:], psA[:], AF.Silu),
                        [psA], [g1e], f"E3de{dg}_{tc}")
                    dma("gpsimd",
                        g1f_d[tc * 128:(tc + 1) * 128, dg * 512:(dg + 1) * 512],
                        g1e[:], [g1e], [g1f_d], f"E3do{dg}_{tc}")
            P.barrier()
            P.emit_block()

        if DEBUG:
            zin_dbg = nc.dram_tensor("zin_dbg", [512, S], BF16,
                                     kind="ExternalOutput").ap()
            zout_dbg = nc.dram_tensor("zout_dbg", [4096, S], BF16,
                                      kind="ExternalOutput").ap()
            sccdbg = nc.dram_tensor("scc_dbg", [8, D], F32,
                                    kind="ExternalOutput").ap()
            dma("gpsimd", zin_dbg, zcc_in, [zcc_in], [zin_dbg], "dbg_zi")
            dma("gpsimd", zout_dbg, zcc_out, [zcc_out], [zout_dbg], "dbg_zo")
            dma("gpsimd", sccdbg, scc_out, [scc_out], [sccdbg], "dbg_sc")
        stF = ExitStack()
        with stF:
            fx = sb(stF, "fx", [128, D], F32)
            fs = sb(stF, "fs", [128, D], BF16)
            fg = sb(stF, "fg", [128, D], BF16)
            fo = sb(stF, "fo", [128, D], F32)
            fo2 = sb(stF, "fo2", [128, D], F32)
            for tc in range(8):
                dma("sync", fx[:], x1_d[tc * 128:(tc + 1) * 128, :], [], [fx],
                    f"F_x{tc}")
                dma("sync", fs[:], s1_d[tc * 128:(tc + 1) * 128, :], [], [fs],
                    f"F_s{tc}")
                dma("sync", fg[:], g1f_d[tc * 128:(tc + 1) * 128, :], [], [fg],
                    f"F_g{tc}")
                vop(lambda e: e.tensor_tensor(fo[:], fs[:], fg[:], ALU.mult),
                    [fs, fg], [fo], tag=f"F_m{tc}")
                vop(lambda e: e.tensor_tensor(fo2[:], fx[:], fo[:], ALU.add),
                    [fx, fo], [fo2], tag=f"F_a{tc}")
                dma("gpsimd", out_d[tc * 128:(tc + 1) * 128, :], fo2[:],
                    [fo2], [out_d], f"F_o{tc}")
            snap = dict(P.counts)

            def fence(e):
                for f in Prog.ENGINES:
                    if f != "sync" and snap[f] > 0:
                        e.wait_ge(P.sems[f], snap[f])
                return e.nop()
            P.op("sync", fence, reads=[out_d], writes=[], inc=1, tag="fence")
            P.emit_block()
    return nc


_NC_CACHE = []
_LAST = [None]


def kernel(**inputs):
    gi = inputs.__getitem__
    x = np.asarray(gi("L0_attn_x"))
    _ = gi("L1_ffn_up_reorder_index")
    WQ = np.asarray(gi("L0_attn_W_Q")); WK = np.asarray(gi("L0_attn_W_K"))
    WV = np.asarray(gi("L0_attn_W_V")); WO = np.asarray(gi("L0_attn_W_O"))
    bQ = np.asarray(gi("L0_attn_b_Q")); bK = np.asarray(gi("L0_attn_b_K"))
    bV = np.asarray(gi("L0_attn_b_V")); bO = np.asarray(gi("L0_attn_b_O"))
    WU = np.asarray(gi("L0_ffn_W_up")); WG = np.asarray(gi("L0_ffn_W_gate"))
    WD = np.asarray(gi("L0_ffn_W_down"))
    l1 = {k: np.asarray(gi(f"L1_attn_W{k}")) for k in "qkvg"}
    l1b = {k: np.asarray(gi(f"L1_attn_b{k}")) for k in "qkvg"}
    fg = np.asarray(gi("L1_ffn_w_gate")); fu = np.asarray(gi("L1_ffn_w_up"))
    fd = np.asarray(gi("L1_ffn_w_down"))
    didx = np.asarray(gi("L1_ffn_down_reorder_index")).astype(np.int32)

    wut = np.ascontiguousarray(WU.T)
    wgt = np.ascontiguousarray(WG.T)
    wdt = np.ascontiguousarray(WD.T)
    l1t = {k: np.ascontiguousarray(l1[k].T) for k in "qkvg"}
    eye = np.eye(128, dtype=np.float32)
    hh = np.arange(896)[None, :]
    pp = np.arange(128)[:, None]
    maskw = np.where(hh >= pp + 384, 0.0, MASKV).astype(ml_dtypes.bfloat16)

    if not _NC_CACHE:
        _NC_CACHE.append(build_program())
    nc = _NC_CACHE[0]

    in_maps = []
    for c in range(8):
        Bb, Sl = c // 4, c % 4
        heads = [4 * Sl + j for j in range(4)]
        zr = np.empty((NH, 128), np.int32)
        for h in range(NH):
            r = 4 * Bb + h // 4
            orig = r * 512 + (h % 4) * 128 + np.arange(128)
            zr[h] = orig * 4 + Sl
        pm = np.zeros((8, 1), np.float32)
        for r in range(8):
            if r // 4 == Bb and r % 4 < Sl:
                pm[r, 0] = 1.0
        m = {
            "xb": x[Bb],
            "xown": x[Bb, Sl * T:(Sl + 1) * T],
            "wq": WQ[heads], "wk": WK[heads], "wv": WV[heads],
            "bq": bQ[heads], "bk": bK[heads], "bv": bV[heads],
            "wo": WO, "bo": bO.reshape(16, 128),
            "wut": wut, "wgt": wgt, "wdt": wdt,
            "l1q": l1t["q"], "l1k": l1t["k"], "l1v": l1t["v"], "l1g": l1t["g"],
            "l1bq": l1b["q"].reshape(16, 128),
            "l1bk": l1b["k"].reshape(16, 128),
            "l1bv": l1b["v"].reshape(16, 128),
            "l1bg": l1b["g"].reshape(16, 128),
            "fwg": fg, "fwu": fu, "fwd": fd,
            "didx": didx.reshape(NI * 128, 1),
            "zridx": zr.reshape(NH * 128, 1), "pmask": pm,
            "maskw": maskw, "eye": eye,
        }
        in_maps.append({k: np.ascontiguousarray(v) for k, v in m.items()})

    res = run_bass_kernel_spmd(nc, in_maps, core_ids=list(range(8)))
    _LAST[0] = res
    out = np.empty((B, S, D), np.float32)
    for c in range(8):
        Bb, Sl = c // 4, c % 4
        out[Bb, Sl * T:(Sl + 1) * T] = np.asarray(res.results[c]["out"])
    return out


# version 09
# speedup vs baseline: 1.0048x; 1.0048x over previous
import numpy as np
import ml_dtypes
import concourse.bass as bass
import concourse.mybir as mybir
from concourse.bass_utils import run_bass_kernel_spmd

F32 = mybir.dt.float32
BF16 = mybir.dt.bfloat16
FP8 = mybir.dt.float8e4
U8 = mybir.dt.uint8
I32 = mybir.dt.int32
AF = mybir.ActivationFunctionType
ALU = mybir.AluOpType
AX = mybir.AxisListType

B, S, D = 2, 4096, 2048
NH, DH = 16, 128
DFF0 = 8192
DFF1 = 5632
NI = DFF1 // 128
NSL = DFF1 // 512
T = 1024
RS = float(1.0 / np.sqrt(128.0))
MASKV = -1131371.0
EPS_LN = 1e-5
DEBUG = False


class Prog:
    ENGINES = ("sync", "scalar", "vector", "tensor", "gpsimd")

    def __init__(self, nc):
        self.nc = nc
        self.ops = []
        self.last_writer = {}
        self.readers = {}
        self.counts = {e: 0 for e in self.ENGINES}
        self.emitted = 0
        self.sems = {}
        self._barriers = []
        self._wm = {e: {f: 0 for f in self.ENGINES} for e in self.ENGINES}

    def init_sems(self, stack):
        for e in self.ENGINES:
            self.sems[e] = stack.enter_context(self.nc.semaphore(f"sem_{e}"))
        self.ccsems = [stack.enter_context(self.nc.semaphore(f"ccsem{i}"))
                       for i in range(2)]

    def op(self, eng, fn, reads=(), writes=(), inc=1, tag=""):
        idx = len(self.ops)
        deps = set()
        for b in reads:
            w = self.last_writer.get(id(b))
            if w is not None:
                deps.add(w)
        for b in writes:
            w = self.last_writer.get(id(b))
            if w is not None:
                deps.add(w)
            for r in self.readers.get(id(b), ()):
                deps.add(r)
        deps.discard(idx)
        self.counts[eng] += inc
        self.ops.append(dict(eng=eng, fn=fn, deps=deps, inc=inc, idx=idx,
                             tag=tag, count=self.counts[eng]))
        for b in reads:
            self.readers.setdefault(id(b), []).append(idx)
        for b in writes:
            self.last_writer[id(b)] = idx
            self.readers[id(b)] = [idx]
        return idx

    def barrier(self):
        self._barriers.append((len(self.ops), dict(self.counts)))

    def emit_block(self):
        start, end = self.emitted, len(self.ops)
        if start == end:
            return
        ops = self.ops
        per_eng = {e: [i for i in range(start, end) if ops[i]["eng"] == e]
                   for e in self.ENGINES}
        wm = self._wm
        sems = self.sems
        barriers = self._barriers

        def run_engine(engobj, ename):
            w = wm[ename]
            for i in per_eng[ename]:
                rec = ops[i]
                for bpos, bsnap in barriers:
                    if bpos <= i:
                        for f in self.ENGINES:
                            if f == ename:
                                w[f] = max(w[f], bsnap[f])
                            elif bsnap[f] > w[f]:
                                engobj.wait_ge(sems[f], bsnap[f])
                                w[f] = bsnap[f]
                need = {}
                for d in rec["deps"]:
                    dr = ops[d]
                    f = dr["eng"]
                    if f == ename == "tensor":
                        continue
                    if dr["count"] > w[f]:
                        need[f] = max(need.get(f, 0), dr["count"])
                for f in sorted(need):
                    engobj.wait_ge(sems[f], need[f])
                    w[f] = need[f]
                instr = rec["fn"](engobj)
                if instr is None:
                    raise RuntimeError(f"op {rec['tag']} returned None")
                instr.then_inc(sems[ename], rec["inc"])

        with self.nc.Block() as block:
            for ename in self.ENGINES:
                if per_eng[ename]:
                    getattr(block, ename)(
                        lambda engobj, _e=ename: run_engine(engobj, _e))
        self.emitted = end


def build_program():
    from contextlib import ExitStack
    nc = bass.Bass("TRN2", target_bir_lowering=False, num_devices=8)

    def din(name, shape, dt=F32):
        return nc.dram_tensor(name, shape, dt, kind="ExternalInput").ap()

    def dint(name, shape, dt=F32, dbg=False):
        kind = "ExternalOutput" if (dbg and DEBUG) else "Internal"
        return nc.dram_tensor(name, shape, dt, kind=kind).ap()

    xb_d = din("xb", [S, D])
    xown_d = din("xown", [T, D])
    wq_d = din("wq", [4, D, DH]); wk_d = din("wk", [4, D, DH]); wv_d = din("wv", [4, D, DH])
    bq_d = din("bq", [4, DH]); bk_d = din("bk", [4, DH]); bv_d = din("bv", [4, DH])
    wo_d = din("wo", [NH, DH, D])
    bo_d = din("bo", [16, 128])
    wut_d = din("wut", [D, DFF0]); wgt_d = din("wgt", [D, DFF0]); wdt_d = din("wdt", [DFF0, D])
    l1w_d = {k: din(f"l1{k}", [D, D]) for k in "qkvg"}
    l1b_d = {k: din(f"l1b{k}", [16, 128]) for k in "qkvg"}
    fwg_d = din("fwg", [DFF1, D]); fwu_d = din("fwu", [DFF1, D]); fwd_d = din("fwd", [D, DFF1])
    didx_d = din("didx", [NI * 128, 1], I32)
    zridx_d = din("zridx", [NH * 128, 1], I32)
    pmask_d = din("pmask", [8, 1])
    maskw_d = din("maskw", [128, 896], BF16)
    eye_d = din("eye", [128, 128])
    out_d = nc.dram_tensor("out", [T, D], F32, kind="ExternalOutput").ap()

    zcc_in = dint("zcc_in", [512, S], BF16)
    zcc_out = dint("zcc_out", [4096, S], BF16)
    scc_in = dint("scc_in", [1, D])
    scc_out = dint("scc_out", [8, D])
    offs_dram = dint("offs_dram", [1, D])
    s0_d = dint("s0_spill", [T, D], BF16, dbg=True)
    g0_d = dint("g0_spill", [T, D], BF16, dbg=True)
    x1_d = dint("x1_spill", [T, D], dbg=True)
    s1_d = dint("s1_spill", [T, D], BF16, dbg=True)
    t0_d = dint("t0_spill", [T, DFF1], dbg=True)
    ternd_d = dint("ternd_spill", [DFF0, D], BF16, dbg=True)
    qts_d = dint("qts_spill", [DFF1, T], BF16, dbg=True)
    qxs_d = dint("qxs_spill", [128, 16 * T], BF16, dbg=True)
    g1f_d = dint("g1f_spill", [T, D], BF16, dbg=True)
    if DEBUG:
        h_dbg = nc.dram_tensor("h_dbg", [DFF0, T], BF16,
                               kind="ExternalOutput").ap()

    P = Prog(nc)
    root = ExitStack()

    def sb(stk, name, shape, dt):
        return stk.enter_context(nc.sbuf_tensor(name, shape, dt))

    with root:
        P.init_sems(root)
        eye_f = sb(root, "eye_f", [128, 128], F32)
        eye_b = sb(root, "eye_b", [128, 128], BF16)
        ones_col_b = sb(root, "ones_col_b", [128, 1], BF16)
        ones_row_b = sb(root, "ones_row_b", [1, 128], BF16)
        ones_row_f = sb(root, "ones_row_f", [1, 128], F32)
        ones_col_f = sb(root, "ones_col_f", [128, 1], F32)
        one11_f = sb(root, "one11_f", [1, 1], F32)
        epsc = sb(root, "epsc", [128, 1], F32)
        thU = sb(root, "thU", [128, 3], F32)
        thG = sb(root, "thG", [128, 3], F32)
        thD = sb(root, "thD", [128, 3], F32)
        psA = root.enter_context(nc.psum_tensor("psA", [128, 512], F32))
        psB = root.enter_context(nc.psum_tensor("psB", [128, 512], F32))
        psE = root.enter_context(nc.psum_tensor("psE", [128, 512], F32))
        psF = root.enter_context(nc.psum_tensor("psF", [128, 512], F32))
        psTs = [root.enter_context(nc.psum_tensor(f"psT{i}", [128, 128], BF16))
                for i in range(3)]
        psV = root.enter_context(nc.psum_tensor("psV", [128, 256], F32))
        tp_ctr = [0]

        def dma(eng, dst_ap, src_ap, reads, writes, tag=""):
            P.op(eng, lambda e, d=dst_ap, s=src_ap: e.dma_start(out=d, in_=s),
                 reads=reads, writes=writes, inc=16, tag=tag)

        def vop(fn, reads, writes, eng="vector", tag=""):
            P.op(eng, fn, reads=reads, writes=writes, inc=1, tag=tag)

        def sop(fn, reads, writes, tag=""):
            P.op("scalar", fn, reads=reads, writes=writes, inc=1, tag=tag)

        def top(fn, reads, writes, tag=""):
            P.op("tensor", fn, reads=reads, writes=writes, inc=1, tag=tag)

        def memset(t, val, tag="ms"):
            vop(lambda e, tt=t, v=val: e.memset(tt[:], v), [], [t], tag=tag)

        def row_to_col(dst_ap, dst_buf, row_ap, row_bufs, tag):
            top(lambda e, r=row_ap: e.matmul(psV[:, 0:1], r, one11_f[:],
                                             start=True, stop=True),
                list(row_bufs) + [one11_f], [psV], tag + "_mm")
            vop(lambda e, d=dst_ap: e.tensor_copy(d, psV[:, 0:1]),
                [psV], [dst_buf], tag=tag + "_cp")

        dma("sync", eye_f[:], eye_d, [], [eye_f], "ld_eye")
        vop(lambda e: e.tensor_copy(eye_b[:], eye_f[:]), [eye_f], [eye_b], tag="eyeb")
        memset(ones_col_b, 1.0, "o1"); memset(ones_row_b, 1.0, "o2")
        memset(ones_row_f, 1.0, "o3"); memset(ones_col_f, 1.0, "o4")
        memset(one11_f, 1.0, "o5")
        memset(epsc, EPS_LN, "o6")

        def make_ln(stk, pref):
            sq = sb(stk, pref + "sq", [128, D], BF16)
            st_ = sb(stk, pref + "st", [128, 8], F32)
            nm = sb(stk, pref + "nm", [128, 1], F32)
            rs_ = sb(stk, pref + "rs", [128, 1], F32)

            def ln(xtile, out_bf_ap, out_bufs, tag, extra_f32=None):
                vop(lambda e: e.tensor_scalar(sq[:], xtile[:], 1.0, None,
                                              ALU.mult, ALU.add,
                                              accum_out=st_[:, 0:1]),
                    [xtile], [sq, st_], tag=tag + "_s")
                sop(lambda e: e.activation(sq[:], xtile[:], AF.Square,
                                           accum_out=st_[:, 1:2]),
                    [xtile], [sq, st_], tag + "_q")
                vop(lambda e: e.tensor_scalar(st_[:, 2:3], st_[:, 0:1],
                                              1.0 / D, None, ALU.mult),
                    [st_], [st_], tag=tag + "_m")
                vop(lambda e: e.tensor_tensor(st_[:, 3:4], st_[:, 2:3],
                                              st_[:, 2:3], ALU.mult),
                    [st_], [st_], tag=tag + "_m2")
                vop(lambda e: e.tensor_scalar(st_[:, 4:5], st_[:, 1:2],
                                              1.0 / D, None, ALU.mult),
                    [st_], [st_], tag=tag + "_e2")
                vop(lambda e: e.tensor_tensor(st_[:, 5:6], st_[:, 4:5],
                                              st_[:, 3:4], ALU.subtract),
                    [st_], [st_], tag=tag + "_v")
                sop(lambda e: e.activation(st_[:, 6:7], st_[:, 5:6], AF.Sqrt,
                                           bias=epsc[:, 0:1]),
                    [st_, epsc], [st_], tag + "_sd")
                vop(lambda e: e.reciprocal(rs_[:], st_[:, 6:7]),
                    [st_], [rs_], tag=tag + "_r")
                vop(lambda e: e.tensor_scalar(nm[:], st_[:, 2:3], -1.0, None,
                                              ALU.mult),
                    [st_], [nm], tag=tag + "_n")
                vop(lambda e: e.tensor_scalar(out_bf_ap, xtile[:], nm[:], rs_[:],
                                              ALU.add, ALU.mult),
                    [xtile, nm, rs_], out_bufs, tag=tag + "_x")
                if extra_f32 is not None:
                    ap_, bufs_ = extra_f32
                    vop(lambda e: e.tensor_scalar(ap_, xtile[:], nm[:], rs_[:],
                                                  ALU.add, ALU.mult),
                        [xtile, nm, rs_], bufs_, tag=tag + "_xf")
            return ln

        def transpose128(src_ap, src_bufs, dst_ap, dst_buf, tag):
            pt = psTs[tp_ctr[0] % 3]
            tp_ctr[0] += 1
            top(lambda e, s=src_ap, p=pt: e.transpose(p[:], s, eye_b[:]),
                list(src_bufs) + [eye_b], [pt], tag + "_t")
            vop(lambda e, d=dst_ap, p=pt: e.tensor_copy(d, p[:]),
                [pt], [dst_buf], tag=tag + "_c")

        stA = ExitStack()
        with stA:
            xnT = [sb(stA, f"xnT{i}", [128, S], BF16) for i in range(16)]
            xt = [sb(stA, "xt0", [128, D], F32)]
            xnb = sb(stA, "xnb", [128, D], BF16)
            lnA = make_ln(stA, "lA")

            for t in range(32):
                xi = xt[0]
                dma("sync", xi[:], xb_d[t * 128:(t + 1) * 128, :], [], [xi],
                    f"A_x{t}")
                lnA(xi, xnb[:], [xnb], f"A_ln{t}")
                for c in range(16):
                    transpose128(xnb[:, c * 128:(c + 1) * 128], [xnb],
                                 xnT[c][:, t * 128:(t + 1) * 128], xnT[c],
                                 f"A_tp{t}_{c}")

            accw = sb(stA, "accw", [128, 16], F32)
            wmt = [sb(stA, f"wmt{i}", [128, 512], F32) for i in range(2)]
            wabs = sb(stA, "wabs", [128, 512], BF16)
            wsum = sb(stA, "wsum", [1, 4], F32)
            memset(accw, 0.0, "accw0")

            def mean_pass(wsrc, nrow, ncol, base, tag):
                k = 0
                ntile = (nrow // 128) * (ncol // 512)
                for r in range(nrow // 128):
                    for c in range(ncol // 512):
                        wt = wmt[k % 2]
                        dma("sync", wt[:],
                            wsrc[r * 128:(r + 1) * 128, c * 512:(c + 1) * 512],
                            [], [wt], f"{tag}l{k}")
                        sop(lambda e, w=wt, kk=k: e.activation(
                                wabs[:], w[:], AF.Abs,
                                accum_out=accw[:, base + (kk % 2):base + (kk % 2) + 1]),
                            [wt], [wabs, accw], f"{tag}a{k}")
                        rd = base + 2 + (k % 2)
                        wr = base + 2 + ((k + 1) % 2)
                        vop(lambda e, kk=k, rd_=rd, wr_=wr: e.tensor_tensor(
                                accw[:, wr_:wr_ + 1],
                                accw[:, rd_:rd_ + 1],
                                accw[:, base + (kk % 2):base + (kk % 2) + 1],
                                ALU.add),
                            [accw], [accw], tag=f"{tag}c{k}")
                        k += 1
                return base + 2 + (ntile % 2)

            def mean_finish(fcol, th_t, idx, tag):
                top(lambda e: e.matmul(psV[0:1, 4:5],
                                       accw[:, fcol:fcol + 1],
                                       ones_col_f[:], start=True, stop=True),
                    [accw, ones_col_f], [psV], tag + "_r")
                vop(lambda e: e.tensor_scalar(wsum[0:1, idx:idx + 1],
                                              psV[0:1, 4:5],
                                              1.0 / 16777216.0, 1e-8,
                                              ALU.mult, ALU.add),
                    [psV], [wsum], tag=tag + "_t")
                top(lambda e: e.matmul(psV[:, 8:9], ones_row_f[:],
                                       wsum[0:1, idx:idx + 1],
                                       start=True, stop=True),
                    [ones_row_f, wsum], [psV], tag + "_b")
                vop(lambda e: e.tensor_copy(th_t[:, 2:3], psV[:, 8:9]),
                    [psV], [th_t], tag=tag + "_f")
                vop(lambda e: e.tensor_scalar(th_t[:, 0:1], psV[:, 8:9], 0.5,
                                              None, ALU.mult),
                    [psV], [th_t], tag=tag + "_p")
                vop(lambda e: e.tensor_scalar(th_t[:, 1:2], psV[:, 8:9], -0.5,
                                              None, ALU.mult),
                    [psV], [th_t], tag=tag + "_n")

            fc = mean_pass(wut_d, D, DFF0, 0, "mu")
            mean_finish(fc, thU, 0, "mu")
            fc = mean_pass(wgt_d, D, DFF0, 5, "mg")
            mean_finish(fc, thG, 1, "mg")
            fc = mean_pass(wdt_d, DFF0, D, 10, "md")
            mean_finish(fc, thD, 2, "md")

            maskw = sb(stA, "maskw_s", [128, 896], BF16)
            dma("sync", maskw[:], maskw_d, [], [maskw], "A_mask")
            wstg = sb(stA, "wstg", [128, 1024], F32)
            wbf = sb(stA, "wbf", [128, D], BF16)
            KT = sb(stA, "KT", [128, S], BF16)
            QT = sb(stA, "QT", [128, S], BF16)
            Vtok = sb(stA, "Vtok", [128, S], BF16)
            vtmp = sb(stA, "vtmp", [128, 512], BF16)
            sqb = sb(stA, "sqb", [128, 512], BF16)
            rsc = sb(stA, "rsc", [1, 512], F32)
            mks = sb(stA, "mks", [1, 4], F32)
            negM = sb(stA, "negM", [1, 512], BF16)
            aT = [sb(stA, f"aT{i}", [128, 512], BF16) for i in range(2)]
            rb = sb(stA, "rb", [128, 512], F32)
            ztile = [sb(stA, f"ztile{i}", [128, 512], BF16) for i in range(2)]
            bcol = sb(stA, "bcol", [128, 3], F32)
            brow = sb(stA, "brow", [1, 128], F32)

            def bias_col(bsrc, p, col, tag):
                dma("sync", brow[:], bsrc[p:p + 1, :], [], [brow], tag + "_l")
                row_to_col(bcol[:, col:col + 1], bcol, brow[:], [brow], tag)

            def proj(wsrc, p, dst, bias_c, tag):
                wrs = wsrc[p].rearrange("(c p) e -> p c e", p=128)
                wsv = wstg[:].rearrange("p (c e) -> p c e", e=128)
                for hf in range(2):
                    dma("sync", wsv, wrs[:, hf * 8:(hf + 1) * 8, :],
                        [], [wstg], tag + f"_w{hf}")
                    vop(lambda e, h_=hf: e.tensor_copy(
                            wbf[:, h_ * 1024:(h_ + 1) * 1024], wstg[:]),
                        [wstg], [wbf], tag=tag + f"_v{hf}")
                for tk in range(8):
                    def mm(e, tkk=tk):
                        for c in range(16):
                            last = e.matmul(
                                psA[:], wbf[:, c * 128:(c + 1) * 128],
                                xnT[c][:, tkk * 512:(tkk + 1) * 512],
                                start=(c == 0), stop=(c == 15))
                        return last
                    top(mm, [wbf] + xnT, [psA], f"{tag}m{tk}")
                    sop(lambda e, tkk=tk: e.activation(
                            dst[:, tkk * 512:(tkk + 1) * 512], psA[:], AF.Identity,
                            bias=bcol[:, bias_c:bias_c + 1]),
                        [psA, bcol], [dst], f"{tag}e{tk}")

            for p in range(4):
                bias_col(bk_d, p, 1, f"p{p}bk")
                bias_col(bq_d, p, 0, f"p{p}bq")
                bias_col(bv_d, p, 2, f"p{p}bv")
                proj(wk_d, p, KT, 1, f"p{p}K")
                memset(mks, 0.0, f"p{p}km0")
                for tk in range(8):
                    sop(lambda e, tkk=tk: e.activation(
                            sqb[:], KT[:, tkk * 512:(tkk + 1) * 512], AF.Square),
                        [KT], [sqb], f"p{p}ks{tk}")
                    top(lambda e: e.matmul(psF[0:1, :], ones_col_b[:], sqb[:],
                                           start=True, stop=True),
                        [sqb, ones_col_b], [psF], f"p{p}kn{tk}")
                    vop(lambda e: e.tensor_copy(rsc[0:1, :], psF[0:1, :]),
                        [psF], [rsc], tag=f"p{p}kc{tk}")
                    vop(lambda e: e.reduce_max(mks[0:1, 2:3], rsc[0:1, :],
                                               axis=AX.X),
                        [rsc], [mks], tag=f"p{p}kd{tk}")
                    vop(lambda e, tkk=tk: e.tensor_tensor(
                            mks[0:1, 3 if tkk % 2 == 0 else 0:
                                4 if tkk % 2 == 0 else 1],
                            mks[0:1, 0 if tkk % 2 == 0 else 3:
                                1 if tkk % 2 == 0 else 4],
                            mks[0:1, 2:3], ALU.max),
                        [mks], [mks], tag=f"p{p}kx{tk}")
                sop(lambda e: e.activation(mks[0:1, 1:2], mks[0:1, 0:1], AF.Sqrt),
                    [mks], [mks], f"p{p}kq")
                proj(wq_d, p, QT, 0, f"p{p}Q")
                wrs_v = wv_d[p].rearrange("(c p) e -> p c e", p=128)
                wsv_v = wstg[:].rearrange("p (c e) -> p c e", e=128)
                for hf in range(2):
                    dma("sync", wsv_v, wrs_v[:, hf * 8:(hf + 1) * 8, :],
                        [], [wstg], f"p{p}Vw{hf}")
                    vop(lambda e, h_=hf: e.tensor_copy(
                            wbf[:, h_ * 1024:(h_ + 1) * 1024], wstg[:]),
                        [wstg], [wbf], tag=f"p{p}Vv{hf}")
                for tk in range(8):
                    def mmv(e, tkk=tk):
                        for c in range(16):
                            last = e.matmul(
                                psB[:], wbf[:, c * 128:(c + 1) * 128],
                                xnT[c][:, tkk * 512:(tkk + 1) * 512],
                                start=(c == 0), stop=(c == 15))
                        return last
                    top(mmv, [wbf] + xnT, [psB], f"p{p}vm{tk}")
                    sop(lambda e: e.activation(vtmp[:], psB[:], AF.Identity,
                                               bias=bcol[:, 2:3]),
                        [psB, bcol], [vtmp], f"p{p}ve{tk}")
                    for k in range(4):
                        transpose128(
                            vtmp[:, k * 128:(k + 1) * 128], [vtmp],
                            Vtok[:, tk * 512 + k * 128:tk * 512 + (k + 1) * 128],
                            Vtok, f"p{p}vt{tk}_{k}")

                for tq in range(8):
                    sop(lambda e, tqq=tq: e.activation(
                            sqb[:], QT[:, tqq * 512:(tqq + 1) * 512], AF.Square),
                        [QT], [sqb], f"p{p}q{tq}s")
                    top(lambda e: e.matmul(psF[0:1, :], ones_col_b[:], sqb[:],
                                           start=True, stop=True),
                        [sqb, ones_col_b], [psF], f"p{p}q{tq}n")
                    sop(lambda e: e.activation(rsc[0:1, :], psF[0:1, :], AF.Sqrt),
                        [psF], [rsc], f"p{p}q{tq}r")
                    vop(lambda e: e.tensor_scalar(negM[0:1, :], rsc[0:1, :],
                                                  mks[0:1, 1:2], -1.0,
                                                  ALU.mult, ALU.mult),
                        [rsc, mks], [negM], tag=f"p{p}q{tq}m")
                    ntk = 4 * tq + 4
                    for tkc in range(ntk):
                        ab = aT[tkc % 2]
                        dg = tkc * 128 - tq * 512
                        def mms(e, tkk=tkc, tqq=tq, dgg=dg):
                            e.matmul(psA[:], KT[:, tkk * 128:(tkk + 1) * 128],
                                     QT[:, tqq * 512:(tqq + 1) * 512],
                                     start=True, stop=False)
                            last = e.matmul(psA[:], ones_row_b[:], negM[0:1, :],
                                            start=False, stop=(dgg < 0))
                            if dgg >= 0:
                                off = 384 - dgg
                                last = e.matmul(psA[:], eye_b[:],
                                                maskw[:, off:off + 512],
                                                start=False, stop=True)
                            return last
                        top(mms, [KT, QT, negM, ones_row_b, eye_b, maskw],
                            [psA], f"p{p}q{tq}k{tkc}s")
                        sop(lambda e, b=ab: e.activation(b[:], psA[:], AF.Exp,
                                                         scale=RS),
                            [psA], [ab], f"p{p}q{tq}k{tkc}e")
                        def mmz(e, tkk=tkc, b=ab, fi=(tkc == 0),
                                la=(tkc == ntk - 1)):
                            e.matmul(psB[:], Vtok[:, tkk * 128:(tkk + 1) * 128],
                                     b[:], start=fi, stop=la)
                            return e.matmul(psE[0:1, :], ones_col_b[:], b[:],
                                            start=fi, stop=la)
                        top(mmz, [Vtok, ab, ones_col_b], [psB, psE],
                            f"p{p}q{tq}k{tkc}z")
                    vop(lambda e: e.reciprocal(rb[0:1, :], psE[0:1, :]),
                        [psE], [rb], tag=f"p{p}q{tq}i")
                    top(lambda e: e.matmul(psA[:], ones_row_f[:], rb[0:1, :],
                                           start=True, stop=True),
                        [ones_row_f, rb], [psA], f"p{p}q{tq}b")
                    vop(lambda e: e.tensor_copy(rb[:], psA[:]),
                        [psA], [rb], tag=f"p{p}q{tq}c")
                    zt = ztile[tq % 2]
                    vop(lambda e, z_=zt: e.tensor_tensor(z_[:], psB[:], rb[:],
                                                         ALU.mult),
                        [psB, rb], [zt], tag=f"p{p}q{tq}z")
                    dma("sync",
                        zcc_in[p * 128:(p + 1) * 128, tq * 512:(tq + 1) * 512],
                        zt[:], [zt], [zcc_in], f"p{p}q{tq}d")

            def cc_z(e):
                e.collective_compute(
                    "AllGather", ALU.bypass,
                    replica_groups=[[0, 1, 2, 3, 4, 5, 6, 7]],
                    ins=[zcc_in.opt()], outs=[zcc_out.opt()]).then_inc(
                        P.ccsems[0], 1)
                e.wait_ge(P.ccsems[0], 1)
                return e.nop()
            P.op("gpsimd", cc_z, reads=[zcc_in], writes=[zcc_out], inc=1,
                 tag="cc_z")
            P.barrier()
            P.emit_block()

        stB = ExitStack()
        with stB:
            xnoT = [sb(stB, f"xnoT{i}", [128, T], BF16) for i in range(16)]
            stB1 = ExitStack()
            with stB1:
                xt0 = sb(stB1, "xtB", [128, D], F32)
                xnb2 = sb(stB1, "xnb2", [128, D], BF16)
                lnB = make_ln(stB1, "lB")
                for t in range(8):
                    dma("sync", xt0[:], xown_d[t * 128:(t + 1) * 128, :],
                        [], [xt0], f"B_x{t}")
                    lnB(xt0, xnb2[:], [xnb2], f"B_ln{t}")
                    for c in range(16):
                        transpose128(xnb2[:, c * 128:(c + 1) * 128], [xnb2],
                                     xnoT[c][:, t * 128:(t + 1) * 128],
                                     xnoT[c], f"B_tp{t}_{c}")
                zsl = [sb(stB1, f"zsl{i}", [128, T], BF16) for i in range(16)]
                zidx = sb(stB1, "zidx", [128, 1], I32)
                wost = sb(stB1, "wost", [128, 512], F32)
                wobf = [sb(stB1, f"wobf{i}", [128, 512], BF16) for i in range(16)]
                borow = sb(stB1, "borow", [1, 512], F32)
                borowb = sb(stB1, "borowb", [1, 512], BF16)
                sev = sb(stB1, "sev", [128, 512], BF16)
                zv = zcc_out.rearrange("r (q c) -> (r q) c", c=T)
                bo_flat = bo_d.rearrange("a b -> (a b)").unsqueeze(0)
                for h in range(NH):
                    dma("gpsimd", zidx[:],
                        zridx_d[h * 128:(h + 1) * 128, 0:1],
                        [], [zidx], f"B_zi{h}")
                    P.op("gpsimd",
                         lambda e, hh=h: e.indirect_dma_start(
                             zsl[hh][:], None, zv,
                             bass.IndirectOffsetOnAxis(zidx[:], 0)),
                         reads=[zcc_out, zidx], writes=[zsl[h]], inc=16,
                         tag=f"B_zg{h}")
                for dg in range(4):
                    dma("sync", borow[0:1, :],
                        bo_flat[0:1, dg * 512:(dg + 1) * 512],
                        [], [borow], f"B_bo{dg}")
                    vop(lambda e: e.tensor_copy(borowb[0:1, :], borow[0:1, :]),
                        [borow], [borowb], tag=f"B_boc{dg}")
                    for h in range(NH):
                        dma("sync", wost[:], wo_d[h, :, dg * 512:(dg + 1) * 512],
                            [], [wost], f"B_wo{dg}_{h}")
                        vop(lambda e, hh=h: e.tensor_copy(wobf[hh][:], wost[:]),
                            [wost], [wobf[h]], tag=f"B_wc{dg}_{h}")
                    for tc in range(8):
                        def mmo(e, tcc=tc):
                            for h in range(NH):
                                e.matmul(psA[:],
                                         zsl[h][:, tcc * 128:(tcc + 1) * 128],
                                         wobf[h][:], start=(h == 0), stop=False)
                            return e.matmul(psA[:], ones_row_b[:], borowb[0:1, :],
                                            start=False, stop=True)
                        top(mmo, zsl + wobf + [ones_row_b, borowb], [psA],
                            f"B_om{dg}_{tc}")
                        sop(lambda e: e.activation(sev[:], psA[:], AF.Identity),
                            [psA], [sev], f"B_oe{dg}_{tc}")
                        dma("gpsimd",
                            s0_d[tc * 128:(tc + 1) * 128, dg * 512:(dg + 1) * 512],
                            sev[:], [sev], [s0_d], f"B_od{dg}_{tc}")
                P.barrier()
                P.emit_block()

            hT = [sb(stB, f"hT{i}", [128, T], BF16) for i in range(64)]
            stB2 = ExitStack()
            with stB2:
                tstg = [sb(stB2, f"tstg{i}", [128, 256], F32) for i in range(2)]
                ternu = [sb(stB2, f"tu{i}", [128, 256], BF16) for i in range(16)]
                terng = [sb(stB2, f"tg{i}", [128, 256], BF16) for i in range(16)]
                tlt = sb(stB2, "tlt", [128, 256], BF16)
                tgt2 = sb(stB2, "tgt2", [128, 256], BF16)
                sil = sb(stB2, "sil", [128, 512], BF16)
                gsc = sb(stB2, "gsc", [128, 512], BF16)

                def ternary(dst, src, th, tag):
                    vop(lambda e: e.tensor_scalar(tgt2[:], src[:], th[:, 0:1],
                                                  None, ALU.is_gt),
                        [src, th], [tgt2], tag=tag + "g")
                    vop(lambda e: e.tensor_scalar(tlt[:], src[:], th[:, 1:2],
                                                  None, ALU.is_lt),
                        [src, th], [tlt], eng="gpsimd", tag=tag + "l")
                    vop(lambda e: e.tensor_tensor(dst[:], tgt2[:], tlt[:],
                                                  ALU.subtract),
                        [tgt2, tlt], [dst], tag=tag + "s")

                for es in range(32):
                    for c in range(16):
                        stg = tstg[c % 2]
                        dma("sync", stg[:],
                            wut_d[c * 128:(c + 1) * 128, es * 256:(es + 1) * 256],
                            [], [stg], f"B2u{es}_{c}")
                        ternary(ternu[c], stg, thU, f"B2qu{es}_{c}")
                        dma("sync", stg[:],
                            wgt_d[c * 128:(c + 1) * 128, es * 256:(es + 1) * 256],
                            [], [stg], f"B2g{es}_{c}")
                        ternary(terng[c], stg, thG, f"B2qg{es}_{c}")
                    for e1 in range(2):
                        ech = es * 2 + e1
                        for tk in range(2):
                            def mmu(e, ee=e1, tkk=tk):
                                for c in range(16):
                                    e.matmul(psE[:],
                                             ternu[c][:, ee * 128:(ee + 1) * 128],
                                             xnoT[c][:, tkk * 512:(tkk + 1) * 512],
                                             start=(c == 0), stop=(c == 15))
                                for c in range(16):
                                    last = e.matmul(
                                        psF[:],
                                        terng[c][:, ee * 128:(ee + 1) * 128],
                                        xnoT[c][:, tkk * 512:(tkk + 1) * 512],
                                        start=(c == 0), stop=(c == 15))
                                return last
                            top(mmu, ternu + terng + xnoT, [psE, psF],
                                f"B2m{ech}_{tk}")
                            sop(lambda e: e.activation(sil[:], psE[:], AF.Silu,
                                                       scale=thU[:, 2:3]),
                                [psE, thU], [sil], f"B2s{ech}_{tk}")
                            sop(lambda e: e.activation(gsc[:], psF[:],
                                                       AF.Identity,
                                                       scale=thG[:, 2:3]),
                                [psF, thG], [gsc], f"B2g2{ech}_{tk}")
                            vop(lambda e, ec=ech, tkk=tk: e.tensor_tensor(
                                    hT[ec][:, tkk * 512:(tkk + 1) * 512],
                                    sil[:], gsc[:], ALU.mult),
                                [sil, gsc], [hT[ech]], tag=f"B2h{ech}_{tk}")
                if DEBUG:
                    for ec in range(64):
                        dma("gpsimd", h_dbg[ec * 128:(ec + 1) * 128, :],
                            hT[ec][:], [hT[ec]], [h_dbg], f"dbg_h{ec}")
                P.barrier()
                P.emit_block()

            stB3 = ExitStack()
            with stB3:
                dstg = [sb(stB3, f"dstg{i}", [128, D], F32) for i in range(2)]
                dt_b = sb(stB3, "dt_b", [128, D], BF16)
                dlt = sb(stB3, "dlt", [128, D], BF16)
                dgt = sb(stB3, "dgt", [128, D], BF16)
                for ec in range(64):
                    stg = dstg[ec % 2]
                    dma("sync", stg[:], wdt_d[ec * 128:(ec + 1) * 128, :],
                        [], [stg], f"B3l{ec}")
                    vop(lambda e, s_=stg: e.tensor_scalar(dgt[:], s_[:],
                                                  thD[:, 0:1],
                                                  None, ALU.is_gt),
                        [stg, thD], [dgt], tag=f"B3g{ec}")
                    vop(lambda e, s_=stg: e.tensor_scalar(dlt[:], s_[:],
                                                  thD[:, 1:2],
                                                  None, ALU.is_lt),
                        [stg, thD], [dlt], eng="gpsimd", tag=f"B3t{ec}")
                    vop(lambda e: e.tensor_tensor(dt_b[:], dgt[:], dlt[:],
                                                  ALU.subtract),
                        [dgt, dlt], [dt_b], tag=f"B3s{ec}")
                    dma("gpsimd", ternd_d[ec * 128:(ec + 1) * 128, :], dt_b[:],
                        [dt_b], [ternd_d], f"B3d{ec}")
                tdt = [sb(stB3, f"tdt{i}", [128, 512], BF16) for i in range(2)]
                g0e = sb(stB3, "g0e", [128, 512], BF16)
                PS4 = [psA, psB, psE, psF]
                for dg in range(4):
                    for th_half in range(2):
                        for ec in range(64):
                            tb = tdt[ec % 2]
                            dma("sync", tb[:],
                                ternd_d[ec * 128:(ec + 1) * 128,
                                        dg * 512:(dg + 1) * 512],
                                [ternd_d], [tb], f"B3r{dg}_{th_half}_{ec}")
                            def mmd(e, tb_=tb, ecc=ec, thh=th_half):
                                for t4 in range(4):
                                    tc = thh * 4 + t4
                                    last = e.matmul(
                                        PS4[t4][:],
                                        hT[ecc][:, tc * 128:(tc + 1) * 128],
                                        tb_[:], start=(ecc == 0),
                                        stop=(ecc == 63))
                                return last
                            top(mmd, [tb] + hT, PS4, f"B3m{dg}_{th_half}_{ec}")
                        for t4 in range(4):
                            tc = th_half * 4 + t4
                            sop(lambda e, t4_=t4: e.activation(
                                    g0e[:], PS4[t4_][:], AF.Silu,
                                    scale=thD[:, 2:3]),
                                [PS4[t4], thD], [g0e], f"B3e{dg}_{th_half}_{t4}")
                            dma("gpsimd",
                                g0_d[tc * 128:(tc + 1) * 128,
                                     dg * 512:(dg + 1) * 512],
                                g0e[:], [g0e], [g0_d], f"B3o{dg}_{th_half}_{t4}")
                P.barrier()
                P.emit_block()
            P.barrier()
            P.emit_block()

        stB4 = ExitStack()
        with stB4:
            s0t = sb(stB4, "s0t", [128, D], BF16)
            g0t = sb(stB4, "g0t", [128, D], BF16)
            xot = sb(stB4, "xot", [128, D], F32)
            sgt = sb(stB4, "sgt", [128, D], F32)
            x1o = sb(stB4, "x1o", [128, D], F32)
            for tc in range(8):
                dma("sync", s0t[:], s0_d[tc * 128:(tc + 1) * 128, :], [], [s0t],
                    f"B4s{tc}")
                dma("sync", g0t[:], g0_d[tc * 128:(tc + 1) * 128, :], [], [g0t],
                    f"B4g{tc}")
                dma("sync", xot[:], xown_d[tc * 128:(tc + 1) * 128, :], [], [xot],
                    f"B4x{tc}")
                vop(lambda e: e.tensor_tensor(sgt[:], s0t[:], g0t[:], ALU.mult),
                    [s0t, g0t], [sgt], tag=f"B4m{tc}")
                vop(lambda e: e.tensor_tensor(x1o[:], xot[:], sgt[:], ALU.add),
                    [xot, sgt], [x1o], tag=f"B4a{tc}")
                dma("gpsimd", x1_d[tc * 128:(tc + 1) * 128, :], x1o[:],
                    [x1o], [x1_d], f"B4o{tc}")
            P.barrier()
            P.emit_block()

        stDo = ExitStack()
        with stDo:
            qT = sb(stDo, "qT", [128, 16 * T], BF16)
            gT = sb(stDo, "gT", [128, 16 * T], BF16)
            kvT = sb(stDo, "kvT", [128, 16 * T], BF16)
            stD = ExitStack()
            with stD:
                xn1T = sb(stD, "xn1T", [128, 16 * T], BF16)
                stD1 = ExitStack()
                with stD1:
                    qxsT = sb(stD1, "qxsT", [128, 16 * T], BF16)
                    x1t = sb(stD1, "x1t", [128, D], F32)
                    xn1b = sb(stD1, "xn1b", [128, D], BF16)
                    xn1f = sb(stD1, "xn1f", [128, D], F32)
                    qarg = sb(stD1, "qarg", [128, D], F32)
                    q8 = sb(stD1, "q8", [128, D], U8)
                    qxsb = sb(stD1, "qxsb", [128, D], BF16)
                    rmx = sb(stD1, "rmx", [128, 4], F32)
                    lnD = make_ln(stD1, "lD")
                    for t in range(8):
                        dma("sync", x1t[:], x1_d[t * 128:(t + 1) * 128, :],
                            [], [x1t], f"D_x{t}")
                        lnD(x1t, xn1b[:], [xn1b], f"D_ln{t}",
                            extra_f32=(xn1f[:], [xn1f]))
                        vop(lambda e: e.reduce_max(rmx[:, 0:1], xn1f[:],
                                                   axis=AX.X,
                                                   apply_absolute_value=True),
                            [xn1f], [rmx], tag=f"D_rm{t}")
                        vop(lambda e: e.reciprocal(rmx[:, 1:2], rmx[:, 0:1]),
                            [rmx], [rmx], tag=f"D_rr{t}")
                        vop(lambda e: e.tensor_scalar(rmx[:, 2:3], rmx[:, 1:2],
                                                      224.0, None, ALU.mult),
                            [rmx], [rmx], tag=f"D_rs{t}")
                        vop(lambda e: e.tensor_scalar(rmx[:, 3:4], rmx[:, 0:1],
                                                      1.0 / 224.0, None,
                                                      ALU.mult),
                            [rmx], [rmx], tag=f"D_rd{t}")
                        vop(lambda e: e.tensor_scalar(qarg[:], xn1f[:],
                                                      rmx[:, 2:3], None,
                                                      ALU.mult),
                            [xn1f, rmx], [qarg], tag=f"D_qa{t}")
                        sop(lambda e: e.copy(q8[:].bitcast(FP8), qarg[:]),
                            [qarg], [q8], f"D_q8{t}")
                        sop(lambda e: e.activation(qxsb[:], q8[:].bitcast(FP8),
                                                   AF.Identity, scale=rmx[:, 3:4]),
                            [q8, rmx], [qxsb], f"D_qs{t}")
                        for c in range(16):
                            transpose128(xn1b[:, c * 128:(c + 1) * 128], [xn1b],
                                         xn1T[:, c * T + t * 128:
                                              c * T + (t + 1) * 128],
                                         xn1T, f"D_t1{t}_{c}")
                            transpose128(qxsb[:, c * 128:(c + 1) * 128], [qxsb],
                                         qxsT[:, c * T + t * 128:
                                              c * T + (t + 1) * 128],
                                         qxsT, f"D_t2{t}_{c}")
                    dma("gpsimd", qxs_d, qxsT[:], [qxsT], [qxs_d], "D_qxspill")
                    P.barrier()
                    P.emit_block()

                stD2 = ExitStack()
                with stD2:
                    kT = sb(stD2, "kT", [128, 16 * T], BF16)
                    wst1 = [sb(stD2, f"wst1{i}", [128, 512], F32)
                            for i in range(2)]
                    wbf1 = [sb(stD2, f"wbf1{i}", [128, 512], BF16)
                            for i in range(16)]
                    bc1 = sb(stD2, "bc1", [128, 16], F32)
                    br1 = sb(stD2, "br1", [1, 128], F32)
                    vev = sb(stD2, "vev", [128, 512], BF16)

                    def l1proj(key, dst, act, tag, is_v=False):
                        wsrc = l1w_d[key]
                        bsrc = l1b_d[key]
                        for i in range(16):
                            dma("sync", br1[:], bsrc[i:i + 1, :], [], [br1],
                                f"{tag}bl{i}")
                            row_to_col(bc1[:, i:i + 1], bc1, br1[:], [br1],
                                       f"{tag}bc{i}")
                        for eb in range(4):
                            for c in range(16):
                                w_ = wst1[c % 2]
                                dma("sync", w_[:],
                                    wsrc[c * 128:(c + 1) * 128,
                                         eb * 512:(eb + 1) * 512],
                                    [], [w_], f"{tag}w{eb}_{c}")
                                vop(lambda e, cc=c, ww=w_: e.tensor_copy(
                                        wbf1[cc][:], ww[:]),
                                    [w_], [wbf1[c]], tag=f"{tag}v{eb}_{c}")
                            for e1 in range(4):
                                ec = eb * 4 + e1
                                for tk in range(2):
                                    def mmp(e, ee=e1, tkk=tk):
                                        for c in range(16):
                                            last = e.matmul(
                                                psA[:],
                                                wbf1[c][:, ee * 128:
                                                        (ee + 1) * 128],
                                                xn1T[:, c * T + tkk * 512:
                                                     c * T + (tkk + 1) * 512],
                                                start=(c == 0), stop=(c == 15))
                                        return last
                                    top(mmp, wbf1 + [xn1T], [psA],
                                        f"{tag}m{ec}_{tk}")
                                    if is_v:
                                        sop(lambda e, ecc=ec: e.activation(
                                                vev[:], psA[:], AF.Identity,
                                                bias=bc1[:, ecc:ecc + 1]),
                                            [psA, bc1], [vev],
                                            f"{tag}e{ec}_{tk}")
                                        vop(lambda e, ecc=ec, tkk=tk:
                                            e.tensor_tensor(
                                                kvT[:, ecc * T + tkk * 512:
                                                    ecc * T + (tkk + 1) * 512],
                                                kT[:, ecc * T + tkk * 512:
                                                   ecc * T + (tkk + 1) * 512],
                                                vev[:], ALU.mult),
                                            [kT, vev], [kvT],
                                            tag=f"{tag}kv{ec}_{tk}")
                                    else:
                                        sop(lambda e, ecc=ec, tkk=tk:
                                            e.activation(
                                                dst[:, ecc * T + tkk * 512:
                                                    ecc * T + (tkk + 1) * 512],
                                                psA[:], act,
                                                bias=bc1[:, ecc:ecc + 1]),
                                            [psA, bc1], [dst],
                                            f"{tag}e{ec}_{tk}")

                    l1proj("q", qT, AF.Identity, "Dq")
                    l1proj("k", kT, AF.Identity, "Dk")
                    l1proj("v", None, AF.Identity, "Dv", is_v=True)
                    l1proj("g", gT, AF.Sigmoid, "Dg")
                    P.barrier()
                    P.emit_block()
                P.barrier()
                P.emit_block()

            stD3 = ExitStack()
            with stD3:
                ssc = sb(stD3, "ssc", [128, 16], F32)
                sscb = sb(stD3, "sscb", [128, 16], F32)
                for ec in range(16):
                    vop(lambda e, ecc=ec: e.reduce_sum(
                            ssc[:, ecc:ecc + 1],
                            kvT[:, ecc * T:(ecc + 1) * T], axis=AX.X),
                        [kvT], [ssc], tag=f"D_ss{ec}")
                vop(lambda e: e.tensor_copy(sscb[:], ssc[:]), [ssc], [sscb],
                    tag="D_sscp")
                top(lambda e: e.transpose(psV[0:16, 0:128], sscb[:], eye_f[:]),
                    [sscb, eye_f], [psV], "D_sst")
                ssrow = sb(stD3, "ssrow", [16, 128], F32)
                vop(lambda e: e.tensor_copy(ssrow[:], psV[0:16, 0:128]),
                    [psV], [ssrow], tag="D_ssv")
                dma("gpsimd",
                    scc_in.rearrange("o (a b) -> (o a) b", b=128),
                    ssrow[:], [ssrow], [scc_in], "D_ssd")
                def cc_s(e):
                    e.collective_compute(
                        "AllGather", ALU.bypass,
                        replica_groups=[[0, 1, 2, 3, 4, 5, 6, 7]],
                        ins=[scc_in.opt()], outs=[scc_out.opt()]).then_inc(
                            P.ccsems[1], 1)
                    e.wait_ge(P.ccsems[1], 1)
                    return e.nop()
                P.op("gpsimd", cc_s, reads=[scc_in], writes=[scc_out], inc=1,
                     tag="cc_s")
                gsb = sb(stD3, "gsb", [8, D], F32)
                pmk = sb(stD3, "pmk", [8, 1], F32)
                offr = sb(stD3, "offr", [1, D], F32)
                offc = sb(stD3, "offc", [128, 16], F32)
                dma("sync", gsb[:], scc_out, [scc_out], [gsb], "D_gl")
                dma("sync", pmk[:], pmask_d, [], [pmk], "D_pm")
                for j in range(4):
                    top(lambda e, jj=j: e.matmul(
                            psA[0:1, :], pmk[:],
                            gsb[:, jj * 512:(jj + 1) * 512],
                            start=True, stop=True),
                        [pmk, gsb], [psA], f"D_of{j}")
                    vop(lambda e, jj=j: e.tensor_copy(
                            offr[0:1, jj * 512:(jj + 1) * 512], psA[0:1, :]),
                        [psA], [offr], tag=f"D_oc{j}")
                for ec in range(16):
                    row_to_col(offc[:, ec:ec + 1], offc,
                               offr[0:1, ec * 128:(ec + 1) * 128], [offr],
                               f"D_ol{ec}")
                kvsT = sb(stD3, "kvsT", [128, 16 * T], BF16)
                for ec in range(16):
                    vop(lambda e, ecc=ec: e.tensor_tensor_scan(
                            kvsT[:, ecc * T:(ecc + 1) * T],
                            kvT[:, ecc * T:(ecc + 1) * T],
                            kvT[:, ecc * T:(ecc + 1) * T],
                            offc[:, ecc:ecc + 1], ALU.add, ALU.bypass),
                        [kvT, offc], [kvsT], tag=f"D_sc{ec}")
                s1f = sb(stD3, "s1f", [128, 16 * T], BF16)
                vop(lambda e: e.tensor_tensor(s1f[:], qT[:], kvsT[:], ALU.mult),
                    [qT, kvsT], [s1f], tag="D_s1a")
                vop(lambda e: e.tensor_tensor(kvsT[:], s1f[:], gT[:], ALU.mult),
                    [s1f, gT], [kvsT], tag="D_s1b")
                stok = sb(stD3, "stok", [128, D], BF16)
                for tc in range(8):
                    for ec in range(16):
                        transpose128(
                            kvsT[:, ec * T + tc * 128:ec * T + (tc + 1) * 128],
                            [kvsT], stok[:, ec * 128:(ec + 1) * 128], stok,
                            f"D_st{tc}_{ec}")
                    dma("gpsimd", s1_d[tc * 128:(tc + 1) * 128, :], stok[:],
                        [stok], [s1_d], f"D_so{tc}")
                P.barrier()
                P.emit_block()

        if True:
            stE1 = ExitStack()
            with stE1:
                qxsT = sb(stE1, "qxsT_e", [128, 16 * T], BF16)
                dma("sync", qxsT[:], qxs_d, [qxs_d], [qxsT], "E1_qxld")
                fst = [sb(stE1, f"fst{i}", [128, D], F32) for i in range(2)]
                farg = sb(stE1, "farg", [128, D], F32)
                f8 = sb(stE1, "f8", [128, D], U8)
                fqc = sb(stE1, "fqc", [128, D], BF16)
                rmw = sb(stE1, "rmw", [128, 4], F32)
                qwsu = sb(stE1, "qwsu", [128, 16 * 512], BF16)
                qwsg = sb(stE1, "qwsg", [128, 16 * 512], BF16)
                silE = sb(stE1, "silE", [128, 512], BF16)
                t0e = sb(stE1, "t0e", [128, 512], F32)
                rmt = sb(stE1, "rmt", [128, 16], F32)
                rmt2 = sb(stE1, "rmt2", [128, 8], F32)
                memset(rmt, 0.0, "E_rmt0")

                def quant_rows(wsrc, r0, dstws, ich, tag):
                    stg = fst[ich % 2]
                    dma("sync", stg[:], wsrc[r0:r0 + 128, :], [], [stg],
                        tag + "l")
                    vop(lambda e: e.reduce_max(rmw[:, 0:1], stg[:], axis=AX.X,
                                               apply_absolute_value=True),
                        [stg], [rmw], tag=tag + "m")
                    vop(lambda e: e.reciprocal(rmw[:, 1:2], rmw[:, 0:1]),
                        [rmw], [rmw], tag=tag + "r")
                    vop(lambda e: e.tensor_scalar(rmw[:, 2:3], rmw[:, 1:2],
                                                  224.0, None, ALU.mult),
                        [rmw], [rmw], tag=tag + "s")
                    vop(lambda e: e.tensor_scalar(rmw[:, 3:4], rmw[:, 0:1],
                                                  1.0 / 224.0, None, ALU.mult),
                        [rmw], [rmw], tag=tag + "d")
                    vop(lambda e: e.tensor_scalar(farg[:], stg[:], rmw[:, 2:3],
                                                  None, ALU.mult),
                        [stg, rmw], [farg], tag=tag + "a")
                    sop(lambda e: e.copy(f8[:].bitcast(FP8), farg[:]),
                        [farg], [f8], tag + "8")
                    sop(lambda e: e.activation(fqc[:], f8[:].bitcast(FP8),
                                               AF.Identity, scale=rmw[:, 3:4]),
                        [f8, rmw], [fqc], tag + "q")
                    for c in range(16):
                        transpose128(fqc[:, c * 128:(c + 1) * 128], [fqc],
                                     dstws[:, c * 512 + ich * 128:
                                           c * 512 + (ich + 1) * 128],
                                     dstws, f"{tag}t{c}")

                for isl in range(NSL):
                    for ich in range(4):
                        quant_rows(fwu_d, isl * 512 + ich * 128, qwsu, ich,
                                   f"E1u{isl}_{ich}")
                        quant_rows(fwg_d, isl * 512 + ich * 128, qwsg, ich,
                                   f"E1g{isl}_{ich}")
                    for tc in range(8):
                        def mmug(e, tcc=tc):
                            for c in range(16):
                                e.matmul(psE[:],
                                         qxsT[:, c * T + tcc * 128:
                                              c * T + (tcc + 1) * 128],
                                         qwsu[:, c * 512:(c + 1) * 512],
                                         start=(c == 0), stop=(c == 15))
                            for c in range(16):
                                last = e.matmul(
                                    psF[:],
                                    qxsT[:, c * T + tcc * 128:
                                         c * T + (tcc + 1) * 128],
                                    qwsg[:, c * 512:(c + 1) * 512],
                                    start=(c == 0), stop=(c == 15))
                            return last
                        top(mmug, [qxsT, qwsu, qwsg], [psE, psF],
                            f"E1m{isl}_{tc}")
                        sop(lambda e: e.activation(silE[:], psF[:], AF.Silu),
                            [psF], [silE], f"E1s{isl}_{tc}")
                        vop(lambda e: e.tensor_tensor(t0e[:], silE[:], psE[:],
                                                      ALU.mult),
                            [silE, psE], [t0e], tag=f"E1t{isl}_{tc}")
                        dma("gpsimd",
                            t0_d[tc * 128:(tc + 1) * 128,
                                 isl * 512:(isl + 1) * 512],
                            t0e[:], [t0e], [t0_d], f"E1d{isl}_{tc}")
                        vop(lambda e, tcc=tc: e.reduce_max(
                                rmt2[:, tcc:tcc + 1], t0e[:], axis=AX.X,
                                apply_absolute_value=True),
                            [t0e], [rmt2], tag=f"E1r{isl}_{tc}")
                        vop(lambda e, tcc=tc, ii=isl: e.tensor_tensor(
                                rmt[:, 2 * tcc + (ii + 1) % 2:
                                    2 * tcc + (ii + 1) % 2 + 1],
                                rmt[:, 2 * tcc + ii % 2:2 * tcc + ii % 2 + 1],
                                rmt2[:, tcc:tcc + 1], ALU.max),
                            [rmt, rmt2], [rmt], tag=f"E1x{isl}_{tc}")
                P.barrier()
                P.emit_block()

                stE2 = ExitStack()
                with stE2:
                    t0r = [sb(stE2, f"t0r{i}", [128, DFF1], F32) for i in range(2)]
                    targ = sb(stE2, "targ", [128, DFF1], F32)
                    t8 = sb(stE2, "t8", [128, DFF1], U8)
                    tqs = sb(stE2, "tqs", [128, DFF1], BF16)
                    rmtc = sb(stE2, "rmtc", [128, 4], F32)
                    ttile = sb(stE2, "ttile", [128, 128], BF16)
                    for tc in range(8):
                        tr = t0r[tc % 2]
                        dma("sync", tr[:], t0_d[tc * 128:(tc + 1) * 128, :],
                            [], [tr], f"E2l{tc}")
                        vop(lambda e, tcc=tc: e.reciprocal(
                                rmtc[:, 0:1], rmt[:, 2 * tcc + 1:2 * tcc + 2]),
                            [rmt], [rmtc], tag=f"E2r{tc}")
                        vop(lambda e: e.tensor_scalar(rmtc[:, 2:3],
                                                      rmtc[:, 0:1], 224.0,
                                                      None, ALU.mult),
                            [rmtc], [rmtc], tag=f"E2s{tc}")
                        vop(lambda e, tcc=tc: e.tensor_scalar(
                                rmtc[:, 1:2], rmt[:, 2 * tcc + 1:2 * tcc + 2],
                                1.0 / 224.0, None, ALU.mult),
                            [rmt, rmtc], [rmtc], tag=f"E2d{tc}")
                        vop(lambda e, tr_=tr: e.tensor_scalar(
                                targ[:], tr_[:], rmtc[:, 2:3], None, ALU.mult),
                            [tr, rmtc], [targ], tag=f"E2a{tc}")
                        sop(lambda e: e.copy(t8[:].bitcast(FP8), targ[:]),
                            [targ], [t8], f"E28{tc}")
                        sop(lambda e: e.activation(tqs[:], t8[:].bitcast(FP8),
                                                   AF.Identity, scale=rmtc[:, 1:2]),
                            [t8, rmtc], [tqs], f"E2q{tc}")
                        for ich in range(NI):
                            transpose128(tqs[:, ich * 128:(ich + 1) * 128],
                                         [tqs], ttile[:], ttile,
                                         f"E2t{tc}_{ich}")
                            dma("gpsimd",
                                qts_d[ich * 128:(ich + 1) * 128,
                                      tc * 128:(tc + 1) * 128],
                                ttile[:], [ttile], [qts_d], f"E2o{tc}_{ich}")
                    P.barrier()
                    P.emit_block()
            P.barrier()
            P.emit_block()

        stE3 = ExitStack()
        with stE3:
            qt1T = sb(stE3, "qt1T", [128, NI * 128 * 8], BF16)
            dix = sb(stE3, "dix", [128, 1], I32)
            for ich in range(NI):
                dma("gpsimd", dix[:],
                    didx_d[ich * 128:(ich + 1) * 128, 0:1],
                    [], [dix], f"E3i{ich}")
                P.op("gpsimd",
                     lambda e, ii=ich: e.indirect_dma_start(
                         qt1T[:, ii * T:(ii + 1) * T], None, qts_d,
                         bass.IndirectOffsetOnAxis(dix[:], 0)),
                     reads=[qts_d, dix], writes=[qt1T], inc=16, tag=f"E3g{ich}")

            fdst = sb(stE3, "fdst", [128, DFF1], F32)
            fdarg = sb(stE3, "fdarg", [128, DFF1], F32)
            fd8 = sb(stE3, "fd8", [128, DFF1], U8)
            fdqc = sb(stE3, "fdqc", [128, DFF1], BF16)
            rmd = sb(stE3, "rmd", [128, 4], F32)
            qdsT = sb(stE3, "qdsT", [128, NI * 512], BF16)
            g1e = sb(stE3, "g1e", [128, 512], BF16)
            for dg in range(4):
                for dc in range(4):
                    r0 = dg * 512 + dc * 128
                    dma("sync", fdst[:], fwd_d[r0:r0 + 128, :], [], [fdst],
                        f"E3l{dg}_{dc}")
                    vop(lambda e: e.reduce_max(rmd[:, 0:1], fdst[:], axis=AX.X,
                                               apply_absolute_value=True),
                        [fdst], [rmd], tag=f"E3m{dg}_{dc}")
                    vop(lambda e: e.reciprocal(rmd[:, 1:2], rmd[:, 0:1]),
                        [rmd], [rmd], tag=f"E3r{dg}_{dc}")
                    vop(lambda e: e.tensor_scalar(rmd[:, 2:3], rmd[:, 1:2],
                                                  224.0, None, ALU.mult),
                        [rmd], [rmd], tag=f"E3s{dg}_{dc}")
                    vop(lambda e: e.tensor_scalar(rmd[:, 3:4], rmd[:, 0:1],
                                                  1.0 / 224.0, None, ALU.mult),
                        [rmd], [rmd], tag=f"E3d{dg}_{dc}")
                    vop(lambda e: e.tensor_scalar(fdarg[:], fdst[:],
                                                  rmd[:, 2:3], None, ALU.mult),
                        [fdst, rmd], [fdarg], tag=f"E3a{dg}_{dc}")
                    sop(lambda e: e.copy(fd8[:].bitcast(FP8), fdarg[:]),
                        [fdarg], [fd8], f"E38{dg}_{dc}")
                    sop(lambda e: e.activation(fdqc[:], fd8[:].bitcast(FP8),
                                               AF.Identity, scale=rmd[:, 3:4]),
                        [fd8, rmd], [fdqc], f"E3q{dg}_{dc}")
                    for ich in range(NI):
                        transpose128(fdqc[:, ich * 128:(ich + 1) * 128], [fdqc],
                                     qdsT[:, ich * 512 + dc * 128:
                                          ich * 512 + (dc + 1) * 128],
                                     qdsT, f"E3t{dg}_{dc}_{ich}")
                for tc in range(8):
                    def mmdn(e, tcc=tc):
                        for ich in range(NI):
                            last = e.matmul(
                                psA[:],
                                qt1T[:, ich * T + tcc * 128:
                                     ich * T + (tcc + 1) * 128],
                                qdsT[:, ich * 512:(ich + 1) * 512],
                                start=(ich == 0), stop=(ich == NI - 1))
                        return last
                    top(mmdn, [qt1T, qdsT], [psA], f"E3dm{dg}_{tc}")
                    sop(lambda e: e.activation(g1e[:], psA[:], AF.Silu),
                        [psA], [g1e], f"E3de{dg}_{tc}")
                    dma("gpsimd",
                        g1f_d[tc * 128:(tc + 1) * 128, dg * 512:(dg + 1) * 512],
                        g1e[:], [g1e], [g1f_d], f"E3do{dg}_{tc}")
            P.barrier()
            P.emit_block()

        if DEBUG:
            zin_dbg = nc.dram_tensor("zin_dbg", [512, S], BF16,
                                     kind="ExternalOutput").ap()
            zout_dbg = nc.dram_tensor("zout_dbg", [4096, S], BF16,
                                      kind="ExternalOutput").ap()
            sccdbg = nc.dram_tensor("scc_dbg", [8, D], F32,
                                    kind="ExternalOutput").ap()
            dma("gpsimd", zin_dbg, zcc_in, [zcc_in], [zin_dbg], "dbg_zi")
            dma("gpsimd", zout_dbg, zcc_out, [zcc_out], [zout_dbg], "dbg_zo")
            dma("gpsimd", sccdbg, scc_out, [scc_out], [sccdbg], "dbg_sc")
        stF = ExitStack()
        with stF:
            fx = sb(stF, "fx", [128, D], F32)
            fs = sb(stF, "fs", [128, D], BF16)
            fg = sb(stF, "fg", [128, D], BF16)
            fo = sb(stF, "fo", [128, D], F32)
            fo2 = sb(stF, "fo2", [128, D], F32)
            for tc in range(8):
                dma("sync", fx[:], x1_d[tc * 128:(tc + 1) * 128, :], [], [fx],
                    f"F_x{tc}")
                dma("sync", fs[:], s1_d[tc * 128:(tc + 1) * 128, :], [], [fs],
                    f"F_s{tc}")
                dma("sync", fg[:], g1f_d[tc * 128:(tc + 1) * 128, :], [], [fg],
                    f"F_g{tc}")
                vop(lambda e: e.tensor_tensor(fo[:], fs[:], fg[:], ALU.mult),
                    [fs, fg], [fo], tag=f"F_m{tc}")
                vop(lambda e: e.tensor_tensor(fo2[:], fx[:], fo[:], ALU.add),
                    [fx, fo], [fo2], tag=f"F_a{tc}")
                dma("gpsimd", out_d[tc * 128:(tc + 1) * 128, :], fo2[:],
                    [fo2], [out_d], f"F_o{tc}")
            snap = dict(P.counts)

            def fence(e):
                for f in Prog.ENGINES:
                    if f != "sync" and snap[f] > 0:
                        e.wait_ge(P.sems[f], snap[f])
                return e.nop()
            P.op("sync", fence, reads=[out_d], writes=[], inc=1, tag="fence")
            P.emit_block()
    return nc


_NC_CACHE = []
_LAST = [None]


def kernel(**inputs):
    gi = inputs.__getitem__
    x = np.asarray(gi("L0_attn_x"))
    _ = gi("L1_ffn_up_reorder_index")
    WQ = np.asarray(gi("L0_attn_W_Q")); WK = np.asarray(gi("L0_attn_W_K"))
    WV = np.asarray(gi("L0_attn_W_V")); WO = np.asarray(gi("L0_attn_W_O"))
    bQ = np.asarray(gi("L0_attn_b_Q")); bK = np.asarray(gi("L0_attn_b_K"))
    bV = np.asarray(gi("L0_attn_b_V")); bO = np.asarray(gi("L0_attn_b_O"))
    WU = np.asarray(gi("L0_ffn_W_up")); WG = np.asarray(gi("L0_ffn_W_gate"))
    WD = np.asarray(gi("L0_ffn_W_down"))
    l1 = {k: np.asarray(gi(f"L1_attn_W{k}")) for k in "qkvg"}
    l1b = {k: np.asarray(gi(f"L1_attn_b{k}")) for k in "qkvg"}
    fg = np.asarray(gi("L1_ffn_w_gate")); fu = np.asarray(gi("L1_ffn_w_up"))
    fd = np.asarray(gi("L1_ffn_w_down"))
    didx = np.asarray(gi("L1_ffn_down_reorder_index")).astype(np.int32)

    wut = np.ascontiguousarray(WU.T)
    wgt = np.ascontiguousarray(WG.T)
    wdt = np.ascontiguousarray(WD.T)
    l1t = {k: np.ascontiguousarray(l1[k].T) for k in "qkvg"}
    eye = np.eye(128, dtype=np.float32)
    hh = np.arange(896)[None, :]
    pp = np.arange(128)[:, None]
    maskw = np.where(hh >= pp + 384, 0.0, MASKV).astype(ml_dtypes.bfloat16)

    if not _NC_CACHE:
        _NC_CACHE.append(build_program())
    nc = _NC_CACHE[0]

    in_maps = []
    for c in range(8):
        Bb, Sl = c // 4, c % 4
        heads = [4 * Sl + j for j in range(4)]
        zr = np.empty((NH, 128), np.int32)
        for h in range(NH):
            r = 4 * Bb + h // 4
            orig = r * 512 + (h % 4) * 128 + np.arange(128)
            zr[h] = orig * 4 + Sl
        pm = np.zeros((8, 1), np.float32)
        for r in range(8):
            if r // 4 == Bb and r % 4 < Sl:
                pm[r, 0] = 1.0
        m = {
            "xb": x[Bb],
            "xown": x[Bb, Sl * T:(Sl + 1) * T],
            "wq": WQ[heads], "wk": WK[heads], "wv": WV[heads],
            "bq": bQ[heads], "bk": bK[heads], "bv": bV[heads],
            "wo": WO, "bo": bO.reshape(16, 128),
            "wut": wut, "wgt": wgt, "wdt": wdt,
            "l1q": l1t["q"], "l1k": l1t["k"], "l1v": l1t["v"], "l1g": l1t["g"],
            "l1bq": l1b["q"].reshape(16, 128),
            "l1bk": l1b["k"].reshape(16, 128),
            "l1bv": l1b["v"].reshape(16, 128),
            "l1bg": l1b["g"].reshape(16, 128),
            "fwg": fg, "fwu": fu, "fwd": fd,
            "didx": didx.reshape(NI * 128, 1),
            "zridx": zr.reshape(NH * 128, 1), "pmask": pm,
            "maskw": maskw, "eye": eye,
        }
        in_maps.append({k: np.ascontiguousarray(v) for k, v in m.items()})

    res = run_bass_kernel_spmd(nc, in_maps, core_ids=list(range(8)))
    _LAST[0] = res
    out = np.empty((B, S, D), np.float32)
    for c in range(8):
        Bb, Sl = c // 4, c % 4
        out[Bb, Sl * T:(Sl + 1) * T] = np.asarray(res.results[c]["out"])
    return out


# version 10
# speedup vs baseline: 1.3654x; 1.3588x over previous
import numpy as np
import ml_dtypes
import concourse.bass as bass
import concourse.mybir as mybir
from concourse.bass_utils import run_bass_kernel_spmd

F32 = mybir.dt.float32
BF16 = mybir.dt.bfloat16
FP8 = mybir.dt.float8e4
U8 = mybir.dt.uint8
I32 = mybir.dt.int32
AF = mybir.ActivationFunctionType
ALU = mybir.AluOpType
AX = mybir.AxisListType

B, S, D = 2, 4096, 2048
NH, DH = 16, 128
DFF0 = 8192
DFF1 = 5632
NI = DFF1 // 128
NSL = DFF1 // 512
T = 1024
RS = float(1.0 / np.sqrt(128.0))
MASKV = -1131371.0
EPS_LN = 1e-5
DEBUG = False


class Prog:
    ENGINES = ("sync", "scalar", "vector", "tensor", "gpsimd")

    def __init__(self, nc):
        self.nc = nc
        self.ops = []
        self.last_writer = {}
        self.readers = {}
        self.counts = {e: 0 for e in self.ENGINES}
        self.emitted = 0
        self.sems = {}
        self._barriers = []
        self._wm = {e: {f: 0 for f in self.ENGINES} for e in self.ENGINES}

    def init_sems(self, stack):
        for e in self.ENGINES:
            self.sems[e] = stack.enter_context(self.nc.semaphore(f"sem_{e}"))
        self.ccsems = [stack.enter_context(self.nc.semaphore(f"ccsem{i}"))
                       for i in range(2)]

    def op(self, eng, fn, reads=(), writes=(), inc=1, tag=""):
        idx = len(self.ops)
        deps = set()
        for b in reads:
            w = self.last_writer.get(id(b))
            if w is not None:
                deps.add(w)
        for b in writes:
            w = self.last_writer.get(id(b))
            if w is not None:
                deps.add(w)
            for r in self.readers.get(id(b), ()):
                deps.add(r)
        deps.discard(idx)
        self.counts[eng] += inc
        self.ops.append(dict(eng=eng, fn=fn, deps=deps, inc=inc, idx=idx,
                             tag=tag, count=self.counts[eng]))
        for b in reads:
            self.readers.setdefault(id(b), []).append(idx)
        for b in writes:
            self.last_writer[id(b)] = idx
            self.readers[id(b)] = [idx]
        return idx

    def barrier(self):
        self._barriers.append((len(self.ops), dict(self.counts)))

    def emit_block(self):
        start, end = self.emitted, len(self.ops)
        if start == end:
            return
        ops = self.ops
        per_eng = {e: [i for i in range(start, end) if ops[i]["eng"] == e]
                   for e in self.ENGINES}
        wm = self._wm
        sems = self.sems
        barriers = self._barriers

        def run_engine(engobj, ename):
            w = wm[ename]
            for i in per_eng[ename]:
                rec = ops[i]
                for bpos, bsnap in barriers:
                    if bpos <= i:
                        for f in self.ENGINES:
                            if f == ename:
                                w[f] = max(w[f], bsnap[f])
                            elif bsnap[f] > w[f]:
                                engobj.wait_ge(sems[f], bsnap[f])
                                w[f] = bsnap[f]
                need = {}
                for d in rec["deps"]:
                    dr = ops[d]
                    f = dr["eng"]
                    if f == ename == "tensor":
                        continue
                    if dr["count"] > w[f]:
                        need[f] = max(need.get(f, 0), dr["count"])
                for f in sorted(need):
                    engobj.wait_ge(sems[f], need[f])
                    w[f] = need[f]
                instr = rec["fn"](engobj)
                if instr is None:
                    raise RuntimeError(f"op {rec['tag']} returned None")
                instr.then_inc(sems[ename], rec["inc"])

        with self.nc.Block() as block:
            for ename in self.ENGINES:
                if per_eng[ename]:
                    getattr(block, ename)(
                        lambda engobj, _e=ename: run_engine(engobj, _e))
        self.emitted = end


def build_program():
    from contextlib import ExitStack
    nc = bass.Bass("TRN2", target_bir_lowering=False, num_devices=8)

    def din(name, shape, dt=F32):
        return nc.dram_tensor(name, shape, dt, kind="ExternalInput").ap()

    def dint(name, shape, dt=F32, dbg=False):
        kind = "ExternalOutput" if (dbg and DEBUG) else "Internal"
        return nc.dram_tensor(name, shape, dt, kind=kind).ap()

    xb_d = din("xb", [S, D])
    xown_d = din("xown", [T, D])
    wq_d = din("wq", [4, D, DH]); wk_d = din("wk", [4, D, DH]); wv_d = din("wv", [4, D, DH])
    bq_d = din("bq", [4, DH]); bk_d = din("bk", [4, DH]); bv_d = din("bv", [4, DH])
    wo_d = din("wo", [NH, DH, D])
    bo_d = din("bo", [16, 128])
    wut_d = din("wut", [D, DFF0]); wgt_d = din("wgt", [D, DFF0]); wdt_d = din("wdt", [DFF0, D])
    l1w_d = {k: din(f"l1{k}", [D, D]) for k in "qkvg"}
    l1b_d = {k: din(f"l1b{k}", [16, 128]) for k in "qkvg"}
    fwg_d = din("fwg", [DFF1, D]); fwu_d = din("fwu", [DFF1, D]); fwd_d = din("fwd", [D, DFF1])
    didx_d = din("didx", [NI * 128, 1], I32)
    zridx_d = din("zridx", [NH * 128, 1], I32)
    pmask_d = din("pmask", [8, 1])
    maskw_d = din("maskw", [128, 896], BF16)
    eye_d = din("eye", [128, 128])
    out_d = nc.dram_tensor("out", [T, D], F32, kind="ExternalOutput").ap()

    zcc_in = dint("zcc_in", [512, S], BF16)
    zcc_out = dint("zcc_out", [4096, S], BF16)
    scc_in = dint("scc_in", [1, D])
    scc_out = dint("scc_out", [8, D])
    offs_dram = dint("offs_dram", [1, D])
    s0_d = dint("s0_spill", [T, D], BF16, dbg=True)
    g0_d = dint("g0_spill", [T, D], BF16, dbg=True)
    x1_d = dint("x1_spill", [T, D], dbg=True)
    s1_d = dint("s1_spill", [T, D], BF16, dbg=True)
    t0_d = dint("t0_spill", [T, DFF1], dbg=True)
    ternd_d = dint("ternd_spill", [DFF0, D], BF16, dbg=True)
    qts_d = dint("qts_spill", [DFF1, T], BF16, dbg=True)
    qxs_d = dint("qxs_spill", [128, 16 * T], BF16, dbg=True)
    g1f_d = dint("g1f_spill", [T, D], BF16, dbg=True)
    if DEBUG:
        h_dbg = nc.dram_tensor("h_dbg", [DFF0, T], BF16,
                               kind="ExternalOutput").ap()

    P = Prog(nc)
    root = ExitStack()

    def sb(stk, name, shape, dt):
        return stk.enter_context(nc.sbuf_tensor(name, shape, dt))

    with root:
        P.init_sems(root)
        eye_f = sb(root, "eye_f", [128, 128], F32)
        eye_b = sb(root, "eye_b", [128, 128], BF16)
        ones_col_b = sb(root, "ones_col_b", [128, 1], BF16)
        ones_row_b = sb(root, "ones_row_b", [1, 128], BF16)
        ones_row_f = sb(root, "ones_row_f", [1, 128], F32)
        ones_col_f = sb(root, "ones_col_f", [128, 1], F32)
        one11_f = sb(root, "one11_f", [1, 1], F32)
        epsc = sb(root, "epsc", [128, 1], F32)
        thU = sb(root, "thU", [128, 3], F32)
        thG = sb(root, "thG", [128, 3], F32)
        thD = sb(root, "thD", [128, 3], F32)
        psA = root.enter_context(nc.psum_tensor("psA", [128, 512], F32))
        psB = root.enter_context(nc.psum_tensor("psB", [128, 512], F32))
        psE = root.enter_context(nc.psum_tensor("psE", [128, 512], F32))
        psF = root.enter_context(nc.psum_tensor("psF", [128, 512], F32))
        psTs = [root.enter_context(nc.psum_tensor(f"psT{i}", [128, 128], BF16))
                for i in range(3)]
        psV = root.enter_context(nc.psum_tensor("psV", [128, 256], F32))
        tp_ctr = [0]

        def dma(eng, dst_ap, src_ap, reads, writes, tag=""):
            P.op(eng, lambda e, d=dst_ap, s=src_ap: e.dma_start(out=d, in_=s),
                 reads=reads, writes=writes, inc=16, tag=tag)

        def vop(fn, reads, writes, eng="vector", tag=""):
            P.op(eng, fn, reads=reads, writes=writes, inc=1, tag=tag)

        def sop(fn, reads, writes, tag=""):
            P.op("scalar", fn, reads=reads, writes=writes, inc=1, tag=tag)

        def top(fn, reads, writes, tag=""):
            P.op("tensor", fn, reads=reads, writes=writes, inc=1, tag=tag)

        def memset(t, val, tag="ms"):
            vop(lambda e, tt=t, v=val: e.memset(tt[:], v), [], [t], tag=tag)

        def row_to_col(dst_ap, dst_buf, row_ap, row_bufs, tag):
            top(lambda e, r=row_ap: e.matmul(psV[:, 0:1], r, one11_f[:],
                                             start=True, stop=True),
                list(row_bufs) + [one11_f], [psV], tag + "_mm")
            vop(lambda e, d=dst_ap: e.tensor_copy(d, psV[:, 0:1]),
                [psV], [dst_buf], tag=tag + "_cp")

        dma("sync", eye_f[:], eye_d, [], [eye_f], "ld_eye")
        vop(lambda e: e.tensor_copy(eye_b[:], eye_f[:]), [eye_f], [eye_b], tag="eyeb")
        memset(ones_col_b, 1.0, "o1"); memset(ones_row_b, 1.0, "o2")
        memset(ones_row_f, 1.0, "o3"); memset(ones_col_f, 1.0, "o4")
        memset(one11_f, 1.0, "o5")
        memset(epsc, EPS_LN, "o6")

        def make_ln(stk, pref):
            sq = sb(stk, pref + "sq", [128, D], BF16)
            st_ = sb(stk, pref + "st", [128, 8], F32)
            nm = sb(stk, pref + "nm", [128, 1], F32)
            rs_ = sb(stk, pref + "rs", [128, 1], F32)

            def ln(xtile, out_bf_ap, out_bufs, tag, extra_f32=None):
                vop(lambda e: e.tensor_scalar(sq[:], xtile[:], 1.0, None,
                                              ALU.mult, ALU.add,
                                              accum_out=st_[:, 0:1]),
                    [xtile], [sq, st_], tag=tag + "_s")
                sop(lambda e: e.activation(sq[:], xtile[:], AF.Square,
                                           accum_out=st_[:, 1:2]),
                    [xtile], [sq, st_], tag + "_q")
                vop(lambda e: e.tensor_scalar(st_[:, 2:3], st_[:, 0:1],
                                              1.0 / D, None, ALU.mult),
                    [st_], [st_], tag=tag + "_m")
                vop(lambda e: e.tensor_tensor(st_[:, 3:4], st_[:, 2:3],
                                              st_[:, 2:3], ALU.mult),
                    [st_], [st_], tag=tag + "_m2")
                vop(lambda e: e.tensor_scalar(st_[:, 4:5], st_[:, 1:2],
                                              1.0 / D, None, ALU.mult),
                    [st_], [st_], tag=tag + "_e2")
                vop(lambda e: e.tensor_tensor(st_[:, 5:6], st_[:, 4:5],
                                              st_[:, 3:4], ALU.subtract),
                    [st_], [st_], tag=tag + "_v")
                sop(lambda e: e.activation(st_[:, 6:7], st_[:, 5:6], AF.Sqrt,
                                           bias=epsc[:, 0:1]),
                    [st_, epsc], [st_], tag + "_sd")
                vop(lambda e: e.reciprocal(rs_[:], st_[:, 6:7]),
                    [st_], [rs_], tag=tag + "_r")
                vop(lambda e: e.tensor_scalar(nm[:], st_[:, 2:3], -1.0, None,
                                              ALU.mult),
                    [st_], [nm], tag=tag + "_n")
                vop(lambda e: e.tensor_scalar(out_bf_ap, xtile[:], nm[:], rs_[:],
                                              ALU.add, ALU.mult),
                    [xtile, nm, rs_], out_bufs, tag=tag + "_x")
                if extra_f32 is not None:
                    ap_, bufs_ = extra_f32
                    vop(lambda e: e.tensor_scalar(ap_, xtile[:], nm[:], rs_[:],
                                                  ALU.add, ALU.mult),
                        [xtile, nm, rs_], bufs_, tag=tag + "_xf")
            return ln

        def transpose128(src_ap, src_bufs, dst_ap, dst_buf, tag):
            pt = psTs[tp_ctr[0] % 3]
            tp_ctr[0] += 1
            top(lambda e, s=src_ap, p=pt: e.transpose(p[:], s, eye_b[:]),
                list(src_bufs) + [eye_b], [pt], tag + "_t")
            vop(lambda e, d=dst_ap, p=pt: e.tensor_copy(d, p[:]),
                [pt], [dst_buf], tag=tag + "_c")

        stA = ExitStack()
        with stA:
            xnT = [sb(stA, f"xnT{i}", [128, S], BF16) for i in range(16)]
            xt = [sb(stA, f"xtA{i}", [128, D], F32) for i in range(2)]
            xnb = sb(stA, "xnb", [128, D], BF16)
            lnA = make_ln(stA, "lA")

            for t in range(32):
                xi = xt[t % 2]
                dma("sync", xi[:], xb_d[t * 128:(t + 1) * 128, :], [], [xi],
                    f"A_x{t}")
                lnA(xi, xnb[:], [xnb], f"A_ln{t}")
                for c in range(16):
                    transpose128(xnb[:, c * 128:(c + 1) * 128], [xnb],
                                 xnT[c][:, t * 128:(t + 1) * 128], xnT[c],
                                 f"A_tp{t}_{c}")

            accw = sb(stA, "accw", [128, 16], F32)
            wmt = [sb(stA, f"wmt{i}", [128, 512], F32) for i in range(2)]
            wabs = sb(stA, "wabs", [128, 512], BF16)
            wsum = sb(stA, "wsum", [1, 4], F32)
            memset(accw, 0.0, "accw0")

            def mean_pass(wsrc, nrow, ncol, base, tag):
                k = 0
                ntile = (nrow // 128) * (ncol // 512)
                for r in range(nrow // 128):
                    for c in range(ncol // 512):
                        wt = wmt[k % 2]
                        dma("gpsimd", wt[:],
                            wsrc[r * 128:(r + 1) * 128, c * 512:(c + 1) * 512],
                            [], [wt], f"{tag}l{k}")
                        sop(lambda e, w=wt, kk=k: e.activation(
                                wabs[:], w[:], AF.Abs,
                                accum_out=accw[:, base + (kk % 2):base + (kk % 2) + 1]),
                            [wt], [wabs, accw], f"{tag}a{k}")
                        rd = base + 2 + (k % 2)
                        wr = base + 2 + ((k + 1) % 2)
                        vop(lambda e, kk=k, rd_=rd, wr_=wr: e.tensor_tensor(
                                accw[:, wr_:wr_ + 1],
                                accw[:, rd_:rd_ + 1],
                                accw[:, base + (kk % 2):base + (kk % 2) + 1],
                                ALU.add),
                            [accw], [accw], tag=f"{tag}c{k}")
                        k += 1
                return base + 2 + (ntile % 2)

            def mean_finish(fcol, th_t, idx, tag):
                top(lambda e: e.matmul(psV[0:1, 4:5],
                                       accw[:, fcol:fcol + 1],
                                       ones_col_f[:], start=True, stop=True),
                    [accw, ones_col_f], [psV], tag + "_r")
                vop(lambda e: e.tensor_scalar(wsum[0:1, idx:idx + 1],
                                              psV[0:1, 4:5],
                                              1.0 / 16777216.0, 1e-8,
                                              ALU.mult, ALU.add),
                    [psV], [wsum], tag=tag + "_t")
                top(lambda e: e.matmul(psV[:, 8:9], ones_row_f[:],
                                       wsum[0:1, idx:idx + 1],
                                       start=True, stop=True),
                    [ones_row_f, wsum], [psV], tag + "_b")
                vop(lambda e: e.tensor_copy(th_t[:, 2:3], psV[:, 8:9]),
                    [psV], [th_t], tag=tag + "_f")
                vop(lambda e: e.tensor_scalar(th_t[:, 0:1], psV[:, 8:9], 0.5,
                                              None, ALU.mult),
                    [psV], [th_t], tag=tag + "_p")
                vop(lambda e: e.tensor_scalar(th_t[:, 1:2], psV[:, 8:9], -0.5,
                                              None, ALU.mult),
                    [psV], [th_t], tag=tag + "_n")

            fc = mean_pass(wut_d, D, DFF0, 0, "mu")
            mean_finish(fc, thU, 0, "mu")
            fc = mean_pass(wgt_d, D, DFF0, 5, "mg")
            mean_finish(fc, thG, 1, "mg")
            fc = mean_pass(wdt_d, DFF0, D, 10, "md")
            mean_finish(fc, thD, 2, "md")

            maskw = sb(stA, "maskw_s", [128, 896], BF16)
            dma("sync", maskw[:], maskw_d, [], [maskw], "A_mask")
            wstg = sb(stA, "wstg", [128, 1024], F32)
            wbf = sb(stA, "wbf", [128, D], BF16)
            KT = sb(stA, "KT", [128, S], BF16)
            QT = sb(stA, "QT", [128, S], BF16)
            Vtok = sb(stA, "Vtok", [128, S], BF16)
            vtmp = sb(stA, "vtmp", [128, 512], BF16)
            sqb = sb(stA, "sqb", [128, 512], BF16)
            rsc = sb(stA, "rsc", [1, 512], F32)
            mks = sb(stA, "mks", [1, 4], F32)
            negM = sb(stA, "negM", [1, 512], BF16)
            aT = [sb(stA, f"aT{i}", [128, 512], BF16) for i in range(2)]
            rb = sb(stA, "rb", [128, 512], F32)
            ztile = [sb(stA, f"ztile{i}", [128, 512], BF16) for i in range(2)]
            bcol = sb(stA, "bcol", [128, 3], F32)
            brow = sb(stA, "brow", [1, 128], F32)

            def bias_col(bsrc, p, col, tag):
                dma("sync", brow[:], bsrc[p:p + 1, :], [], [brow], tag + "_l")
                row_to_col(bcol[:, col:col + 1], bcol, brow[:], [brow], tag)

            def proj(wsrc, p, dst, bias_c, tag):
                wrs = wsrc[p].rearrange("(c p) e -> p c e", p=128)
                wsv = wstg[:].rearrange("p (c e) -> p c e", e=128)
                for hf in range(2):
                    dma("sync", wsv, wrs[:, hf * 8:(hf + 1) * 8, :],
                        [], [wstg], tag + f"_w{hf}")
                    vop(lambda e, h_=hf: e.tensor_copy(
                            wbf[:, h_ * 1024:(h_ + 1) * 1024], wstg[:]),
                        [wstg], [wbf], tag=tag + f"_v{hf}")
                for tk in range(8):
                    bk = psA if tk % 2 == 0 else psF
                    def mm(e, tkk=tk, bk_=bk):
                        for c in range(16):
                            last = e.matmul(
                                bk_[:], wbf[:, c * 128:(c + 1) * 128],
                                xnT[c][:, tkk * 512:(tkk + 1) * 512],
                                start=(c == 0), stop=(c == 15))
                        return last
                    top(mm, [wbf] + xnT, [bk], f"{tag}m{tk}")
                    sop(lambda e, tkk=tk, bk_=bk: e.activation(
                            dst[:, tkk * 512:(tkk + 1) * 512], bk_[:],
                            AF.Identity,
                            bias=bcol[:, bias_c:bias_c + 1]),
                        [bk, bcol], [dst], f"{tag}e{tk}")

            for p in range(4):
                bias_col(bk_d, p, 1, f"p{p}bk")
                bias_col(bq_d, p, 0, f"p{p}bq")
                bias_col(bv_d, p, 2, f"p{p}bv")
                proj(wk_d, p, KT, 1, f"p{p}K")
                memset(mks, 0.0, f"p{p}km0")
                for tk in range(8):
                    sop(lambda e, tkk=tk: e.activation(
                            sqb[:], KT[:, tkk * 512:(tkk + 1) * 512], AF.Square),
                        [KT], [sqb], f"p{p}ks{tk}")
                    top(lambda e: e.matmul(psF[0:1, :], ones_col_b[:], sqb[:],
                                           start=True, stop=True),
                        [sqb, ones_col_b], [psF], f"p{p}kn{tk}")
                    vop(lambda e: e.tensor_copy(rsc[0:1, :], psF[0:1, :]),
                        [psF], [rsc], tag=f"p{p}kc{tk}")
                    vop(lambda e: e.reduce_max(mks[0:1, 2:3], rsc[0:1, :],
                                               axis=AX.X),
                        [rsc], [mks], tag=f"p{p}kd{tk}")
                    vop(lambda e, tkk=tk: e.tensor_tensor(
                            mks[0:1, 3 if tkk % 2 == 0 else 0:
                                4 if tkk % 2 == 0 else 1],
                            mks[0:1, 0 if tkk % 2 == 0 else 3:
                                1 if tkk % 2 == 0 else 4],
                            mks[0:1, 2:3], ALU.max),
                        [mks], [mks], tag=f"p{p}kx{tk}")
                sop(lambda e: e.activation(mks[0:1, 1:2], mks[0:1, 0:1], AF.Sqrt),
                    [mks], [mks], f"p{p}kq")
                proj(wq_d, p, QT, 0, f"p{p}Q")
                wrs_v = wv_d[p].rearrange("(c p) e -> p c e", p=128)
                wsv_v = wstg[:].rearrange("p (c e) -> p c e", e=128)
                for hf in range(2):
                    dma("sync", wsv_v, wrs_v[:, hf * 8:(hf + 1) * 8, :],
                        [], [wstg], f"p{p}Vw{hf}")
                    vop(lambda e, h_=hf: e.tensor_copy(
                            wbf[:, h_ * 1024:(h_ + 1) * 1024], wstg[:]),
                        [wstg], [wbf], tag=f"p{p}Vv{hf}")
                for tk in range(8):
                    bk = psB if tk % 2 == 0 else psE
                    def mmv(e, tkk=tk, bk_=bk):
                        for c in range(16):
                            last = e.matmul(
                                bk_[:], wbf[:, c * 128:(c + 1) * 128],
                                xnT[c][:, tkk * 512:(tkk + 1) * 512],
                                start=(c == 0), stop=(c == 15))
                        return last
                    top(mmv, [wbf] + xnT, [bk], f"p{p}vm{tk}")
                    sop(lambda e, bk_=bk: e.activation(vtmp[:], bk_[:],
                                                       AF.Identity,
                                                       bias=bcol[:, 2:3]),
                        [bk, bcol], [vtmp], f"p{p}ve{tk}")
                    for k in range(4):
                        transpose128(
                            vtmp[:, k * 128:(k + 1) * 128], [vtmp],
                            Vtok[:, tk * 512 + k * 128:tk * 512 + (k + 1) * 128],
                            Vtok, f"p{p}vt{tk}_{k}")

                for tq in range(8):
                    sop(lambda e, tqq=tq: e.activation(
                            sqb[:], QT[:, tqq * 512:(tqq + 1) * 512], AF.Square),
                        [QT], [sqb], f"p{p}q{tq}s")
                    top(lambda e: e.matmul(psF[0:1, :], ones_col_b[:], sqb[:],
                                           start=True, stop=True),
                        [sqb, ones_col_b], [psF], f"p{p}q{tq}n")
                    sop(lambda e: e.activation(rsc[0:1, :], psF[0:1, :], AF.Sqrt),
                        [psF], [rsc], f"p{p}q{tq}r")
                    vop(lambda e: e.tensor_scalar(negM[0:1, :], rsc[0:1, :],
                                                  mks[0:1, 1:2], -1.0,
                                                  ALU.mult, ALU.mult),
                        [rsc, mks], [negM], tag=f"p{p}q{tq}m")
                    ntk = 4 * tq + 4
                    for tkc in range(ntk):
                        ab = aT[tkc % 2]
                        sbank = psA if tkc % 2 == 0 else psF
                        dg = tkc * 128 - tq * 512
                        def mms(e, tkk=tkc, tqq=tq, dgg=dg, bk=sbank):
                            e.matmul(bk[:], KT[:, tkk * 128:(tkk + 1) * 128],
                                     QT[:, tqq * 512:(tqq + 1) * 512],
                                     start=True, stop=False)
                            last = e.matmul(bk[:], ones_row_b[:], negM[0:1, :],
                                            start=False, stop=(dgg < 0))
                            if dgg >= 0:
                                off = 384 - dgg
                                last = e.matmul(bk[:], eye_b[:],
                                                maskw[:, off:off + 512],
                                                start=False, stop=True)
                            return last
                        top(mms, [KT, QT, negM, ones_row_b, eye_b, maskw],
                            [sbank], f"p{p}q{tq}k{tkc}s")
                        sop(lambda e, b=ab, bk=sbank: e.activation(
                                b[:], bk[:], AF.Exp, scale=RS),
                            [sbank], [ab], f"p{p}q{tq}k{tkc}e")
                        def mmz(e, tkk=tkc, b=ab, fi=(tkc == 0),
                                la=(tkc == ntk - 1)):
                            e.matmul(psB[:], Vtok[:, tkk * 128:(tkk + 1) * 128],
                                     b[:], start=fi, stop=la)
                            return e.matmul(psE[0:1, :], ones_col_b[:], b[:],
                                            start=fi, stop=la)
                        top(mmz, [Vtok, ab, ones_col_b], [psB, psE],
                            f"p{p}q{tq}k{tkc}z")
                    vop(lambda e: e.reciprocal(rb[0:1, :], psE[0:1, :]),
                        [psE], [rb], tag=f"p{p}q{tq}i")
                    top(lambda e: e.matmul(psA[:], ones_row_f[:], rb[0:1, :],
                                           start=True, stop=True),
                        [ones_row_f, rb], [psA], f"p{p}q{tq}b")
                    vop(lambda e: e.tensor_copy(rb[:], psA[:]),
                        [psA], [rb], tag=f"p{p}q{tq}c")
                    zt = ztile[tq % 2]
                    vop(lambda e, z_=zt: e.tensor_tensor(z_[:], psB[:], rb[:],
                                                         ALU.mult),
                        [psB, rb], [zt], tag=f"p{p}q{tq}z")
                    dma("sync",
                        zcc_in[p * 128:(p + 1) * 128, tq * 512:(tq + 1) * 512],
                        zt[:], [zt], [zcc_in], f"p{p}q{tq}d")

            def cc_z(e):
                e.collective_compute(
                    "AllGather", ALU.bypass,
                    replica_groups=[[0, 1, 2, 3, 4, 5, 6, 7]],
                    ins=[zcc_in.opt()], outs=[zcc_out.opt()]).then_inc(
                        P.ccsems[0], 1)
                e.wait_ge(P.ccsems[0], 1)
                return e.nop()
            P.op("gpsimd", cc_z, reads=[zcc_in], writes=[zcc_out], inc=1,
                 tag="cc_z")
            P.barrier()
            P.emit_block()

        stB = ExitStack()
        with stB:
            xnoT = [sb(stB, f"xnoT{i}", [128, T], BF16) for i in range(16)]
            stB1 = ExitStack()
            with stB1:
                xt0 = sb(stB1, "xtB", [128, D], F32)
                xnb2 = sb(stB1, "xnb2", [128, D], BF16)
                lnB = make_ln(stB1, "lB")
                for t in range(8):
                    dma("sync", xt0[:], xown_d[t * 128:(t + 1) * 128, :],
                        [], [xt0], f"B_x{t}")
                    lnB(xt0, xnb2[:], [xnb2], f"B_ln{t}")
                    for c in range(16):
                        transpose128(xnb2[:, c * 128:(c + 1) * 128], [xnb2],
                                     xnoT[c][:, t * 128:(t + 1) * 128],
                                     xnoT[c], f"B_tp{t}_{c}")
                zsl = [sb(stB1, f"zsl{i}", [128, T], BF16) for i in range(16)]
                zidx = sb(stB1, "zidx", [128, 1], I32)
                wost = sb(stB1, "wost", [128, 512], F32)
                wobf = [sb(stB1, f"wobf{i}", [128, 512], BF16) for i in range(16)]
                borow = sb(stB1, "borow", [1, 512], F32)
                borowb = sb(stB1, "borowb", [1, 512], BF16)
                sev = sb(stB1, "sev", [128, 512], BF16)
                zv = zcc_out.rearrange("r (q c) -> (r q) c", c=T)
                bo_flat = bo_d.rearrange("a b -> (a b)").unsqueeze(0)
                for h in range(NH):
                    dma("gpsimd", zidx[:],
                        zridx_d[h * 128:(h + 1) * 128, 0:1],
                        [], [zidx], f"B_zi{h}")
                    P.op("gpsimd",
                         lambda e, hh=h: e.indirect_dma_start(
                             zsl[hh][:], None, zv,
                             bass.IndirectOffsetOnAxis(zidx[:], 0)),
                         reads=[zcc_out, zidx], writes=[zsl[h]], inc=16,
                         tag=f"B_zg{h}")
                for dg in range(4):
                    dma("sync", borow[0:1, :],
                        bo_flat[0:1, dg * 512:(dg + 1) * 512],
                        [], [borow], f"B_bo{dg}")
                    vop(lambda e: e.tensor_copy(borowb[0:1, :], borow[0:1, :]),
                        [borow], [borowb], tag=f"B_boc{dg}")
                    for h in range(NH):
                        dma("sync", wost[:], wo_d[h, :, dg * 512:(dg + 1) * 512],
                            [], [wost], f"B_wo{dg}_{h}")
                        vop(lambda e, hh=h: e.tensor_copy(wobf[hh][:], wost[:]),
                            [wost], [wobf[h]], tag=f"B_wc{dg}_{h}")
                    for tc in range(8):
                        bk = psA if tc % 2 == 0 else psF
                        def mmo(e, tcc=tc, bk_=bk):
                            for h in range(NH):
                                e.matmul(bk_[:],
                                         zsl[h][:, tcc * 128:(tcc + 1) * 128],
                                         wobf[h][:], start=(h == 0), stop=False)
                            return e.matmul(bk_[:], ones_row_b[:],
                                            borowb[0:1, :],
                                            start=False, stop=True)
                        top(mmo, zsl + wobf + [ones_row_b, borowb], [bk],
                            f"B_om{dg}_{tc}")
                        sop(lambda e, bk_=bk: e.activation(sev[:], bk_[:],
                                                           AF.Identity),
                            [bk], [sev], f"B_oe{dg}_{tc}")
                        dma("gpsimd",
                            s0_d[tc * 128:(tc + 1) * 128, dg * 512:(dg + 1) * 512],
                            sev[:], [sev], [s0_d], f"B_od{dg}_{tc}")
                P.barrier()
                P.emit_block()

            hT = [sb(stB, f"hT{i}", [128, T], BF16) for i in range(64)]
            stB2 = ExitStack()
            with stB2:
                tstg = [sb(stB2, f"tstg{i}", [128, 256], F32) for i in range(2)]
                ternu = [sb(stB2, f"tu{i}", [128, 256], BF16) for i in range(16)]
                terng = [sb(stB2, f"tg{i}", [128, 256], BF16) for i in range(16)]
                tlt = sb(stB2, "tlt", [128, 256], BF16)
                tgt2 = sb(stB2, "tgt2", [128, 256], BF16)
                sil = sb(stB2, "sil", [128, 512], BF16)
                gsc = sb(stB2, "gsc", [128, 512], BF16)

                def ternary(dst, src, th, tag):
                    vop(lambda e: e.tensor_scalar(tgt2[:], src[:], th[:, 0:1],
                                                  None, ALU.is_gt),
                        [src, th], [tgt2], tag=tag + "g")
                    vop(lambda e: e.tensor_scalar(tlt[:], src[:], th[:, 1:2],
                                                  None, ALU.is_lt),
                        [src, th], [tlt], tag=tag + "l")
                    vop(lambda e: e.tensor_tensor(dst[:], tgt2[:], tlt[:],
                                                  ALU.subtract),
                        [tgt2, tlt], [dst], tag=tag + "s")

                for es in range(32):
                    for c in range(16):
                        stg = tstg[c % 2]
                        dma("sync", stg[:],
                            wut_d[c * 128:(c + 1) * 128, es * 256:(es + 1) * 256],
                            [], [stg], f"B2u{es}_{c}")
                        ternary(ternu[c], stg, thU, f"B2qu{es}_{c}")
                        dma("sync", stg[:],
                            wgt_d[c * 128:(c + 1) * 128, es * 256:(es + 1) * 256],
                            [], [stg], f"B2g{es}_{c}")
                        ternary(terng[c], stg, thG, f"B2qg{es}_{c}")
                    for e1 in range(2):
                        ech = es * 2 + e1
                        for tk in range(2):
                            def mmu(e, ee=e1, tkk=tk):
                                for c in range(16):
                                    e.matmul(psE[:],
                                             ternu[c][:, ee * 128:(ee + 1) * 128],
                                             xnoT[c][:, tkk * 512:(tkk + 1) * 512],
                                             start=(c == 0), stop=(c == 15))
                                for c in range(16):
                                    last = e.matmul(
                                        psF[:],
                                        terng[c][:, ee * 128:(ee + 1) * 128],
                                        xnoT[c][:, tkk * 512:(tkk + 1) * 512],
                                        start=(c == 0), stop=(c == 15))
                                return last
                            top(mmu, ternu + terng + xnoT, [psE, psF],
                                f"B2m{ech}_{tk}")
                            sop(lambda e: e.activation(sil[:], psE[:], AF.Silu,
                                                       scale=thU[:, 2:3]),
                                [psE, thU], [sil], f"B2s{ech}_{tk}")
                            sop(lambda e: e.activation(gsc[:], psF[:],
                                                       AF.Identity,
                                                       scale=thG[:, 2:3]),
                                [psF, thG], [gsc], f"B2g2{ech}_{tk}")
                            vop(lambda e, ec=ech, tkk=tk: e.tensor_tensor(
                                    hT[ec][:, tkk * 512:(tkk + 1) * 512],
                                    sil[:], gsc[:], ALU.mult),
                                [sil, gsc], [hT[ech]], tag=f"B2h{ech}_{tk}")
                if DEBUG:
                    for ec in range(64):
                        dma("gpsimd", h_dbg[ec * 128:(ec + 1) * 128, :],
                            hT[ec][:], [hT[ec]], [h_dbg], f"dbg_h{ec}")
                P.barrier()
                P.emit_block()

            stB3 = ExitStack()
            with stB3:
                dstg = [sb(stB3, f"dstg{i}", [128, D], F32) for i in range(2)]
                dt_b = sb(stB3, "dt_b", [128, D], BF16)
                dlt = sb(stB3, "dlt", [128, D], BF16)
                dgt = sb(stB3, "dgt", [128, D], BF16)
                for ec in range(64):
                    stg = dstg[ec % 2]
                    dma("sync", stg[:], wdt_d[ec * 128:(ec + 1) * 128, :],
                        [], [stg], f"B3l{ec}")
                    vop(lambda e, s_=stg: e.tensor_scalar(dgt[:], s_[:],
                                                  thD[:, 0:1],
                                                  None, ALU.is_gt),
                        [stg, thD], [dgt], tag=f"B3g{ec}")
                    vop(lambda e, s_=stg: e.tensor_scalar(dlt[:], s_[:],
                                                  thD[:, 1:2],
                                                  None, ALU.is_lt),
                        [stg, thD], [dlt], tag=f"B3t{ec}")
                    vop(lambda e: e.tensor_tensor(dt_b[:], dgt[:], dlt[:],
                                                  ALU.subtract),
                        [dgt, dlt], [dt_b], tag=f"B3s{ec}")
                    dma("gpsimd", ternd_d[ec * 128:(ec + 1) * 128, :], dt_b[:],
                        [dt_b], [ternd_d], f"B3d{ec}")
                tdt = [sb(stB3, f"tdt{i}", [128, 512], BF16) for i in range(2)]
                g0e = sb(stB3, "g0e", [128, 512], BF16)
                PS4 = [psA, psB, psE, psF]
                for dg in range(4):
                    for th_half in range(2):
                        for ec in range(64):
                            tb = tdt[ec % 2]
                            dma("sync", tb[:],
                                ternd_d[ec * 128:(ec + 1) * 128,
                                        dg * 512:(dg + 1) * 512],
                                [ternd_d], [tb], f"B3r{dg}_{th_half}_{ec}")
                            def mmd(e, tb_=tb, ecc=ec, thh=th_half):
                                for t4 in range(4):
                                    tc = thh * 4 + t4
                                    last = e.matmul(
                                        PS4[t4][:],
                                        hT[ecc][:, tc * 128:(tc + 1) * 128],
                                        tb_[:], start=(ecc == 0),
                                        stop=(ecc == 63))
                                return last
                            top(mmd, [tb] + hT, PS4, f"B3m{dg}_{th_half}_{ec}")
                        for t4 in range(4):
                            tc = th_half * 4 + t4
                            sop(lambda e, t4_=t4: e.activation(
                                    g0e[:], PS4[t4_][:], AF.Silu,
                                    scale=thD[:, 2:3]),
                                [PS4[t4], thD], [g0e], f"B3e{dg}_{th_half}_{t4}")
                            dma("gpsimd",
                                g0_d[tc * 128:(tc + 1) * 128,
                                     dg * 512:(dg + 1) * 512],
                                g0e[:], [g0e], [g0_d], f"B3o{dg}_{th_half}_{t4}")
                P.barrier()
                P.emit_block()
            P.barrier()
            P.emit_block()

        stB4 = ExitStack()
        with stB4:
            s0t = sb(stB4, "s0t", [128, D], BF16)
            g0t = sb(stB4, "g0t", [128, D], BF16)
            xot = sb(stB4, "xot", [128, D], F32)
            sgt = sb(stB4, "sgt", [128, D], F32)
            x1o = sb(stB4, "x1o", [128, D], F32)
            for tc in range(8):
                dma("sync", s0t[:], s0_d[tc * 128:(tc + 1) * 128, :], [], [s0t],
                    f"B4s{tc}")
                dma("sync", g0t[:], g0_d[tc * 128:(tc + 1) * 128, :], [], [g0t],
                    f"B4g{tc}")
                dma("sync", xot[:], xown_d[tc * 128:(tc + 1) * 128, :], [], [xot],
                    f"B4x{tc}")
                vop(lambda e: e.tensor_tensor(sgt[:], s0t[:], g0t[:], ALU.mult),
                    [s0t, g0t], [sgt], tag=f"B4m{tc}")
                vop(lambda e: e.tensor_tensor(x1o[:], xot[:], sgt[:], ALU.add),
                    [xot, sgt], [x1o], tag=f"B4a{tc}")
                dma("gpsimd", x1_d[tc * 128:(tc + 1) * 128, :], x1o[:],
                    [x1o], [x1_d], f"B4o{tc}")
            P.barrier()
            P.emit_block()

        stDo = ExitStack()
        with stDo:
            qT = sb(stDo, "qT", [128, 16 * T], BF16)
            gT = sb(stDo, "gT", [128, 16 * T], BF16)
            kvT = sb(stDo, "kvT", [128, 16 * T], BF16)
            stD = ExitStack()
            with stD:
                xn1T = sb(stD, "xn1T", [128, 16 * T], BF16)
                stD1 = ExitStack()
                with stD1:
                    qxsT = sb(stD1, "qxsT", [128, 16 * T], BF16)
                    x1t = sb(stD1, "x1t", [128, D], F32)
                    xn1b = sb(stD1, "xn1b", [128, D], BF16)
                    xn1f = sb(stD1, "xn1f", [128, D], F32)
                    qarg = sb(stD1, "qarg", [128, D], F32)
                    q8 = sb(stD1, "q8", [128, D], U8)
                    qxsb = sb(stD1, "qxsb", [128, D], BF16)
                    rmx = sb(stD1, "rmx", [128, 4], F32)
                    lnD = make_ln(stD1, "lD")
                    for t in range(8):
                        dma("sync", x1t[:], x1_d[t * 128:(t + 1) * 128, :],
                            [], [x1t], f"D_x{t}")
                        lnD(x1t, xn1b[:], [xn1b], f"D_ln{t}",
                            extra_f32=(xn1f[:], [xn1f]))
                        vop(lambda e: e.reduce_max(rmx[:, 0:1], xn1f[:],
                                                   axis=AX.X,
                                                   apply_absolute_value=True),
                            [xn1f], [rmx], tag=f"D_rm{t}")
                        vop(lambda e: e.reciprocal(rmx[:, 1:2], rmx[:, 0:1]),
                            [rmx], [rmx], tag=f"D_rr{t}")
                        vop(lambda e: e.tensor_scalar(rmx[:, 2:3], rmx[:, 1:2],
                                                      224.0, None, ALU.mult),
                            [rmx], [rmx], tag=f"D_rs{t}")
                        vop(lambda e: e.tensor_scalar(rmx[:, 3:4], rmx[:, 0:1],
                                                      1.0 / 224.0, None,
                                                      ALU.mult),
                            [rmx], [rmx], tag=f"D_rd{t}")
                        vop(lambda e: e.tensor_scalar(qarg[:], xn1f[:],
                                                      rmx[:, 2:3], None,
                                                      ALU.mult),
                            [xn1f, rmx], [qarg], tag=f"D_qa{t}")
                        sop(lambda e: e.copy(q8[:].bitcast(FP8), qarg[:]),
                            [qarg], [q8], f"D_q8{t}")
                        sop(lambda e: e.activation(qxsb[:], q8[:].bitcast(FP8),
                                                   AF.Identity, scale=rmx[:, 3:4]),
                            [q8, rmx], [qxsb], f"D_qs{t}")
                        for c in range(16):
                            transpose128(xn1b[:, c * 128:(c + 1) * 128], [xn1b],
                                         xn1T[:, c * T + t * 128:
                                              c * T + (t + 1) * 128],
                                         xn1T, f"D_t1{t}_{c}")
                            transpose128(qxsb[:, c * 128:(c + 1) * 128], [qxsb],
                                         qxsT[:, c * T + t * 128:
                                              c * T + (t + 1) * 128],
                                         qxsT, f"D_t2{t}_{c}")
                    dma("gpsimd", qxs_d, qxsT[:], [qxsT], [qxs_d], "D_qxspill")
                    P.barrier()
                    P.emit_block()

                stD2 = ExitStack()
                with stD2:
                    kT = sb(stD2, "kT", [128, 16 * T], BF16)
                    wst1 = [sb(stD2, f"wst1{i}", [128, 512], F32)
                            for i in range(2)]
                    wbf1 = [sb(stD2, f"wbf1{i}", [128, 512], BF16)
                            for i in range(16)]
                    bc1 = sb(stD2, "bc1", [128, 16], F32)
                    br1 = sb(stD2, "br1", [1, 128], F32)
                    vev = sb(stD2, "vev", [128, 512], BF16)

                    def l1proj(key, dst, act, tag, is_v=False):
                        wsrc = l1w_d[key]
                        bsrc = l1b_d[key]
                        for i in range(16):
                            dma("sync", br1[:], bsrc[i:i + 1, :], [], [br1],
                                f"{tag}bl{i}")
                            row_to_col(bc1[:, i:i + 1], bc1, br1[:], [br1],
                                       f"{tag}bc{i}")
                        for eb in range(4):
                            for c in range(16):
                                w_ = wst1[c % 2]
                                dma("sync", w_[:],
                                    wsrc[c * 128:(c + 1) * 128,
                                         eb * 512:(eb + 1) * 512],
                                    [], [w_], f"{tag}w{eb}_{c}")
                                vop(lambda e, cc=c, ww=w_: e.tensor_copy(
                                        wbf1[cc][:], ww[:]),
                                    [w_], [wbf1[c]], tag=f"{tag}v{eb}_{c}")
                            for e1 in range(4):
                                ec = eb * 4 + e1
                                for tk in range(2):
                                    bk = psA if (2 * ec + tk) % 2 == 0 else psB
                                    def mmp(e, ee=e1, tkk=tk, bk_=bk):
                                        for c in range(16):
                                            last = e.matmul(
                                                bk_[:],
                                                wbf1[c][:, ee * 128:
                                                        (ee + 1) * 128],
                                                xn1T[:, c * T + tkk * 512:
                                                     c * T + (tkk + 1) * 512],
                                                start=(c == 0), stop=(c == 15))
                                        return last
                                    top(mmp, wbf1 + [xn1T], [bk],
                                        f"{tag}m{ec}_{tk}")
                                    if is_v:
                                        sop(lambda e, ecc=ec, bk_=bk:
                                            e.activation(
                                                vev[:], bk_[:], AF.Identity,
                                                bias=bc1[:, ecc:ecc + 1]),
                                            [bk, bc1], [vev],
                                            f"{tag}e{ec}_{tk}")
                                        vop(lambda e, ecc=ec, tkk=tk:
                                            e.tensor_tensor(
                                                kvT[:, ecc * T + tkk * 512:
                                                    ecc * T + (tkk + 1) * 512],
                                                kT[:, ecc * T + tkk * 512:
                                                   ecc * T + (tkk + 1) * 512],
                                                vev[:], ALU.mult),
                                            [kT, vev], [kvT],
                                            tag=f"{tag}kv{ec}_{tk}")
                                    else:
                                        sop(lambda e, ecc=ec, tkk=tk, bk_=bk:
                                            e.activation(
                                                dst[:, ecc * T + tkk * 512:
                                                    ecc * T + (tkk + 1) * 512],
                                                bk_[:], act,
                                                bias=bc1[:, ecc:ecc + 1]),
                                            [bk, bc1], [dst],
                                            f"{tag}e{ec}_{tk}")

                    l1proj("q", qT, AF.Identity, "Dq")
                    l1proj("k", kT, AF.Identity, "Dk")
                    l1proj("v", None, AF.Identity, "Dv", is_v=True)
                    l1proj("g", gT, AF.Sigmoid, "Dg")
                    P.barrier()
                    P.emit_block()
                P.barrier()
                P.emit_block()

            stD3 = ExitStack()
            with stD3:
                ssc = sb(stD3, "ssc", [128, 16], F32)
                sscb = sb(stD3, "sscb", [128, 16], F32)
                for ec in range(16):
                    vop(lambda e, ecc=ec: e.reduce_sum(
                            ssc[:, ecc:ecc + 1],
                            kvT[:, ecc * T:(ecc + 1) * T], axis=AX.X),
                        [kvT], [ssc], tag=f"D_ss{ec}")
                vop(lambda e: e.tensor_copy(sscb[:], ssc[:]), [ssc], [sscb],
                    tag="D_sscp")
                top(lambda e: e.transpose(psV[0:16, 0:128], sscb[:], eye_f[:]),
                    [sscb, eye_f], [psV], "D_sst")
                ssrow = sb(stD3, "ssrow", [16, 128], F32)
                vop(lambda e: e.tensor_copy(ssrow[:], psV[0:16, 0:128]),
                    [psV], [ssrow], tag="D_ssv")
                dma("gpsimd",
                    scc_in.rearrange("o (a b) -> (o a) b", b=128),
                    ssrow[:], [ssrow], [scc_in], "D_ssd")
                def cc_s(e):
                    e.collective_compute(
                        "AllGather", ALU.bypass,
                        replica_groups=[[0, 1, 2, 3, 4, 5, 6, 7]],
                        ins=[scc_in.opt()], outs=[scc_out.opt()]).then_inc(
                            P.ccsems[1], 1)
                    e.wait_ge(P.ccsems[1], 1)
                    return e.nop()
                P.op("gpsimd", cc_s, reads=[scc_in], writes=[scc_out], inc=1,
                     tag="cc_s")
                gsb = sb(stD3, "gsb", [8, D], F32)
                pmk = sb(stD3, "pmk", [8, 1], F32)
                offr = sb(stD3, "offr", [1, D], F32)
                offc = sb(stD3, "offc", [128, 16], F32)
                dma("sync", gsb[:], scc_out, [scc_out], [gsb], "D_gl")
                dma("sync", pmk[:], pmask_d, [], [pmk], "D_pm")
                for j in range(4):
                    top(lambda e, jj=j: e.matmul(
                            psA[0:1, :], pmk[:],
                            gsb[:, jj * 512:(jj + 1) * 512],
                            start=True, stop=True),
                        [pmk, gsb], [psA], f"D_of{j}")
                    vop(lambda e, jj=j: e.tensor_copy(
                            offr[0:1, jj * 512:(jj + 1) * 512], psA[0:1, :]),
                        [psA], [offr], tag=f"D_oc{j}")
                for ec in range(16):
                    row_to_col(offc[:, ec:ec + 1], offc,
                               offr[0:1, ec * 128:(ec + 1) * 128], [offr],
                               f"D_ol{ec}")
                kvsT = sb(stD3, "kvsT", [128, 16 * T], BF16)
                for ec in range(16):
                    vop(lambda e, ecc=ec: e.tensor_tensor_scan(
                            kvsT[:, ecc * T:(ecc + 1) * T],
                            kvT[:, ecc * T:(ecc + 1) * T],
                            kvT[:, ecc * T:(ecc + 1) * T],
                            offc[:, ecc:ecc + 1], ALU.add, ALU.bypass),
                        [kvT, offc], [kvsT], tag=f"D_sc{ec}")
                s1f = sb(stD3, "s1f", [128, 16 * T], BF16)
                vop(lambda e: e.tensor_tensor(s1f[:], qT[:], kvsT[:], ALU.mult),
                    [qT, kvsT], [s1f], tag="D_s1a")
                vop(lambda e: e.tensor_tensor(kvsT[:], s1f[:], gT[:], ALU.mult),
                    [s1f, gT], [kvsT], tag="D_s1b")
                stok = sb(stD3, "stok", [128, D], BF16)
                for tc in range(8):
                    for ec in range(16):
                        transpose128(
                            kvsT[:, ec * T + tc * 128:ec * T + (tc + 1) * 128],
                            [kvsT], stok[:, ec * 128:(ec + 1) * 128], stok,
                            f"D_st{tc}_{ec}")
                    dma("gpsimd", s1_d[tc * 128:(tc + 1) * 128, :], stok[:],
                        [stok], [s1_d], f"D_so{tc}")
                P.barrier()
                P.emit_block()

        if True:
            stE1 = ExitStack()
            with stE1:
                qxsT = sb(stE1, "qxsT_e", [128, 16 * T], BF16)
                dma("sync", qxsT[:], qxs_d, [qxs_d], [qxsT], "E1_qxld")
                fst = [sb(stE1, f"fst{i}", [128, D], F32) for i in range(2)]
                farg = sb(stE1, "farg", [128, D], F32)
                f8 = sb(stE1, "f8", [128, D], U8)
                fqc = sb(stE1, "fqc", [128, D], BF16)
                rmw = sb(stE1, "rmw", [128, 4], F32)
                qwsu = sb(stE1, "qwsu", [128, 16 * 512], BF16)
                qwsg = sb(stE1, "qwsg", [128, 16 * 512], BF16)
                silE = sb(stE1, "silE", [128, 512], BF16)
                t0e = sb(stE1, "t0e", [128, 512], F32)
                rmt = sb(stE1, "rmt", [128, 16], F32)
                rmt2 = sb(stE1, "rmt2", [128, 8], F32)
                memset(rmt, 0.0, "E_rmt0")

                def quant_rows(wsrc, r0, dstws, ich, tag):
                    stg = fst[ich % 2]
                    dma("sync", stg[:], wsrc[r0:r0 + 128, :], [], [stg],
                        tag + "l")
                    vop(lambda e: e.reduce_max(rmw[:, 0:1], stg[:], axis=AX.X,
                                               apply_absolute_value=True),
                        [stg], [rmw], tag=tag + "m")
                    vop(lambda e: e.reciprocal(rmw[:, 1:2], rmw[:, 0:1]),
                        [rmw], [rmw], tag=tag + "r")
                    vop(lambda e: e.tensor_scalar(rmw[:, 2:3], rmw[:, 1:2],
                                                  224.0, None, ALU.mult),
                        [rmw], [rmw], tag=tag + "s")
                    vop(lambda e: e.tensor_scalar(rmw[:, 3:4], rmw[:, 0:1],
                                                  1.0 / 224.0, None, ALU.mult),
                        [rmw], [rmw], tag=tag + "d")
                    vop(lambda e: e.tensor_scalar(farg[:], stg[:], rmw[:, 2:3],
                                                  None, ALU.mult),
                        [stg, rmw], [farg], tag=tag + "a")
                    sop(lambda e: e.copy(f8[:].bitcast(FP8), farg[:]),
                        [farg], [f8], tag + "8")
                    sop(lambda e: e.activation(fqc[:], f8[:].bitcast(FP8),
                                               AF.Identity, scale=rmw[:, 3:4]),
                        [f8, rmw], [fqc], tag + "q")
                    for c in range(16):
                        transpose128(fqc[:, c * 128:(c + 1) * 128], [fqc],
                                     dstws[:, c * 512 + ich * 128:
                                           c * 512 + (ich + 1) * 128],
                                     dstws, f"{tag}t{c}")

                for isl in range(NSL):
                    for ich in range(4):
                        quant_rows(fwu_d, isl * 512 + ich * 128, qwsu, ich,
                                   f"E1u{isl}_{ich}")
                        quant_rows(fwg_d, isl * 512 + ich * 128, qwsg, ich,
                                   f"E1g{isl}_{ich}")
                    for tc in range(8):
                        def mmug(e, tcc=tc):
                            for c in range(16):
                                e.matmul(psE[:],
                                         qxsT[:, c * T + tcc * 128:
                                              c * T + (tcc + 1) * 128],
                                         qwsu[:, c * 512:(c + 1) * 512],
                                         start=(c == 0), stop=(c == 15))
                            for c in range(16):
                                last = e.matmul(
                                    psF[:],
                                    qxsT[:, c * T + tcc * 128:
                                         c * T + (tcc + 1) * 128],
                                    qwsg[:, c * 512:(c + 1) * 512],
                                    start=(c == 0), stop=(c == 15))
                            return last
                        top(mmug, [qxsT, qwsu, qwsg], [psE, psF],
                            f"E1m{isl}_{tc}")
                        sop(lambda e: e.activation(silE[:], psF[:], AF.Silu),
                            [psF], [silE], f"E1s{isl}_{tc}")
                        vop(lambda e: e.tensor_tensor(t0e[:], silE[:], psE[:],
                                                      ALU.mult),
                            [silE, psE], [t0e], tag=f"E1t{isl}_{tc}")
                        dma("gpsimd",
                            t0_d[tc * 128:(tc + 1) * 128,
                                 isl * 512:(isl + 1) * 512],
                            t0e[:], [t0e], [t0_d], f"E1d{isl}_{tc}")
                        vop(lambda e, tcc=tc: e.reduce_max(
                                rmt2[:, tcc:tcc + 1], t0e[:], axis=AX.X,
                                apply_absolute_value=True),
                            [t0e], [rmt2], tag=f"E1r{isl}_{tc}")
                        vop(lambda e, tcc=tc, ii=isl: e.tensor_tensor(
                                rmt[:, 2 * tcc + (ii + 1) % 2:
                                    2 * tcc + (ii + 1) % 2 + 1],
                                rmt[:, 2 * tcc + ii % 2:2 * tcc + ii % 2 + 1],
                                rmt2[:, tcc:tcc + 1], ALU.max),
                            [rmt, rmt2], [rmt], tag=f"E1x{isl}_{tc}")
                P.barrier()
                P.emit_block()

                stE2 = ExitStack()
                with stE2:
                    t0r = [sb(stE2, f"t0r{i}", [128, DFF1], F32) for i in range(2)]
                    targ = sb(stE2, "targ", [128, DFF1], F32)
                    t8 = sb(stE2, "t8", [128, DFF1], U8)
                    tqs = sb(stE2, "tqs", [128, DFF1], BF16)
                    rmtc = sb(stE2, "rmtc", [128, 4], F32)
                    qrow = sb(stE2, "qrow", [128, DFF1], BF16)
                    for tc in range(8):
                        tr = t0r[tc % 2]
                        dma("sync", tr[:], t0_d[tc * 128:(tc + 1) * 128, :],
                            [], [tr], f"E2l{tc}")
                        vop(lambda e, tcc=tc: e.reciprocal(
                                rmtc[:, 0:1], rmt[:, 2 * tcc + 1:2 * tcc + 2]),
                            [rmt], [rmtc], tag=f"E2r{tc}")
                        vop(lambda e: e.tensor_scalar(rmtc[:, 2:3],
                                                      rmtc[:, 0:1], 224.0,
                                                      None, ALU.mult),
                            [rmtc], [rmtc], tag=f"E2s{tc}")
                        vop(lambda e, tcc=tc: e.tensor_scalar(
                                rmtc[:, 1:2], rmt[:, 2 * tcc + 1:2 * tcc + 2],
                                1.0 / 224.0, None, ALU.mult),
                            [rmt, rmtc], [rmtc], tag=f"E2d{tc}")
                        vop(lambda e, tr_=tr: e.tensor_scalar(
                                targ[:], tr_[:], rmtc[:, 2:3], None, ALU.mult),
                            [tr, rmtc], [targ], tag=f"E2a{tc}")
                        sop(lambda e: e.copy(t8[:].bitcast(FP8), targ[:]),
                            [targ], [t8], f"E28{tc}")
                        sop(lambda e: e.activation(tqs[:], t8[:].bitcast(FP8),
                                                   AF.Identity, scale=rmtc[:, 1:2]),
                            [t8, rmtc], [tqs], f"E2q{tc}")
                        for ich in range(NI):
                            transpose128(tqs[:, ich * 128:(ich + 1) * 128],
                                         [tqs],
                                         qrow[:, ich * 128:(ich + 1) * 128],
                                         qrow, f"E2t{tc}_{ich}")
                        dma("gpsimd",
                            qts_d[:, tc * 128:(tc + 1) * 128]
                            .rearrange("(a p) t -> p a t", p=128),
                            qrow[:].rearrange("p (a t) -> p a t", t=128),
                            [qrow], [qts_d], f"E2o{tc}")
                    P.barrier()
                    P.emit_block()
            P.barrier()
            P.emit_block()

        stE3 = ExitStack()
        with stE3:
            qt1T = sb(stE3, "qt1T", [128, NI * 128 * 8], BF16)
            dix = sb(stE3, "dix", [128, 1], I32)
            for ich in range(NI):
                dma("gpsimd", dix[:],
                    didx_d[ich * 128:(ich + 1) * 128, 0:1],
                    [], [dix], f"E3i{ich}")
                P.op("gpsimd",
                     lambda e, ii=ich: e.indirect_dma_start(
                         qt1T[:, ii * T:(ii + 1) * T], None, qts_d,
                         bass.IndirectOffsetOnAxis(dix[:], 0)),
                     reads=[qts_d, dix], writes=[qt1T], inc=16, tag=f"E3g{ich}")

            fdst = sb(stE3, "fdst", [128, DFF1], F32)
            fdarg = sb(stE3, "fdarg", [128, DFF1], F32)
            fd8 = sb(stE3, "fd8", [128, DFF1], U8)
            fdqc = sb(stE3, "fdqc", [128, DFF1], BF16)
            rmd = sb(stE3, "rmd", [128, 4], F32)
            qdsT = sb(stE3, "qdsT", [128, NI * 512], BF16)
            g1e = sb(stE3, "g1e", [128, 512], BF16)
            for dg in range(4):
                for dc in range(4):
                    r0 = dg * 512 + dc * 128
                    dma("sync", fdst[:], fwd_d[r0:r0 + 128, :], [], [fdst],
                        f"E3l{dg}_{dc}")
                    vop(lambda e: e.reduce_max(rmd[:, 0:1], fdst[:], axis=AX.X,
                                               apply_absolute_value=True),
                        [fdst], [rmd], tag=f"E3m{dg}_{dc}")
                    vop(lambda e: e.reciprocal(rmd[:, 1:2], rmd[:, 0:1]),
                        [rmd], [rmd], tag=f"E3r{dg}_{dc}")
                    vop(lambda e: e.tensor_scalar(rmd[:, 2:3], rmd[:, 1:2],
                                                  224.0, None, ALU.mult),
                        [rmd], [rmd], tag=f"E3s{dg}_{dc}")
                    vop(lambda e: e.tensor_scalar(rmd[:, 3:4], rmd[:, 0:1],
                                                  1.0 / 224.0, None, ALU.mult),
                        [rmd], [rmd], tag=f"E3d{dg}_{dc}")
                    vop(lambda e: e.tensor_scalar(fdarg[:], fdst[:],
                                                  rmd[:, 2:3], None, ALU.mult),
                        [fdst, rmd], [fdarg], tag=f"E3a{dg}_{dc}")
                    sop(lambda e: e.copy(fd8[:].bitcast(FP8), fdarg[:]),
                        [fdarg], [fd8], f"E38{dg}_{dc}")
                    sop(lambda e: e.activation(fdqc[:], fd8[:].bitcast(FP8),
                                               AF.Identity, scale=rmd[:, 3:4]),
                        [fd8, rmd], [fdqc], f"E3q{dg}_{dc}")
                    for ich in range(NI):
                        transpose128(fdqc[:, ich * 128:(ich + 1) * 128], [fdqc],
                                     qdsT[:, ich * 512 + dc * 128:
                                          ich * 512 + (dc + 1) * 128],
                                     qdsT, f"E3t{dg}_{dc}_{ich}")
                for tc in range(8):
                    bk = psA if tc % 2 == 0 else psB
                    def mmdn(e, tcc=tc, bk_=bk):
                        for ich in range(NI):
                            last = e.matmul(
                                bk_[:],
                                qt1T[:, ich * T + tcc * 128:
                                     ich * T + (tcc + 1) * 128],
                                qdsT[:, ich * 512:(ich + 1) * 512],
                                start=(ich == 0), stop=(ich == NI - 1))
                        return last
                    top(mmdn, [qt1T, qdsT], [bk], f"E3dm{dg}_{tc}")
                    sop(lambda e, bk_=bk: e.activation(g1e[:], bk_[:], AF.Silu),
                        [bk], [g1e], f"E3de{dg}_{tc}")
                    dma("gpsimd",
                        g1f_d[tc * 128:(tc + 1) * 128, dg * 512:(dg + 1) * 512],
                        g1e[:], [g1e], [g1f_d], f"E3do{dg}_{tc}")
            P.barrier()
            P.emit_block()

        if DEBUG:
            zin_dbg = nc.dram_tensor("zin_dbg", [512, S], BF16,
                                     kind="ExternalOutput").ap()
            zout_dbg = nc.dram_tensor("zout_dbg", [4096, S], BF16,
                                      kind="ExternalOutput").ap()
            sccdbg = nc.dram_tensor("scc_dbg", [8, D], F32,
                                    kind="ExternalOutput").ap()
            dma("gpsimd", zin_dbg, zcc_in, [zcc_in], [zin_dbg], "dbg_zi")
            dma("gpsimd", zout_dbg, zcc_out, [zcc_out], [zout_dbg], "dbg_zo")
            dma("gpsimd", sccdbg, scc_out, [scc_out], [sccdbg], "dbg_sc")
        stF = ExitStack()
        with stF:
            fx = sb(stF, "fx", [128, D], F32)
            fs = sb(stF, "fs", [128, D], BF16)
            fg = sb(stF, "fg", [128, D], BF16)
            fo = sb(stF, "fo", [128, D], F32)
            fo2 = sb(stF, "fo2", [128, D], F32)
            for tc in range(8):
                dma("sync", fx[:], x1_d[tc * 128:(tc + 1) * 128, :], [], [fx],
                    f"F_x{tc}")
                dma("sync", fs[:], s1_d[tc * 128:(tc + 1) * 128, :], [], [fs],
                    f"F_s{tc}")
                dma("sync", fg[:], g1f_d[tc * 128:(tc + 1) * 128, :], [], [fg],
                    f"F_g{tc}")
                vop(lambda e: e.tensor_tensor(fo[:], fs[:], fg[:], ALU.mult),
                    [fs, fg], [fo], tag=f"F_m{tc}")
                vop(lambda e: e.tensor_tensor(fo2[:], fx[:], fo[:], ALU.add),
                    [fx, fo], [fo2], tag=f"F_a{tc}")
                dma("gpsimd", out_d[tc * 128:(tc + 1) * 128, :], fo2[:],
                    [fo2], [out_d], f"F_o{tc}")
            snap = dict(P.counts)

            def fence(e):
                for f in Prog.ENGINES:
                    if f != "sync" and snap[f] > 0:
                        e.wait_ge(P.sems[f], snap[f])
                return e.nop()
            P.op("sync", fence, reads=[out_d], writes=[], inc=1, tag="fence")
            P.emit_block()
    return nc


_NC_CACHE = []
_LAST = [None]


def kernel(**inputs):
    gi = inputs.__getitem__
    x = np.asarray(gi("L0_attn_x"))
    _ = gi("L1_ffn_up_reorder_index")
    WQ = np.asarray(gi("L0_attn_W_Q")); WK = np.asarray(gi("L0_attn_W_K"))
    WV = np.asarray(gi("L0_attn_W_V")); WO = np.asarray(gi("L0_attn_W_O"))
    bQ = np.asarray(gi("L0_attn_b_Q")); bK = np.asarray(gi("L0_attn_b_K"))
    bV = np.asarray(gi("L0_attn_b_V")); bO = np.asarray(gi("L0_attn_b_O"))
    WU = np.asarray(gi("L0_ffn_W_up")); WG = np.asarray(gi("L0_ffn_W_gate"))
    WD = np.asarray(gi("L0_ffn_W_down"))
    l1 = {k: np.asarray(gi(f"L1_attn_W{k}")) for k in "qkvg"}
    l1b = {k: np.asarray(gi(f"L1_attn_b{k}")) for k in "qkvg"}
    fg = np.asarray(gi("L1_ffn_w_gate")); fu = np.asarray(gi("L1_ffn_w_up"))
    fd = np.asarray(gi("L1_ffn_w_down"))
    didx = np.asarray(gi("L1_ffn_down_reorder_index")).astype(np.int32)

    wut = np.ascontiguousarray(WU.T)
    wgt = np.ascontiguousarray(WG.T)
    wdt = np.ascontiguousarray(WD.T)
    l1t = {k: np.ascontiguousarray(l1[k].T) for k in "qkvg"}
    eye = np.eye(128, dtype=np.float32)
    hh = np.arange(896)[None, :]
    pp = np.arange(128)[:, None]
    maskw = np.where(hh >= pp + 384, 0.0, MASKV).astype(ml_dtypes.bfloat16)

    if not _NC_CACHE:
        _NC_CACHE.append(build_program())
    nc = _NC_CACHE[0]

    in_maps = []
    for c in range(8):
        Bb, Sl = c // 4, c % 4
        heads = [4 * Sl + j for j in range(4)]
        zr = np.empty((NH, 128), np.int32)
        for h in range(NH):
            r = 4 * Bb + h // 4
            orig = r * 512 + (h % 4) * 128 + np.arange(128)
            zr[h] = orig * 4 + Sl
        pm = np.zeros((8, 1), np.float32)
        for r in range(8):
            if r // 4 == Bb and r % 4 < Sl:
                pm[r, 0] = 1.0
        m = {
            "xb": x[Bb],
            "xown": x[Bb, Sl * T:(Sl + 1) * T],
            "wq": WQ[heads], "wk": WK[heads], "wv": WV[heads],
            "bq": bQ[heads], "bk": bK[heads], "bv": bV[heads],
            "wo": WO, "bo": bO.reshape(16, 128),
            "wut": wut, "wgt": wgt, "wdt": wdt,
            "l1q": l1t["q"], "l1k": l1t["k"], "l1v": l1t["v"], "l1g": l1t["g"],
            "l1bq": l1b["q"].reshape(16, 128),
            "l1bk": l1b["k"].reshape(16, 128),
            "l1bv": l1b["v"].reshape(16, 128),
            "l1bg": l1b["g"].reshape(16, 128),
            "fwg": fg, "fwu": fu, "fwd": fd,
            "didx": didx.reshape(NI * 128, 1),
            "zridx": zr.reshape(NH * 128, 1), "pmask": pm,
            "maskw": maskw, "eye": eye,
        }
        in_maps.append({k: np.ascontiguousarray(v) for k, v in m.items()})

    res = run_bass_kernel_spmd(nc, in_maps, core_ids=list(range(8)))
    _LAST[0] = res
    out = np.empty((B, S, D), np.float32)
    for c in range(8):
        Bb, Sl = c // 4, c % 4
        out[Bb, Sl * T:(Sl + 1) * T] = np.asarray(res.results[c]["out"])
    return out


# version 13
# speedup vs baseline: 1.3819x; 1.0121x over previous
import numpy as np
import ml_dtypes
import concourse.bass as bass
import concourse.mybir as mybir
from concourse.bass_utils import run_bass_kernel_spmd

F32 = mybir.dt.float32
BF16 = mybir.dt.bfloat16
FP8 = mybir.dt.float8e4
U8 = mybir.dt.uint8
I32 = mybir.dt.int32
AF = mybir.ActivationFunctionType
ALU = mybir.AluOpType
AX = mybir.AxisListType

B, S, D = 2, 4096, 2048
NH, DH = 16, 128
DFF0 = 8192
DFF1 = 5632
NI = DFF1 // 128
NSL = DFF1 // 512
T = 1024
RS = float(1.0 / np.sqrt(128.0))
MASKV = -1131371.0
EPS_LN = 1e-5
DEBUG = False


class Prog:
    ENGINES = ("sync", "scalar", "vector", "tensor", "gpsimd")

    def __init__(self, nc):
        self.nc = nc
        self.ops = []
        self.last_writer = {}
        self.readers = {}
        self.counts = {e: 0 for e in self.ENGINES}
        self.emitted = 0
        self.sems = {}
        self._barriers = []
        self._wm = {e: {f: 0 for f in self.ENGINES} for e in self.ENGINES}

    def init_sems(self, stack):
        for e in self.ENGINES:
            self.sems[e] = stack.enter_context(self.nc.semaphore(f"sem_{e}"))
        self.ccsems = [stack.enter_context(self.nc.semaphore(f"ccsem{i}"))
                       for i in range(2)]

    def op(self, eng, fn, reads=(), writes=(), inc=1, tag=""):
        idx = len(self.ops)
        deps = set()
        for b in reads:
            w = self.last_writer.get(id(b))
            if w is not None:
                deps.add(w)
        for b in writes:
            w = self.last_writer.get(id(b))
            if w is not None:
                deps.add(w)
            for r in self.readers.get(id(b), ()):
                deps.add(r)
        deps.discard(idx)
        self.counts[eng] += inc
        self.ops.append(dict(eng=eng, fn=fn, deps=deps, inc=inc, idx=idx,
                             tag=tag, count=self.counts[eng]))
        for b in reads:
            self.readers.setdefault(id(b), []).append(idx)
        for b in writes:
            self.last_writer[id(b)] = idx
            self.readers[id(b)] = [idx]
        return idx

    def barrier(self):
        self._barriers.append((len(self.ops), dict(self.counts)))

    def emit_block(self):
        start, end = self.emitted, len(self.ops)
        if start == end:
            return
        ops = self.ops
        per_eng = {e: [i for i in range(start, end) if ops[i]["eng"] == e]
                   for e in self.ENGINES}
        wm = self._wm
        sems = self.sems
        barriers = self._barriers

        def run_engine(engobj, ename):
            w = wm[ename]
            for i in per_eng[ename]:
                rec = ops[i]
                for bpos, bsnap in barriers:
                    if bpos <= i:
                        for f in self.ENGINES:
                            if f == ename:
                                w[f] = max(w[f], bsnap[f])
                            elif bsnap[f] > w[f]:
                                engobj.wait_ge(sems[f], bsnap[f])
                                w[f] = bsnap[f]
                need = {}
                for d in rec["deps"]:
                    dr = ops[d]
                    f = dr["eng"]
                    if f == ename == "tensor":
                        continue
                    if dr["count"] > w[f]:
                        need[f] = max(need.get(f, 0), dr["count"])
                for f in sorted(need):
                    engobj.wait_ge(sems[f], need[f])
                    w[f] = need[f]
                instr = rec["fn"](engobj)
                if instr is None:
                    raise RuntimeError(f"op {rec['tag']} returned None")
                instr.then_inc(sems[ename], rec["inc"])

        with self.nc.Block() as block:
            for ename in self.ENGINES:
                if per_eng[ename]:
                    getattr(block, ename)(
                        lambda engobj, _e=ename: run_engine(engobj, _e))
        self.emitted = end


def build_program():
    from contextlib import ExitStack
    nc = bass.Bass("TRN2", target_bir_lowering=False, num_devices=8)

    def din(name, shape, dt=F32):
        return nc.dram_tensor(name, shape, dt, kind="ExternalInput").ap()

    def dint(name, shape, dt=F32, dbg=False):
        kind = "ExternalOutput" if (dbg and DEBUG) else "Internal"
        return nc.dram_tensor(name, shape, dt, kind=kind).ap()

    xb_d = din("xb", [S, D])
    xown_d = din("xown", [T, D])
    wq_d = din("wq", [4, D, DH]); wk_d = din("wk", [4, D, DH]); wv_d = din("wv", [4, D, DH])
    bq_d = din("bq", [4, DH]); bk_d = din("bk", [4, DH]); bv_d = din("bv", [4, DH])
    wo_d = din("wo", [NH, DH, D])
    bo_d = din("bo", [16, 128])
    wut_d = din("wut", [D, DFF0]); wgt_d = din("wgt", [D, DFF0]); wdt_d = din("wdt", [DFF0, D])
    l1w_d = {k: din(f"l1{k}", [D, D]) for k in "qkvg"}
    l1b_d = {k: din(f"l1b{k}", [16, 128]) for k in "qkvg"}
    fwg_d = din("fwg", [DFF1, D]); fwu_d = din("fwu", [DFF1, D]); fwd_d = din("fwd", [D, DFF1])
    didx_d = din("didx", [NI * 128, 1], I32)
    zridx_d = din("zridx", [NH * 128, 1], I32)
    pmask_d = din("pmask", [8, 1])
    maskw_d = din("maskw", [128, 896], BF16)
    eye_d = din("eye", [128, 128])
    out_d = nc.dram_tensor("out", [T, D], F32, kind="ExternalOutput").ap()

    zcc_in = dint("zcc_in", [512, S], BF16)
    zcc_out = dint("zcc_out", [4096, S], BF16)
    scc_in = dint("scc_in", [1, D])
    scc_out = dint("scc_out", [8, D])
    offs_dram = dint("offs_dram", [1, D])
    s0_d = dint("s0_spill", [T, D], BF16, dbg=True)
    g0_d = dint("g0_spill", [T, D], BF16, dbg=True)
    x1_d = dint("x1_spill", [T, D], dbg=True)
    s1_d = dint("s1_spill", [T, D], BF16, dbg=True)
    t0_d = dint("t0_spill", [T, DFF1], dbg=True)
    ternd_d = dint("ternd_spill", [DFF0, D], BF16, dbg=True)
    qts_d = dint("qts_spill", [DFF1, T], BF16, dbg=True)
    qxs_d = dint("qxs_spill", [128, 16 * T], BF16, dbg=True)
    g1f_d = dint("g1f_spill", [T, D], BF16, dbg=True)
    if DEBUG:
        h_dbg = nc.dram_tensor("h_dbg", [DFF0, T], BF16,
                               kind="ExternalOutput").ap()

    P = Prog(nc)
    root = ExitStack()

    def sb(stk, name, shape, dt):
        return stk.enter_context(nc.sbuf_tensor(name, shape, dt))

    with root:
        P.init_sems(root)
        eye_f = sb(root, "eye_f", [128, 128], F32)
        eye_b = sb(root, "eye_b", [128, 128], BF16)
        ones_col_b = sb(root, "ones_col_b", [128, 1], BF16)
        ones_row_b = sb(root, "ones_row_b", [1, 128], BF16)
        ones_row_f = sb(root, "ones_row_f", [1, 128], F32)
        ones_col_f = sb(root, "ones_col_f", [128, 1], F32)
        one11_f = sb(root, "one11_f", [1, 1], F32)
        epsc = sb(root, "epsc", [128, 1], F32)
        thU = sb(root, "thU", [128, 3], F32)
        thG = sb(root, "thG", [128, 3], F32)
        thD = sb(root, "thD", [128, 3], F32)
        psA = root.enter_context(nc.psum_tensor("psA", [128, 512], F32))
        psB = root.enter_context(nc.psum_tensor("psB", [128, 512], F32))
        psE = root.enter_context(nc.psum_tensor("psE", [128, 512], F32))
        psF = root.enter_context(nc.psum_tensor("psF", [128, 512], F32))
        psTs = [root.enter_context(nc.psum_tensor(f"psT{i}", [128, 128], BF16))
                for i in range(3)]
        psV = root.enter_context(nc.psum_tensor("psV", [128, 256], F32))
        tp_ctr = [0]

        def dma(eng, dst_ap, src_ap, reads, writes, tag=""):
            P.op(eng, lambda e, d=dst_ap, s=src_ap: e.dma_start(out=d, in_=s),
                 reads=reads, writes=writes, inc=16, tag=tag)

        def vop(fn, reads, writes, eng="vector", tag=""):
            P.op(eng, fn, reads=reads, writes=writes, inc=1, tag=tag)

        def sop(fn, reads, writes, tag=""):
            P.op("scalar", fn, reads=reads, writes=writes, inc=1, tag=tag)

        def top(fn, reads, writes, tag=""):
            P.op("tensor", fn, reads=reads, writes=writes, inc=1, tag=tag)

        def memset(t, val, tag="ms"):
            vop(lambda e, tt=t, v=val: e.memset(tt[:], v), [], [t], tag=tag)

        def row_to_col(dst_ap, dst_buf, row_ap, row_bufs, tag):
            top(lambda e, r=row_ap: e.matmul(psV[:, 0:1], r, one11_f[:],
                                             start=True, stop=True),
                list(row_bufs) + [one11_f], [psV], tag + "_mm")
            vop(lambda e, d=dst_ap: e.tensor_copy(d, psV[:, 0:1]),
                [psV], [dst_buf], tag=tag + "_cp")

        zsb = sb(root, "zsb", [128, 512], BF16)
        zsf = sb(root, "zsf", [128, 128], F32)
        memset(zsb, 0.0, "zsb0")
        memset(zsf, 0.0, "zsf0")
        dma("sync", eye_f[:], eye_d, [], [eye_f], "ld_eye")
        vop(lambda e: e.tensor_copy(eye_b[:], eye_f[:]), [eye_f], [eye_b], tag="eyeb")
        memset(ones_col_b, 1.0, "o1"); memset(ones_row_b, 1.0, "o2")
        memset(ones_row_f, 1.0, "o3"); memset(ones_col_f, 1.0, "o4")
        memset(one11_f, 1.0, "o5")
        memset(epsc, EPS_LN, "o6")

        def make_ln(stk, pref):
            sq = sb(stk, pref + "sq", [128, D], BF16)
            st_ = sb(stk, pref + "st", [128, 8], F32)
            nm = sb(stk, pref + "nm", [128, 1], F32)
            rs_ = sb(stk, pref + "rs", [128, 1], F32)

            def ln(xtile, out_bf_ap, out_bufs, tag, extra_f32=None):
                vop(lambda e: e.tensor_scalar(sq[:], xtile[:], 1.0, None,
                                              ALU.mult, ALU.add,
                                              accum_out=st_[:, 0:1]),
                    [xtile], [sq, st_], tag=tag + "_s")
                sop(lambda e: e.activation(sq[:], xtile[:], AF.Square,
                                           accum_out=st_[:, 1:2]),
                    [xtile], [sq, st_], tag + "_q")
                vop(lambda e: e.tensor_scalar(st_[:, 2:3], st_[:, 0:1],
                                              1.0 / D, None, ALU.mult),
                    [st_], [st_], tag=tag + "_m")
                vop(lambda e: e.tensor_tensor(st_[:, 3:4], st_[:, 2:3],
                                              st_[:, 2:3], ALU.mult),
                    [st_], [st_], tag=tag + "_m2")
                vop(lambda e: e.tensor_scalar(st_[:, 4:5], st_[:, 1:2],
                                              1.0 / D, None, ALU.mult),
                    [st_], [st_], tag=tag + "_e2")
                vop(lambda e: e.tensor_tensor(st_[:, 5:6], st_[:, 4:5],
                                              st_[:, 3:4], ALU.subtract),
                    [st_], [st_], tag=tag + "_v")
                sop(lambda e: e.activation(st_[:, 6:7], st_[:, 5:6], AF.Sqrt,
                                           bias=epsc[:, 0:1]),
                    [st_, epsc], [st_], tag + "_sd")
                vop(lambda e: e.reciprocal(rs_[:], st_[:, 6:7]),
                    [st_], [rs_], tag=tag + "_r")
                vop(lambda e: e.tensor_scalar(nm[:], st_[:, 2:3], -1.0, None,
                                              ALU.mult),
                    [st_], [nm], tag=tag + "_n")
                vop(lambda e: e.tensor_scalar(out_bf_ap, xtile[:], nm[:], rs_[:],
                                              ALU.add, ALU.mult),
                    [xtile, nm, rs_], out_bufs, tag=tag + "_x")
                if extra_f32 is not None:
                    ap_, bufs_ = extra_f32
                    vop(lambda e: e.tensor_scalar(ap_, xtile[:], nm[:], rs_[:],
                                                  ALU.add, ALU.mult),
                        [xtile, nm, rs_], bufs_, tag=tag + "_xf")
            return ln

        def transpose128(src_ap, src_bufs, dst_ap, dst_buf, tag):
            pt = psTs[tp_ctr[0] % 3]
            tp_ctr[0] += 1
            top(lambda e, s=src_ap, p=pt: e.transpose(p[:], s, eye_b[:]),
                list(src_bufs) + [eye_b], [pt], tag + "_t")
            vop(lambda e, d=dst_ap, p=pt: e.tensor_copy(d, p[:]),
                [pt], [dst_buf], tag=tag + "_c")

        stA = ExitStack()
        with stA:
            xnT = [sb(stA, f"xnT{i}", [128, S], BF16) for i in range(16)]
            xt = [sb(stA, f"xtA{i}", [128, D], F32) for i in range(2)]
            xnb = sb(stA, "xnb", [128, D], BF16)
            lnA = make_ln(stA, "lA")

            for t in range(32):
                xi = xt[t % 2]
                dma("sync", xi[:], xb_d[t * 128:(t + 1) * 128, :], [], [xi],
                    f"A_x{t}")
                lnA(xi, xnb[:], [xnb], f"A_ln{t}")
                for c in range(16):
                    transpose128(xnb[:, c * 128:(c + 1) * 128], [xnb],
                                 xnT[c][:, t * 128:(t + 1) * 128], xnT[c],
                                 f"A_tp{t}_{c}")

            zvw = zcc_out.rearrange("(a p) (b c) -> a b p c", p=128, c=512)
            for a in range(32):
                for b in range(8):
                    dma("sync", zvw[a, b], zsb[:], [zsb], [zcc_out], f"zz{a}_{b}")
            dma("sync",
                scc_out.rearrange("a b -> (a b)").rearrange("(p c) -> p c", p=128),
                zsf[:], [zsf], [scc_out], "zzs")
            accw = sb(stA, "accw", [128, 16], F32)
            wmt = [sb(stA, f"wmt{i}", [128, 512], F32) for i in range(2)]
            wabs = sb(stA, "wabs", [128, 512], BF16)
            wsum = sb(stA, "wsum", [1, 4], F32)
            memset(accw, 0.0, "accw0")

            def mean_pass(wsrc, nrow, ncol, base, tag):
                k = 0
                ntile = (nrow // 128) * (ncol // 512)
                for r in range(nrow // 128):
                    for c in range(ncol // 512):
                        wt = wmt[k % 2]
                        dma("gpsimd", wt[:],
                            wsrc[r * 128:(r + 1) * 128, c * 512:(c + 1) * 512],
                            [], [wt], f"{tag}l{k}")
                        sop(lambda e, w=wt, kk=k: e.activation(
                                wabs[:], w[:], AF.Abs,
                                accum_out=accw[:, base + (kk % 2):base + (kk % 2) + 1]),
                            [wt], [wabs, accw], f"{tag}a{k}")
                        rd = base + 2 + (k % 2)
                        wr = base + 2 + ((k + 1) % 2)
                        vop(lambda e, kk=k, rd_=rd, wr_=wr: e.tensor_tensor(
                                accw[:, wr_:wr_ + 1],
                                accw[:, rd_:rd_ + 1],
                                accw[:, base + (kk % 2):base + (kk % 2) + 1],
                                ALU.add),
                            [accw], [accw], tag=f"{tag}c{k}")
                        k += 1
                return base + 2 + (ntile % 2)

            def mean_finish(fcol, th_t, idx, tag):
                top(lambda e: e.matmul(psV[0:1, 4:5],
                                       accw[:, fcol:fcol + 1],
                                       ones_col_f[:], start=True, stop=True),
                    [accw, ones_col_f], [psV], tag + "_r")
                vop(lambda e: e.tensor_scalar(wsum[0:1, idx:idx + 1],
                                              psV[0:1, 4:5],
                                              1.0 / 16777216.0, 1e-8,
                                              ALU.mult, ALU.add),
                    [psV], [wsum], tag=tag + "_t")
                top(lambda e: e.matmul(psV[:, 8:9], ones_row_f[:],
                                       wsum[0:1, idx:idx + 1],
                                       start=True, stop=True),
                    [ones_row_f, wsum], [psV], tag + "_b")
                vop(lambda e: e.tensor_copy(th_t[:, 2:3], psV[:, 8:9]),
                    [psV], [th_t], tag=tag + "_f")
                vop(lambda e: e.tensor_scalar(th_t[:, 0:1], psV[:, 8:9], 0.5,
                                              None, ALU.mult),
                    [psV], [th_t], tag=tag + "_p")
                vop(lambda e: e.tensor_scalar(th_t[:, 1:2], psV[:, 8:9], -0.5,
                                              None, ALU.mult),
                    [psV], [th_t], tag=tag + "_n")

            fc = mean_pass(wut_d, D, DFF0, 0, "mu")
            mean_finish(fc, thU, 0, "mu")
            fc = mean_pass(wgt_d, D, DFF0, 5, "mg")
            mean_finish(fc, thG, 1, "mg")
            fc = mean_pass(wdt_d, DFF0, D, 10, "md")
            mean_finish(fc, thD, 2, "md")

            maskw = sb(stA, "maskw_s", [128, 896], BF16)
            dma("sync", maskw[:], maskw_d, [], [maskw], "A_mask")
            wstg = sb(stA, "wstg", [128, 1024], F32)
            wbf = sb(stA, "wbf", [128, D], BF16)
            KT = sb(stA, "KT", [128, S], BF16)
            QT = sb(stA, "QT", [128, S], BF16)
            Vtok = sb(stA, "Vtok", [128, S], BF16)
            vtmp = sb(stA, "vtmp", [128, 512], BF16)
            sqb = sb(stA, "sqb", [128, 512], BF16)
            rsc = sb(stA, "rsc", [1, 512], F32)
            mks = sb(stA, "mks", [1, 4], F32)
            negM = sb(stA, "negM", [1, 512], BF16)
            aT = [sb(stA, f"aT{i}", [128, 512], BF16) for i in range(2)]
            rb = sb(stA, "rb", [128, 512], F32)
            ztile = sb(stA, "ztile", [128, 512], BF16)
            bcol = sb(stA, "bcol", [128, 3], F32)
            brow = sb(stA, "brow", [1, 128], F32)

            def bias_col(bsrc, p, col, tag):
                dma("sync", brow[:], bsrc[p:p + 1, :], [], [brow], tag + "_l")
                row_to_col(bcol[:, col:col + 1], bcol, brow[:], [brow], tag)

            def proj(wsrc, p, dst, bias_c, tag):
                wrs = wsrc[p].rearrange("(c p) e -> p c e", p=128)
                wsv = wstg[:].rearrange("p (c e) -> p c e", e=128)
                for hf in range(2):
                    dma("sync", wsv, wrs[:, hf * 8:(hf + 1) * 8, :],
                        [], [wstg], tag + f"_w{hf}")
                    vop(lambda e, h_=hf: e.tensor_copy(
                            wbf[:, h_ * 1024:(h_ + 1) * 1024], wstg[:]),
                        [wstg], [wbf], tag=tag + f"_v{hf}")
                for tk in range(8):
                    bk = psA if tk % 2 == 0 else psF
                    def mm(e, tkk=tk, bk_=bk):
                        for c in range(16):
                            last = e.matmul(
                                bk_[:], wbf[:, c * 128:(c + 1) * 128],
                                xnT[c][:, tkk * 512:(tkk + 1) * 512],
                                start=(c == 0), stop=(c == 15))
                        return last
                    top(mm, [wbf] + xnT, [bk], f"{tag}m{tk}")
                    sop(lambda e, tkk=tk, bk_=bk: e.activation(
                            dst[:, tkk * 512:(tkk + 1) * 512], bk_[:],
                            AF.Identity,
                            bias=bcol[:, bias_c:bias_c + 1]),
                        [bk, bcol], [dst], f"{tag}e{tk}")

            for p in range(4):
                bias_col(bk_d, p, 1, f"p{p}bk")
                bias_col(bq_d, p, 0, f"p{p}bq")
                bias_col(bv_d, p, 2, f"p{p}bv")
                proj(wk_d, p, KT, 1, f"p{p}K")
                memset(mks, 0.0, f"p{p}km0")
                for tk in range(8):
                    sop(lambda e, tkk=tk: e.activation(
                            sqb[:], KT[:, tkk * 512:(tkk + 1) * 512], AF.Square),
                        [KT], [sqb], f"p{p}ks{tk}")
                    top(lambda e: e.matmul(psF[0:1, :], ones_col_b[:], sqb[:],
                                           start=True, stop=True),
                        [sqb, ones_col_b], [psF], f"p{p}kn{tk}")
                    vop(lambda e: e.tensor_copy(rsc[0:1, :], psF[0:1, :]),
                        [psF], [rsc], tag=f"p{p}kc{tk}")
                    vop(lambda e: e.reduce_max(mks[0:1, 2:3], rsc[0:1, :],
                                               axis=AX.X),
                        [rsc], [mks], tag=f"p{p}kd{tk}")
                    vop(lambda e, tkk=tk: e.tensor_tensor(
                            mks[0:1, 3 if tkk % 2 == 0 else 0:
                                4 if tkk % 2 == 0 else 1],
                            mks[0:1, 0 if tkk % 2 == 0 else 3:
                                1 if tkk % 2 == 0 else 4],
                            mks[0:1, 2:3], ALU.max),
                        [mks], [mks], tag=f"p{p}kx{tk}")
                sop(lambda e: e.activation(mks[0:1, 1:2], mks[0:1, 0:1], AF.Sqrt),
                    [mks], [mks], f"p{p}kq")
                proj(wq_d, p, QT, 0, f"p{p}Q")
                wrs_v = wv_d[p].rearrange("(c p) e -> p c e", p=128)
                wsv_v = wstg[:].rearrange("p (c e) -> p c e", e=128)
                for hf in range(2):
                    dma("sync", wsv_v, wrs_v[:, hf * 8:(hf + 1) * 8, :],
                        [], [wstg], f"p{p}Vw{hf}")
                    vop(lambda e, h_=hf: e.tensor_copy(
                            wbf[:, h_ * 1024:(h_ + 1) * 1024], wstg[:]),
                        [wstg], [wbf], tag=f"p{p}Vv{hf}")
                for tk in range(8):
                    bk = psB if tk % 2 == 0 else psE
                    def mmv(e, tkk=tk, bk_=bk):
                        for c in range(16):
                            last = e.matmul(
                                bk_[:], wbf[:, c * 128:(c + 1) * 128],
                                xnT[c][:, tkk * 512:(tkk + 1) * 512],
                                start=(c == 0), stop=(c == 15))
                        return last
                    top(mmv, [wbf] + xnT, [bk], f"p{p}vm{tk}")
                    sop(lambda e, bk_=bk: e.activation(vtmp[:], bk_[:],
                                                       AF.Identity,
                                                       bias=bcol[:, 2:3]),
                        [bk, bcol], [vtmp], f"p{p}ve{tk}")
                    for k in range(4):
                        transpose128(
                            vtmp[:, k * 128:(k + 1) * 128], [vtmp],
                            Vtok[:, tk * 512 + k * 128:tk * 512 + (k + 1) * 128],
                            Vtok, f"p{p}vt{tk}_{k}")

                for tq in range(8):
                    sop(lambda e, tqq=tq: e.activation(
                            sqb[:], QT[:, tqq * 512:(tqq + 1) * 512], AF.Square),
                        [QT], [sqb], f"p{p}q{tq}s")
                    top(lambda e: e.matmul(psF[0:1, :], ones_col_b[:], sqb[:],
                                           start=True, stop=True),
                        [sqb, ones_col_b], [psF], f"p{p}q{tq}n")
                    sop(lambda e: e.activation(rsc[0:1, :], psF[0:1, :], AF.Sqrt),
                        [psF], [rsc], f"p{p}q{tq}r")
                    vop(lambda e: e.tensor_scalar(negM[0:1, :], rsc[0:1, :],
                                                  mks[0:1, 1:2], -1.0,
                                                  ALU.mult, ALU.mult),
                        [rsc, mks], [negM], tag=f"p{p}q{tq}m")
                    ntk = 4 * tq + 4
                    for tkc in range(ntk):
                        ab = aT[tkc % 2]
                        sbank = psA if tkc % 2 == 0 else psF
                        dg = tkc * 128 - tq * 512
                        def mms(e, tkk=tkc, tqq=tq, dgg=dg, bk=sbank):
                            e.matmul(bk[:], KT[:, tkk * 128:(tkk + 1) * 128],
                                     QT[:, tqq * 512:(tqq + 1) * 512],
                                     start=True, stop=False)
                            last = e.matmul(bk[:], ones_row_b[:], negM[0:1, :],
                                            start=False, stop=(dgg < 0))
                            if dgg >= 0:
                                off = 384 - dgg
                                last = e.matmul(bk[:], eye_b[:],
                                                maskw[:, off:off + 512],
                                                start=False, stop=True)
                            return last
                        top(mms, [KT, QT, negM, ones_row_b, eye_b, maskw],
                            [sbank], f"p{p}q{tq}k{tkc}s")
                        sop(lambda e, b=ab, bk=sbank: e.activation(
                                b[:], bk[:], AF.Exp, scale=RS),
                            [sbank], [ab], f"p{p}q{tq}k{tkc}e")
                        def mmz(e, tkk=tkc, b=ab, fi=(tkc == 0),
                                la=(tkc == ntk - 1)):
                            e.matmul(psB[:], Vtok[:, tkk * 128:(tkk + 1) * 128],
                                     b[:], start=fi, stop=la)
                            return e.matmul(psE[0:1, :], ones_col_b[:], b[:],
                                            start=fi, stop=la)
                        top(mmz, [Vtok, ab, ones_col_b], [psB, psE],
                            f"p{p}q{tq}k{tkc}z")
                    vop(lambda e: e.reciprocal(rb[0:1, :], psE[0:1, :]),
                        [psE], [rb], tag=f"p{p}q{tq}i")
                    top(lambda e: e.matmul(psA[:], ones_row_f[:], rb[0:1, :],
                                           start=True, stop=True),
                        [ones_row_f, rb], [psA], f"p{p}q{tq}b")
                    vop(lambda e: e.tensor_copy(rb[:], psA[:]),
                        [psA], [rb], tag=f"p{p}q{tq}c")
                    vop(lambda e: e.tensor_tensor(ztile[:], psB[:], rb[:],
                                                  ALU.mult),
                        [psB, rb], [ztile], tag=f"p{p}q{tq}z"),
                    dma("gpsimd",
                        zcc_in[p * 128:(p + 1) * 128, tq * 512:(tq + 1) * 512],
                        ztile[:], [ztile], [zcc_in], f"p{p}q{tq}d")

            def cc_z(e):
                e.collective_compute(
                    "AllGather", ALU.bypass,
                    replica_groups=[[0, 1, 2, 3, 4, 5, 6, 7]],
                    ins=[zcc_in.opt()], outs=[zcc_out.opt()]).then_inc(
                        P.ccsems[0], 1)
                e.wait_ge(P.ccsems[0], 1)
                return e.nop()
            P.op("gpsimd", cc_z, reads=[zcc_in], writes=[zcc_out], inc=1,
                 tag="cc_z")
            P.barrier()
            P.emit_block()

        stB = ExitStack()
        with stB:
            xnoT = [sb(stB, f"xnoT{i}", [128, T], BF16) for i in range(16)]
            stB1 = ExitStack()
            with stB1:
                xt0 = sb(stB1, "xtB", [128, D], F32)
                xnb2 = sb(stB1, "xnb2", [128, D], BF16)
                lnB = make_ln(stB1, "lB")
                for t in range(8):
                    dma("sync", xt0[:], xown_d[t * 128:(t + 1) * 128, :],
                        [], [xt0], f"B_x{t}")
                    lnB(xt0, xnb2[:], [xnb2], f"B_ln{t}")
                    for c in range(16):
                        transpose128(xnb2[:, c * 128:(c + 1) * 128], [xnb2],
                                     xnoT[c][:, t * 128:(t + 1) * 128],
                                     xnoT[c], f"B_tp{t}_{c}")
                zsl = [sb(stB1, f"zsl{i}", [128, T], BF16) for i in range(16)]
                zidx = sb(stB1, "zidx", [128, 1], I32)
                wost = sb(stB1, "wost", [128, 512], F32)
                wobf = [sb(stB1, f"wobf{i}", [128, 512], BF16) for i in range(16)]
                borow = sb(stB1, "borow", [1, 512], F32)
                borowb = sb(stB1, "borowb", [1, 512], BF16)
                sev = sb(stB1, "sev", [128, 512], BF16)
                zv = zcc_out.rearrange("r (q c) -> (r q) c", c=T)
                bo_flat = bo_d.rearrange("a b -> (a b)").unsqueeze(0)
                for h in range(NH):
                    dma("gpsimd", zidx[:],
                        zridx_d[h * 128:(h + 1) * 128, 0:1],
                        [], [zidx], f"B_zi{h}")
                    P.op("gpsimd",
                         lambda e, hh=h: e.indirect_dma_start(
                             zsl[hh][:], None, zv,
                             bass.IndirectOffsetOnAxis(zidx[:], 0)),
                         reads=[zcc_out, zidx], writes=[zsl[h]], inc=16,
                         tag=f"B_zg{h}")
                for dg in range(4):
                    dma("sync", borow[0:1, :],
                        bo_flat[0:1, dg * 512:(dg + 1) * 512],
                        [], [borow], f"B_bo{dg}")
                    vop(lambda e: e.tensor_copy(borowb[0:1, :], borow[0:1, :]),
                        [borow], [borowb], tag=f"B_boc{dg}")
                    for h in range(NH):
                        dma("sync", wost[:], wo_d[h, :, dg * 512:(dg + 1) * 512],
                            [], [wost], f"B_wo{dg}_{h}")
                        vop(lambda e, hh=h: e.tensor_copy(wobf[hh][:], wost[:]),
                            [wost], [wobf[h]], tag=f"B_wc{dg}_{h}")
                    for tc in range(8):
                        bk = psA if tc % 2 == 0 else psF
                        def mmo(e, tcc=tc, bk_=bk):
                            for h in range(NH):
                                e.matmul(bk_[:],
                                         zsl[h][:, tcc * 128:(tcc + 1) * 128],
                                         wobf[h][:], start=(h == 0), stop=False)
                            return e.matmul(bk_[:], ones_row_b[:],
                                            borowb[0:1, :],
                                            start=False, stop=True)
                        top(mmo, zsl + wobf + [ones_row_b, borowb], [bk],
                            f"B_om{dg}_{tc}")
                        sop(lambda e, bk_=bk: e.activation(sev[:], bk_[:],
                                                           AF.Identity),
                            [bk], [sev], f"B_oe{dg}_{tc}")
                        dma("gpsimd",
                            s0_d[tc * 128:(tc + 1) * 128, dg * 512:(dg + 1) * 512],
                            sev[:], [sev], [s0_d], f"B_od{dg}_{tc}")
                P.barrier()
                P.emit_block()

            hT = [sb(stB, f"hT{i}", [128, T], BF16) for i in range(64)]
            stB2 = ExitStack()
            with stB2:
                tstg = [sb(stB2, f"tstg{i}", [128, 256], F32) for i in range(2)]
                ternu = [sb(stB2, f"tu{i}", [128, 256], BF16) for i in range(16)]
                terng = [sb(stB2, f"tg{i}", [128, 256], BF16) for i in range(16)]
                tlt = sb(stB2, "tlt", [128, 256], BF16)
                tgt2 = sb(stB2, "tgt2", [128, 256], BF16)
                sil = sb(stB2, "sil", [128, 512], BF16)
                gsc = sb(stB2, "gsc", [128, 512], BF16)

                def ternary(dst, src, th, tag):
                    vop(lambda e: e.tensor_scalar(tgt2[:], src[:], th[:, 0:1],
                                                  None, ALU.is_gt),
                        [src, th], [tgt2], tag=tag + "g")
                    vop(lambda e: e.tensor_scalar(tlt[:], src[:], th[:, 1:2],
                                                  None, ALU.is_lt),
                        [src, th], [tlt], tag=tag + "l")
                    vop(lambda e: e.tensor_tensor(dst[:], tgt2[:], tlt[:],
                                                  ALU.subtract),
                        [tgt2, tlt], [dst], tag=tag + "s")

                for es in range(32):
                    for c in range(16):
                        stg = tstg[c % 2]
                        dma("sync", stg[:],
                            wut_d[c * 128:(c + 1) * 128, es * 256:(es + 1) * 256],
                            [], [stg], f"B2u{es}_{c}")
                        ternary(ternu[c], stg, thU, f"B2qu{es}_{c}")
                        dma("sync", stg[:],
                            wgt_d[c * 128:(c + 1) * 128, es * 256:(es + 1) * 256],
                            [], [stg], f"B2g{es}_{c}")
                        ternary(terng[c], stg, thG, f"B2qg{es}_{c}")
                    for e1 in range(2):
                        ech = es * 2 + e1
                        for tk in range(2):
                            def mmu(e, ee=e1, tkk=tk):
                                for c in range(16):
                                    e.matmul(psE[:],
                                             ternu[c][:, ee * 128:(ee + 1) * 128],
                                             xnoT[c][:, tkk * 512:(tkk + 1) * 512],
                                             start=(c == 0), stop=(c == 15))
                                for c in range(16):
                                    last = e.matmul(
                                        psF[:],
                                        terng[c][:, ee * 128:(ee + 1) * 128],
                                        xnoT[c][:, tkk * 512:(tkk + 1) * 512],
                                        start=(c == 0), stop=(c == 15))
                                return last
                            top(mmu, ternu + terng + xnoT, [psE, psF],
                                f"B2m{ech}_{tk}")
                            sop(lambda e: e.activation(sil[:], psE[:], AF.Silu,
                                                       scale=thU[:, 2:3]),
                                [psE, thU], [sil], f"B2s{ech}_{tk}")
                            sop(lambda e: e.activation(gsc[:], psF[:],
                                                       AF.Identity,
                                                       scale=thG[:, 2:3]),
                                [psF, thG], [gsc], f"B2g2{ech}_{tk}")
                            vop(lambda e, ec=ech, tkk=tk: e.tensor_tensor(
                                    hT[ec][:, tkk * 512:(tkk + 1) * 512],
                                    sil[:], gsc[:], ALU.mult),
                                [sil, gsc], [hT[ech]], tag=f"B2h{ech}_{tk}")
                if DEBUG:
                    for ec in range(64):
                        dma("gpsimd", h_dbg[ec * 128:(ec + 1) * 128, :],
                            hT[ec][:], [hT[ec]], [h_dbg], f"dbg_h{ec}")
                P.barrier()
                P.emit_block()

            stB3 = ExitStack()
            with stB3:
                dstg = [sb(stB3, f"dstg{i}", [128, D], F32) for i in range(2)]
                dt_b = sb(stB3, "dt_b", [128, D], BF16)
                dlt = sb(stB3, "dlt", [128, D], BF16)
                dgt = sb(stB3, "dgt", [128, D], BF16)
                for ec in range(64):
                    stg = dstg[ec % 2]
                    dma("sync", stg[:], wdt_d[ec * 128:(ec + 1) * 128, :],
                        [], [stg], f"B3l{ec}")
                    vop(lambda e, s_=stg: e.tensor_scalar(dgt[:], s_[:],
                                                  thD[:, 0:1],
                                                  None, ALU.is_gt),
                        [stg, thD], [dgt], tag=f"B3g{ec}")
                    vop(lambda e, s_=stg: e.tensor_scalar(dlt[:], s_[:],
                                                  thD[:, 1:2],
                                                  None, ALU.is_lt),
                        [stg, thD], [dlt], tag=f"B3t{ec}")
                    vop(lambda e: e.tensor_tensor(dt_b[:], dgt[:], dlt[:],
                                                  ALU.subtract),
                        [dgt, dlt], [dt_b], tag=f"B3s{ec}")
                    dma("gpsimd", ternd_d[ec * 128:(ec + 1) * 128, :], dt_b[:],
                        [dt_b], [ternd_d], f"B3d{ec}")
                tdt = [sb(stB3, f"tdt{i}", [128, 512], BF16) for i in range(2)]
                g0e = sb(stB3, "g0e", [128, 512], BF16)
                PS4 = [psA, psB, psE, psF]
                for dg in range(4):
                    for th_half in range(2):
                        for ec in range(64):
                            tb = tdt[ec % 2]
                            dma("sync", tb[:],
                                ternd_d[ec * 128:(ec + 1) * 128,
                                        dg * 512:(dg + 1) * 512],
                                [ternd_d], [tb], f"B3r{dg}_{th_half}_{ec}")
                            def mmd(e, tb_=tb, ecc=ec, thh=th_half):
                                for t4 in range(4):
                                    tc = thh * 4 + t4
                                    last = e.matmul(
                                        PS4[t4][:],
                                        hT[ecc][:, tc * 128:(tc + 1) * 128],
                                        tb_[:], start=(ecc == 0),
                                        stop=(ecc == 63))
                                return last
                            top(mmd, [tb] + hT, PS4, f"B3m{dg}_{th_half}_{ec}")
                        for t4 in range(4):
                            tc = th_half * 4 + t4
                            sop(lambda e, t4_=t4: e.activation(
                                    g0e[:], PS4[t4_][:], AF.Silu,
                                    scale=thD[:, 2:3]),
                                [PS4[t4], thD], [g0e], f"B3e{dg}_{th_half}_{t4}")
                            dma("gpsimd",
                                g0_d[tc * 128:(tc + 1) * 128,
                                     dg * 512:(dg + 1) * 512],
                                g0e[:], [g0e], [g0_d], f"B3o{dg}_{th_half}_{t4}")
                P.barrier()
                P.emit_block()
            P.barrier()
            P.emit_block()

        stB4 = ExitStack()
        with stB4:
            s0t = sb(stB4, "s0t", [128, D], BF16)
            g0t = sb(stB4, "g0t", [128, D], BF16)
            xot = sb(stB4, "xot", [128, D], F32)
            sgt = sb(stB4, "sgt", [128, D], F32)
            x1o = sb(stB4, "x1o", [128, D], F32)
            for tc in range(8):
                dma("sync", s0t[:], s0_d[tc * 128:(tc + 1) * 128, :], [], [s0t],
                    f"B4s{tc}")
                dma("sync", g0t[:], g0_d[tc * 128:(tc + 1) * 128, :], [], [g0t],
                    f"B4g{tc}")
                dma("sync", xot[:], xown_d[tc * 128:(tc + 1) * 128, :], [], [xot],
                    f"B4x{tc}")
                vop(lambda e: e.tensor_tensor(sgt[:], s0t[:], g0t[:], ALU.mult),
                    [s0t, g0t], [sgt], tag=f"B4m{tc}")
                vop(lambda e: e.tensor_tensor(x1o[:], xot[:], sgt[:], ALU.add),
                    [xot, sgt], [x1o], tag=f"B4a{tc}")
                dma("gpsimd", x1_d[tc * 128:(tc + 1) * 128, :], x1o[:],
                    [x1o], [x1_d], f"B4o{tc}")
            P.barrier()
            P.emit_block()

        stDo = ExitStack()
        with stDo:
            qT = sb(stDo, "qT", [128, 16 * T], BF16)
            gT = sb(stDo, "gT", [128, 16 * T], BF16)
            kvT = sb(stDo, "kvT", [128, 16 * T], BF16)
            stD = ExitStack()
            with stD:
                xn1T = sb(stD, "xn1T", [128, 16 * T], BF16)
                stD1 = ExitStack()
                with stD1:
                    qxsT = sb(stD1, "qxsT", [128, 16 * T], BF16)
                    x1t = sb(stD1, "x1t", [128, D], F32)
                    xn1b = sb(stD1, "xn1b", [128, D], BF16)
                    xn1f = sb(stD1, "xn1f", [128, D], F32)
                    qarg = sb(stD1, "qarg", [128, D], F32)
                    q8 = sb(stD1, "q8", [128, D], U8)
                    qxsb = sb(stD1, "qxsb", [128, D], BF16)
                    rmx = sb(stD1, "rmx", [128, 4], F32)
                    lnD = make_ln(stD1, "lD")
                    for t in range(8):
                        dma("sync", x1t[:], x1_d[t * 128:(t + 1) * 128, :],
                            [], [x1t], f"D_x{t}")
                        lnD(x1t, xn1b[:], [xn1b], f"D_ln{t}",
                            extra_f32=(xn1f[:], [xn1f]))
                        vop(lambda e: e.reduce_max(rmx[:, 0:1], xn1f[:],
                                                   axis=AX.X,
                                                   apply_absolute_value=True),
                            [xn1f], [rmx], tag=f"D_rm{t}")
                        vop(lambda e: e.reciprocal(rmx[:, 1:2], rmx[:, 0:1]),
                            [rmx], [rmx], tag=f"D_rr{t}")
                        vop(lambda e: e.tensor_scalar(rmx[:, 2:3], rmx[:, 1:2],
                                                      224.0, None, ALU.mult),
                            [rmx], [rmx], tag=f"D_rs{t}")
                        vop(lambda e: e.tensor_scalar(rmx[:, 3:4], rmx[:, 0:1],
                                                      1.0 / 224.0, None,
                                                      ALU.mult),
                            [rmx], [rmx], tag=f"D_rd{t}")
                        vop(lambda e: e.tensor_scalar(qarg[:], xn1f[:],
                                                      rmx[:, 2:3], None,
                                                      ALU.mult),
                            [xn1f, rmx], [qarg], tag=f"D_qa{t}")
                        sop(lambda e: e.copy(q8[:].bitcast(FP8), qarg[:]),
                            [qarg], [q8], f"D_q8{t}")
                        sop(lambda e: e.activation(qxsb[:], q8[:].bitcast(FP8),
                                                   AF.Identity, scale=rmx[:, 3:4]),
                            [q8, rmx], [qxsb], f"D_qs{t}")
                        for c in range(16):
                            transpose128(xn1b[:, c * 128:(c + 1) * 128], [xn1b],
                                         xn1T[:, c * T + t * 128:
                                              c * T + (t + 1) * 128],
                                         xn1T, f"D_t1{t}_{c}")
                            transpose128(qxsb[:, c * 128:(c + 1) * 128], [qxsb],
                                         qxsT[:, c * T + t * 128:
                                              c * T + (t + 1) * 128],
                                         qxsT, f"D_t2{t}_{c}")
                    dma("gpsimd", qxs_d, qxsT[:], [qxsT], [qxs_d], "D_qxspill")
                    P.barrier()
                    P.emit_block()

                stD2 = ExitStack()
                with stD2:
                    kT = sb(stD2, "kT", [128, 16 * T], BF16)
                    wst1 = [sb(stD2, f"wst1{i}", [128, 512], F32)
                            for i in range(2)]
                    wbf1 = [sb(stD2, f"wbf1{i}", [128, 512], BF16)
                            for i in range(16)]
                    bc1 = sb(stD2, "bc1", [128, 16], F32)
                    br1 = sb(stD2, "br1", [1, 128], F32)
                    vev = sb(stD2, "vev", [128, 512], BF16)

                    def l1proj(key, dst, act, tag, is_v=False):
                        wsrc = l1w_d[key]
                        bsrc = l1b_d[key]
                        for i in range(16):
                            dma("sync", br1[:], bsrc[i:i + 1, :], [], [br1],
                                f"{tag}bl{i}")
                            row_to_col(bc1[:, i:i + 1], bc1, br1[:], [br1],
                                       f"{tag}bc{i}")
                        for eb in range(4):
                            for c in range(16):
                                w_ = wst1[c % 2]
                                dma("sync", w_[:],
                                    wsrc[c * 128:(c + 1) * 128,
                                         eb * 512:(eb + 1) * 512],
                                    [], [w_], f"{tag}w{eb}_{c}")
                                vop(lambda e, cc=c, ww=w_: e.tensor_copy(
                                        wbf1[cc][:], ww[:]),
                                    [w_], [wbf1[c]], tag=f"{tag}v{eb}_{c}")
                            for e1 in range(4):
                                ec = eb * 4 + e1
                                for tk in range(2):
                                    bk = psA if (2 * ec + tk) % 2 == 0 else psB
                                    def mmp(e, ee=e1, tkk=tk, bk_=bk):
                                        for c in range(16):
                                            last = e.matmul(
                                                bk_[:],
                                                wbf1[c][:, ee * 128:
                                                        (ee + 1) * 128],
                                                xn1T[:, c * T + tkk * 512:
                                                     c * T + (tkk + 1) * 512],
                                                start=(c == 0), stop=(c == 15))
                                        return last
                                    top(mmp, wbf1 + [xn1T], [bk],
                                        f"{tag}m{ec}_{tk}")
                                    if is_v:
                                        sop(lambda e, ecc=ec, bk_=bk:
                                            e.activation(
                                                vev[:], bk_[:], AF.Identity,
                                                bias=bc1[:, ecc:ecc + 1]),
                                            [bk, bc1], [vev],
                                            f"{tag}e{ec}_{tk}")
                                        vop(lambda e, ecc=ec, tkk=tk:
                                            e.tensor_tensor(
                                                kvT[:, ecc * T + tkk * 512:
                                                    ecc * T + (tkk + 1) * 512],
                                                kT[:, ecc * T + tkk * 512:
                                                   ecc * T + (tkk + 1) * 512],
                                                vev[:], ALU.mult),
                                            [kT, vev], [kvT],
                                            tag=f"{tag}kv{ec}_{tk}")
                                    else:
                                        sop(lambda e, ecc=ec, tkk=tk, bk_=bk:
                                            e.activation(
                                                dst[:, ecc * T + tkk * 512:
                                                    ecc * T + (tkk + 1) * 512],
                                                bk_[:], act,
                                                bias=bc1[:, ecc:ecc + 1]),
                                            [bk, bc1], [dst],
                                            f"{tag}e{ec}_{tk}")

                    l1proj("q", qT, AF.Identity, "Dq")
                    l1proj("k", kT, AF.Identity, "Dk")
                    l1proj("v", None, AF.Identity, "Dv", is_v=True)
                    l1proj("g", gT, AF.Sigmoid, "Dg")
                    P.barrier()
                    P.emit_block()
                P.barrier()
                P.emit_block()

            stD3 = ExitStack()
            with stD3:
                ssc = sb(stD3, "ssc", [128, 16], F32)
                sscb = sb(stD3, "sscb", [128, 16], F32)
                for ec in range(16):
                    vop(lambda e, ecc=ec: e.reduce_sum(
                            ssc[:, ecc:ecc + 1],
                            kvT[:, ecc * T:(ecc + 1) * T], axis=AX.X),
                        [kvT], [ssc], tag=f"D_ss{ec}")
                vop(lambda e: e.tensor_copy(sscb[:], ssc[:]), [ssc], [sscb],
                    tag="D_sscp")
                top(lambda e: e.transpose(psV[0:16, 0:128], sscb[:], eye_f[:]),
                    [sscb, eye_f], [psV], "D_sst")
                ssrow = sb(stD3, "ssrow", [16, 128], F32)
                vop(lambda e: e.tensor_copy(ssrow[:], psV[0:16, 0:128]),
                    [psV], [ssrow], tag="D_ssv")
                dma("gpsimd",
                    scc_in.rearrange("o (a b) -> (o a) b", b=128),
                    ssrow[:], [ssrow], [scc_in], "D_ssd")
                def cc_s(e):
                    e.collective_compute(
                        "AllGather", ALU.bypass,
                        replica_groups=[[0, 1, 2, 3, 4, 5, 6, 7]],
                        ins=[scc_in.opt()], outs=[scc_out.opt()]).then_inc(
                            P.ccsems[1], 1)
                    e.wait_ge(P.ccsems[1], 1)
                    return e.nop()
                P.op("gpsimd", cc_s, reads=[scc_in], writes=[scc_out], inc=1,
                     tag="cc_s")
                gsb = sb(stD3, "gsb", [8, D], F32)
                pmk = sb(stD3, "pmk", [8, 1], F32)
                offr = sb(stD3, "offr", [1, D], F32)
                offc = sb(stD3, "offc", [128, 16], F32)
                dma("sync", gsb[:], scc_out, [scc_out], [gsb], "D_gl")
                dma("sync", pmk[:], pmask_d, [], [pmk], "D_pm")
                for j in range(4):
                    top(lambda e, jj=j: e.matmul(
                            psA[0:1, :], pmk[:],
                            gsb[:, jj * 512:(jj + 1) * 512],
                            start=True, stop=True),
                        [pmk, gsb], [psA], f"D_of{j}")
                    vop(lambda e, jj=j: e.tensor_copy(
                            offr[0:1, jj * 512:(jj + 1) * 512], psA[0:1, :]),
                        [psA], [offr], tag=f"D_oc{j}")
                for ec in range(16):
                    row_to_col(offc[:, ec:ec + 1], offc,
                               offr[0:1, ec * 128:(ec + 1) * 128], [offr],
                               f"D_ol{ec}")
                kvsT = sb(stD3, "kvsT", [128, 16 * T], BF16)
                for ec in range(16):
                    vop(lambda e, ecc=ec: e.tensor_tensor_scan(
                            kvsT[:, ecc * T:(ecc + 1) * T],
                            kvT[:, ecc * T:(ecc + 1) * T],
                            kvT[:, ecc * T:(ecc + 1) * T],
                            offc[:, ecc:ecc + 1], ALU.add, ALU.bypass),
                        [kvT, offc], [kvsT], tag=f"D_sc{ec}")
                s1f = sb(stD3, "s1f", [128, 16 * T], BF16)
                vop(lambda e: e.tensor_tensor(s1f[:], qT[:], kvsT[:], ALU.mult),
                    [qT, kvsT], [s1f], tag="D_s1a")
                vop(lambda e: e.tensor_tensor(kvsT[:], s1f[:], gT[:], ALU.mult),
                    [s1f, gT], [kvsT], tag="D_s1b")
                stok = sb(stD3, "stok", [128, D], BF16)
                for tc in range(8):
                    for ec in range(16):
                        transpose128(
                            kvsT[:, ec * T + tc * 128:ec * T + (tc + 1) * 128],
                            [kvsT], stok[:, ec * 128:(ec + 1) * 128], stok,
                            f"D_st{tc}_{ec}")
                    dma("gpsimd", s1_d[tc * 128:(tc + 1) * 128, :], stok[:],
                        [stok], [s1_d], f"D_so{tc}")
                P.barrier()
                P.emit_block()

        if True:
            stE1 = ExitStack()
            with stE1:
                qxsT = sb(stE1, "qxsT_e", [128, 16 * T], BF16)
                dma("sync", qxsT[:], qxs_d, [qxs_d], [qxsT], "E1_qxld")
                fst = [sb(stE1, f"fst{i}", [128, D], F32) for i in range(2)]
                farg = sb(stE1, "farg", [128, D], F32)
                f8 = sb(stE1, "f8", [128, D], U8)
                fqc = sb(stE1, "fqc", [128, D], BF16)
                rmw = sb(stE1, "rmw", [128, 4], F32)
                qwsu = sb(stE1, "qwsu", [128, 16 * 512], BF16)
                qwsg = sb(stE1, "qwsg", [128, 16 * 512], BF16)
                silE = sb(stE1, "silE", [128, 512], BF16)
                t0e = sb(stE1, "t0e", [128, 512], F32)
                rmt = sb(stE1, "rmt", [128, 16], F32)
                rmt2 = sb(stE1, "rmt2", [128, 8], F32)
                memset(rmt, 0.0, "E_rmt0")

                def quant_rows(wsrc, r0, dstws, ich, tag):
                    stg = fst[ich % 2]
                    dma("sync", stg[:], wsrc[r0:r0 + 128, :], [], [stg],
                        tag + "l")
                    vop(lambda e: e.reduce_max(rmw[:, 0:1], stg[:], axis=AX.X,
                                               apply_absolute_value=True),
                        [stg], [rmw], tag=tag + "m")
                    vop(lambda e: e.reciprocal(rmw[:, 1:2], rmw[:, 0:1]),
                        [rmw], [rmw], tag=tag + "r")
                    vop(lambda e: e.tensor_scalar(rmw[:, 2:3], rmw[:, 1:2],
                                                  224.0, None, ALU.mult),
                        [rmw], [rmw], tag=tag + "s")
                    vop(lambda e: e.tensor_scalar(rmw[:, 3:4], rmw[:, 0:1],
                                                  1.0 / 224.0, None, ALU.mult),
                        [rmw], [rmw], tag=tag + "d")
                    vop(lambda e: e.tensor_scalar(farg[:], stg[:], rmw[:, 2:3],
                                                  None, ALU.mult),
                        [stg, rmw], [farg], tag=tag + "a")
                    sop(lambda e: e.copy(f8[:].bitcast(FP8), farg[:]),
                        [farg], [f8], tag + "8")
                    sop(lambda e: e.activation(fqc[:], f8[:].bitcast(FP8),
                                               AF.Identity, scale=rmw[:, 3:4]),
                        [f8, rmw], [fqc], tag + "q")
                    for c in range(16):
                        transpose128(fqc[:, c * 128:(c + 1) * 128], [fqc],
                                     dstws[:, c * 512 + ich * 128:
                                           c * 512 + (ich + 1) * 128],
                                     dstws, f"{tag}t{c}")

                for isl in range(NSL):
                    for ich in range(4):
                        quant_rows(fwu_d, isl * 512 + ich * 128, qwsu, ich,
                                   f"E1u{isl}_{ich}")
                        quant_rows(fwg_d, isl * 512 + ich * 128, qwsg, ich,
                                   f"E1g{isl}_{ich}")
                    for tc in range(8):
                        def mmug(e, tcc=tc):
                            for c in range(16):
                                e.matmul(psE[:],
                                         qxsT[:, c * T + tcc * 128:
                                              c * T + (tcc + 1) * 128],
                                         qwsu[:, c * 512:(c + 1) * 512],
                                         start=(c == 0), stop=(c == 15))
                            for c in range(16):
                                last = e.matmul(
                                    psF[:],
                                    qxsT[:, c * T + tcc * 128:
                                         c * T + (tcc + 1) * 128],
                                    qwsg[:, c * 512:(c + 1) * 512],
                                    start=(c == 0), stop=(c == 15))
                            return last
                        top(mmug, [qxsT, qwsu, qwsg], [psE, psF],
                            f"E1m{isl}_{tc}")
                        sop(lambda e: e.activation(silE[:], psF[:], AF.Silu),
                            [psF], [silE], f"E1s{isl}_{tc}")
                        vop(lambda e: e.tensor_tensor(t0e[:], silE[:], psE[:],
                                                      ALU.mult),
                            [silE, psE], [t0e], tag=f"E1t{isl}_{tc}")
                        dma("gpsimd",
                            t0_d[tc * 128:(tc + 1) * 128,
                                 isl * 512:(isl + 1) * 512],
                            t0e[:], [t0e], [t0_d], f"E1d{isl}_{tc}")
                        vop(lambda e, tcc=tc: e.reduce_max(
                                rmt2[:, tcc:tcc + 1], t0e[:], axis=AX.X,
                                apply_absolute_value=True),
                            [t0e], [rmt2], tag=f"E1r{isl}_{tc}")
                        vop(lambda e, tcc=tc, ii=isl: e.tensor_tensor(
                                rmt[:, 2 * tcc + (ii + 1) % 2:
                                    2 * tcc + (ii + 1) % 2 + 1],
                                rmt[:, 2 * tcc + ii % 2:2 * tcc + ii % 2 + 1],
                                rmt2[:, tcc:tcc + 1], ALU.max),
                            [rmt, rmt2], [rmt], tag=f"E1x{isl}_{tc}")
                P.barrier()
                P.emit_block()

                stE2 = ExitStack()
                with stE2:
                    t0r = [sb(stE2, f"t0r{i}", [128, DFF1], F32) for i in range(2)]
                    targ = sb(stE2, "targ", [128, DFF1], F32)
                    t8 = sb(stE2, "t8", [128, DFF1], U8)
                    tqs = sb(stE2, "tqs", [128, DFF1], BF16)
                    rmtc = sb(stE2, "rmtc", [128, 4], F32)
                    qrow = sb(stE2, "qrow", [128, DFF1], BF16)
                    for tc in range(8):
                        tr = t0r[tc % 2]
                        dma("sync", tr[:], t0_d[tc * 128:(tc + 1) * 128, :],
                            [], [tr], f"E2l{tc}")
                        vop(lambda e, tcc=tc: e.reciprocal(
                                rmtc[:, 0:1], rmt[:, 2 * tcc + 1:2 * tcc + 2]),
                            [rmt], [rmtc], tag=f"E2r{tc}")
                        vop(lambda e: e.tensor_scalar(rmtc[:, 2:3],
                                                      rmtc[:, 0:1], 224.0,
                                                      None, ALU.mult),
                            [rmtc], [rmtc], tag=f"E2s{tc}")
                        vop(lambda e, tcc=tc: e.tensor_scalar(
                                rmtc[:, 1:2], rmt[:, 2 * tcc + 1:2 * tcc + 2],
                                1.0 / 224.0, None, ALU.mult),
                            [rmt, rmtc], [rmtc], tag=f"E2d{tc}")
                        vop(lambda e, tr_=tr: e.tensor_scalar(
                                targ[:], tr_[:], rmtc[:, 2:3], None, ALU.mult),
                            [tr, rmtc], [targ], tag=f"E2a{tc}")
                        sop(lambda e: e.copy(t8[:].bitcast(FP8), targ[:]),
                            [targ], [t8], f"E28{tc}")
                        sop(lambda e: e.activation(tqs[:], t8[:].bitcast(FP8),
                                                   AF.Identity, scale=rmtc[:, 1:2]),
                            [t8, rmtc], [tqs], f"E2q{tc}")
                        for ich in range(NI):
                            transpose128(tqs[:, ich * 128:(ich + 1) * 128],
                                         [tqs],
                                         qrow[:, ich * 128:(ich + 1) * 128],
                                         qrow, f"E2t{tc}_{ich}")
                        dma("gpsimd",
                            qts_d[:, tc * 128:(tc + 1) * 128]
                            .rearrange("(a p) t -> p a t", p=128),
                            qrow[:].rearrange("p (a t) -> p a t", t=128),
                            [qrow], [qts_d], f"E2o{tc}")
                    P.barrier()
                    P.emit_block()
            P.barrier()
            P.emit_block()

        stE3 = ExitStack()
        with stE3:
            qt1T = sb(stE3, "qt1T", [128, NI * 128 * 8], BF16)
            dix = sb(stE3, "dix", [128, 1], I32)
            for ich in range(NI):
                dma("gpsimd", dix[:],
                    didx_d[ich * 128:(ich + 1) * 128, 0:1],
                    [], [dix], f"E3i{ich}")
                P.op("gpsimd",
                     lambda e, ii=ich: e.indirect_dma_start(
                         qt1T[:, ii * T:(ii + 1) * T], None, qts_d,
                         bass.IndirectOffsetOnAxis(dix[:], 0)),
                     reads=[qts_d, dix], writes=[qt1T], inc=16, tag=f"E3g{ich}")

            fdst = sb(stE3, "fdst", [128, DFF1], F32)
            fdarg = sb(stE3, "fdarg", [128, DFF1], F32)
            fd8 = sb(stE3, "fd8", [128, DFF1], U8)
            fdqc = sb(stE3, "fdqc", [128, DFF1], BF16)
            rmd = sb(stE3, "rmd", [128, 4], F32)
            qdsT = sb(stE3, "qdsT", [128, NI * 512], BF16)
            g1e = sb(stE3, "g1e", [128, 512], BF16)
            for dg in range(4):
                for dc in range(4):
                    r0 = dg * 512 + dc * 128
                    dma("sync", fdst[:], fwd_d[r0:r0 + 128, :], [], [fdst],
                        f"E3l{dg}_{dc}")
                    vop(lambda e: e.reduce_max(rmd[:, 0:1], fdst[:], axis=AX.X,
                                               apply_absolute_value=True),
                        [fdst], [rmd], tag=f"E3m{dg}_{dc}")
                    vop(lambda e: e.reciprocal(rmd[:, 1:2], rmd[:, 0:1]),
                        [rmd], [rmd], tag=f"E3r{dg}_{dc}")
                    vop(lambda e: e.tensor_scalar(rmd[:, 2:3], rmd[:, 1:2],
                                                  224.0, None, ALU.mult),
                        [rmd], [rmd], tag=f"E3s{dg}_{dc}")
                    vop(lambda e: e.tensor_scalar(rmd[:, 3:4], rmd[:, 0:1],
                                                  1.0 / 224.0, None, ALU.mult),
                        [rmd], [rmd], tag=f"E3d{dg}_{dc}")
                    vop(lambda e: e.tensor_scalar(fdarg[:], fdst[:],
                                                  rmd[:, 2:3], None, ALU.mult),
                        [fdst, rmd], [fdarg], tag=f"E3a{dg}_{dc}")
                    sop(lambda e: e.copy(fd8[:].bitcast(FP8), fdarg[:]),
                        [fdarg], [fd8], f"E38{dg}_{dc}")
                    sop(lambda e: e.activation(fdqc[:], fd8[:].bitcast(FP8),
                                               AF.Identity, scale=rmd[:, 3:4]),
                        [fd8, rmd], [fdqc], f"E3q{dg}_{dc}")
                    for ich in range(NI):
                        transpose128(fdqc[:, ich * 128:(ich + 1) * 128], [fdqc],
                                     qdsT[:, ich * 512 + dc * 128:
                                          ich * 512 + (dc + 1) * 128],
                                     qdsT, f"E3t{dg}_{dc}_{ich}")
                for tc in range(8):
                    bk = psA if tc % 2 == 0 else psB
                    def mmdn(e, tcc=tc, bk_=bk):
                        for ich in range(NI):
                            last = e.matmul(
                                bk_[:],
                                qt1T[:, ich * T + tcc * 128:
                                     ich * T + (tcc + 1) * 128],
                                qdsT[:, ich * 512:(ich + 1) * 512],
                                start=(ich == 0), stop=(ich == NI - 1))
                        return last
                    top(mmdn, [qt1T, qdsT], [bk], f"E3dm{dg}_{tc}")
                    sop(lambda e, bk_=bk: e.activation(g1e[:], bk_[:], AF.Silu),
                        [bk], [g1e], f"E3de{dg}_{tc}")
                    dma("gpsimd",
                        g1f_d[tc * 128:(tc + 1) * 128, dg * 512:(dg + 1) * 512],
                        g1e[:], [g1e], [g1f_d], f"E3do{dg}_{tc}")
            P.barrier()
            P.emit_block()

        if DEBUG:
            zin_dbg = nc.dram_tensor("zin_dbg", [512, S], BF16,
                                     kind="ExternalOutput").ap()
            zout_dbg = nc.dram_tensor("zout_dbg", [4096, S], BF16,
                                      kind="ExternalOutput").ap()
            sccdbg = nc.dram_tensor("scc_dbg", [8, D], F32,
                                    kind="ExternalOutput").ap()
            dma("gpsimd", zin_dbg, zcc_in, [zcc_in], [zin_dbg], "dbg_zi")
            dma("gpsimd", zout_dbg, zcc_out, [zcc_out], [zout_dbg], "dbg_zo")
            dma("gpsimd", sccdbg, scc_out, [scc_out], [sccdbg], "dbg_sc")
        stF = ExitStack()
        with stF:
            fx = sb(stF, "fx", [128, D], F32)
            fs = sb(stF, "fs", [128, D], BF16)
            fg = sb(stF, "fg", [128, D], BF16)
            fo = sb(stF, "fo", [128, D], F32)
            fo2 = sb(stF, "fo2", [128, D], F32)
            for tc in range(8):
                dma("sync", fx[:], x1_d[tc * 128:(tc + 1) * 128, :], [], [fx],
                    f"F_x{tc}")
                dma("sync", fs[:], s1_d[tc * 128:(tc + 1) * 128, :], [], [fs],
                    f"F_s{tc}")
                dma("sync", fg[:], g1f_d[tc * 128:(tc + 1) * 128, :], [], [fg],
                    f"F_g{tc}")
                vop(lambda e: e.tensor_tensor(fo[:], fs[:], fg[:], ALU.mult),
                    [fs, fg], [fo], tag=f"F_m{tc}")
                vop(lambda e: e.tensor_tensor(fo2[:], fx[:], fo[:], ALU.add),
                    [fx, fo], [fo2], tag=f"F_a{tc}")
                dma("gpsimd", out_d[tc * 128:(tc + 1) * 128, :], fo2[:],
                    [fo2], [out_d], f"F_o{tc}")
            snap = dict(P.counts)

            def fence(e):
                for f in Prog.ENGINES:
                    if f != "sync" and snap[f] > 0:
                        e.wait_ge(P.sems[f], snap[f])
                return e.nop()
            P.op("sync", fence, reads=[out_d], writes=[], inc=1, tag="fence")
            P.emit_block()
    return nc


_NC_CACHE = []
_LAST = [None]


def kernel(**inputs):
    gi = inputs.__getitem__
    x = np.asarray(gi("L0_attn_x"))
    _ = gi("L1_ffn_up_reorder_index")
    WQ = np.asarray(gi("L0_attn_W_Q")); WK = np.asarray(gi("L0_attn_W_K"))
    WV = np.asarray(gi("L0_attn_W_V")); WO = np.asarray(gi("L0_attn_W_O"))
    bQ = np.asarray(gi("L0_attn_b_Q")); bK = np.asarray(gi("L0_attn_b_K"))
    bV = np.asarray(gi("L0_attn_b_V")); bO = np.asarray(gi("L0_attn_b_O"))
    WU = np.asarray(gi("L0_ffn_W_up")); WG = np.asarray(gi("L0_ffn_W_gate"))
    WD = np.asarray(gi("L0_ffn_W_down"))
    l1 = {k: np.asarray(gi(f"L1_attn_W{k}")) for k in "qkvg"}
    l1b = {k: np.asarray(gi(f"L1_attn_b{k}")) for k in "qkvg"}
    fg = np.asarray(gi("L1_ffn_w_gate")); fu = np.asarray(gi("L1_ffn_w_up"))
    fd = np.asarray(gi("L1_ffn_w_down"))
    didx = np.asarray(gi("L1_ffn_down_reorder_index")).astype(np.int32)

    wut = np.ascontiguousarray(WU.T)
    wgt = np.ascontiguousarray(WG.T)
    wdt = np.ascontiguousarray(WD.T)
    l1t = {k: np.ascontiguousarray(l1[k].T) for k in "qkvg"}
    eye = np.eye(128, dtype=np.float32)
    hh = np.arange(896)[None, :]
    pp = np.arange(128)[:, None]
    maskw = np.where(hh >= pp + 384, 0.0, MASKV).astype(ml_dtypes.bfloat16)

    if not _NC_CACHE:
        _NC_CACHE.append(build_program())
    nc = _NC_CACHE[0]

    in_maps = []
    for c in range(8):
        Bb, Sl = c // 4, c % 4
        heads = [4 * Sl + j for j in range(4)]
        zr = np.empty((NH, 128), np.int32)
        for h in range(NH):
            r = 4 * Bb + h // 4
            orig = r * 512 + (h % 4) * 128 + np.arange(128)
            zr[h] = orig * 4 + Sl
        pm = np.zeros((8, 1), np.float32)
        for r in range(8):
            if r // 4 == Bb and r % 4 < Sl:
                pm[r, 0] = 1.0
        m = {
            "xb": x[Bb],
            "xown": x[Bb, Sl * T:(Sl + 1) * T],
            "wq": WQ[heads], "wk": WK[heads], "wv": WV[heads],
            "bq": bQ[heads], "bk": bK[heads], "bv": bV[heads],
            "wo": WO, "bo": bO.reshape(16, 128),
            "wut": wut, "wgt": wgt, "wdt": wdt,
            "l1q": l1t["q"], "l1k": l1t["k"], "l1v": l1t["v"], "l1g": l1t["g"],
            "l1bq": l1b["q"].reshape(16, 128),
            "l1bk": l1b["k"].reshape(16, 128),
            "l1bv": l1b["v"].reshape(16, 128),
            "l1bg": l1b["g"].reshape(16, 128),
            "fwg": fg, "fwu": fu, "fwd": fd,
            "didx": didx.reshape(NI * 128, 1),
            "zridx": zr.reshape(NH * 128, 1), "pmask": pm,
            "maskw": maskw, "eye": eye,
        }
        in_maps.append({k: np.ascontiguousarray(v) for k, v in m.items()})

    res = run_bass_kernel_spmd(nc, in_maps, core_ids=list(range(8)))
    _LAST[0] = res
    out = np.empty((B, S, D), np.float32)
    for c in range(8):
        Bb, Sl = c // 4, c % 4
        out[Bb, Sl * T:(Sl + 1) * T] = np.asarray(res.results[c]["out"])
    return out
